# version 02
# baseline (speedup 1.0000x reference)
import numpy as np
import ml_dtypes
from contextlib import ExitStack

import concourse.bass as bass
import concourse.mybir as mybir
from concourse.bass_utils import run_bass_kernel_spmd

F32 = mybir.dt.float32
BF16 = mybir.dt.bfloat16
AX = mybir.AxisListType.X
AF = mybir.ActivationFunctionType
ALU = mybir.AluOpType
BF = ml_dtypes.bfloat16

NH = 32
NKV = 8
EPS = 1e-5
RSQ = 1.0 / np.sqrt(128.0)

ENGINES = ("sync", "gpsimd", "vector", "scalar", "tensor")


class Sched:
    def __init__(self):
        self.q = {e: [] for e in ENGINES}
        self.count = {}
        self.last_wait = {e: {} for e in ENGINES}

    def tok(self, sem):
        return (sem, self.count.get(sem, 0))

    def op(self, eng, build, waits=(), inc=None, inc_by=1):
        w = []
        for t in waits:
            if t is None:
                continue
            s, v = t
            if v > self.last_wait[eng].get(s, 0):
                w.append((s, v))
                self.last_wait[eng][s] = v
        tok = None
        if inc is not None:
            self.count[inc] = self.count.get(inc, 0) + inc_by
            tok = (inc, self.count[inc])
        self.q[eng].append((w, build, inc, inc_by))
        return tok


def build_program():
    nc = bass.Bass("TRN2", target_bir_lowering=False, debug=False)

    dI = lambda n, s, dt=F32: nc.dram_tensor(n, list(s), dt, kind="ExternalInput").ap()
    x = dI("x", (2048, 4096))
    wq_t = dI("wq_t", (32, 128, 32, 128))
    wk_t = dI("wk_t", (8, 128, 32, 128))
    wv_ic = dI("wv_ic", (32, 128, 1024))
    wo_t = dI("wo_t", (8, 32, 128, 512))
    wf_t = dI("wf_t", (8, 32, 128, 512))
    wq_n = dI("wq_n", (4096, 4096))
    wk_n = dI("wk_n", (1024, 4096))
    wv_n = dI("wv_n", (1024, 4096))
    wo_n = dI("wo_n", (4096, 4096))
    qs_cm = dI("qs_cm", (128, 32))
    ks_cm = dI("ks_cm", (128, 8))
    vs_cm = dI("vs_cm", (128, 8))
    os_row = dI("os_row", (1, 4096))
    bias_row = dI("bias_row", (1, 4096))
    ap_t = dI("ap_t", (128, 32, 256))
    b_t = dI("b_t", (256, 4096))
    cs_col = dI("cs_col", (16, 1))
    ss_col = dI("ss_col", (16, 1))
    repR = dI("repR", (16, 128), BF16)
    selC = dI("selC", (16, 2), BF16)
    ones_bf = dI("ones_bf", (128, 128), BF16)
    ident_bf = dI("ident_bf", (128, 128), BF16)

    out = nc.dram_tensor("out", [1024, 4096], F32, kind="ExternalOutput").ap()
    x1_spill = nc.dram_tensor("x1_spill", [1024, 4096], F32, kind="Internal").ap()
    d_brd = nc.dram_tensor("d_brd", [1, 4096], F32, kind="Internal").ap()
    d_brdb = nc.dram_tensor("d_brdb", [1, 4096], BF16, kind="Internal").ap()

    p = Sched()

    with ExitStack() as ctx:
        sb = lambda n, s, dt: ctx.enter_context(nc.sbuf_tensor(n, s, dt))
        R1 = sb("R1", [128, 32768], BF16)
        R2 = sb("R2", [128, 32768], BF16)
        R3 = sb("R3", [128, 32768], BF16)
        SM = sb("SM", [128, 256], F32)
        CB = sb("CB", [128, 40], F32)
        CS = sb("CS", [128, 80], F32)
        ONE = sb("ONE", [128, 128], BF16)
        IDN = sb("IDN", [128, 128], BF16)
        WCM = sb("WCM", [128, 2], F32)
        RZ = sb("RZ", [16, 258], BF16)

        pb = [ctx.enter_context(nc.psum_tensor(f"pb{i}", [128, 512], F32))
              for i in range(8)]
        pbt6 = pb[6].ap().bitcast(BF16)
        pbt7 = pb[7].ap().bitcast(BF16)

        sems = {}
        for s in ("MS", "XS", "DO0", "DO1", "XE", "SP", "XF", "OU",
                  "G0", "G1", "GF0", "GF1", "GF2", "GF3",
                  "GA", "GB", "GC", "GD", "T", "V", "SC"):
            sems[s] = ctx.enter_context(nc.semaphore(s))

        blk = ctx.enter_context(nc.Block())

        ms_chain = [None]

        def dma(eng, dst, src, waits=(), sem=None):
            return p.op(eng, lambda e, d=dst, s=src: e.dma_start(out=d, in_=s),
                        waits=waits, inc=sem, inc_by=16)

        def dmams(dst, src, waits=()):
            t = dma("sync", dst, src,
                    waits=list(waits) + [ms_chain[0]], sem="MS")
            ms_chain[0] = t
            return t

        def op(eng, fn, waits=(), ):
            return p.op(eng, fn, waits=waits,
                        inc={"vector": "V", "scalar": "SC", "tensor": "T"}[eng])

        LNXT = R3[:, 0:16384]
        stage_f32 = R3[:, 16384:24576].bitcast(F32)
        wsig = R3[:, 24576:28672]
        absb = R3[:, 28672:32768]

        bank_r = {}
        bank_w = {}

        t_one = dmams(ONE[:], ones_bf)
        t_idn = dmams(IDN[:], ident_bf)
        t_eps = op("vector", lambda e: e.memset(CB[:, 32:33], EPS))
        t_qs = dmams(CS[:, 0:32], qs_cm)
        t_ks = dmams(CS[:, 32:40], ks_cm)
        t_vs = dmams(CS[:, 40:48], vs_cm)

        sbuf_prev = []

        def s_pass(w_nat, nchunks, outcol0):
            for c in range(nchunks):
                i = len(sbuf_prev)
                buf = absb if (i % 2 == 0) else wsig
                w = [sbuf_prev[-2][1]] if len(sbuf_prev) >= 2 else []
                tdma = dma("gpsimd", buf, w_nat[128 * c:128 * c + 128, :],
                           waits=w, sem="G0" if i % 2 == 0 else "G1")
                tred = op("vector",
                          lambda e, b=buf, cc=c: e.reduce_sum(
                              out=SM[:, outcol0 + cc:outcol0 + cc + 1], in_=b,
                              axis=AX, apply_absolute_value=True),
                          waits=[tdma])
                sbuf_prev.append((i % 2, tred))
            return sbuf_prev[-1][1]

        sq_done = s_pass(wq_n, 32, 0)
        sk_done = s_pass(wk_n, 8, 32)
        sv_done = s_pass(wv_n, 8, 40)
        so_done = s_pass(wo_n, 32, 48)

        t_sq = op("vector", lambda e: e.tensor_scalar(
            out=SM[:, 80:112], in0=SM[:, 0:32], scalar1=1.0 / 4096.0,
            scalar2=1e-8, op0=ALU.mult, op1=ALU.max), waits=[sq_done])
        t_sq2 = op("vector", lambda e: e.tensor_mul(
            out=SM[:, 80:112], in0=SM[:, 80:112], in1=CS[:, 0:32]),
            waits=[t_sq, t_qs])
        t_sk = op("vector", lambda e: e.tensor_scalar(
            out=SM[:, 112:120], in0=SM[:, 32:40], scalar1=1.0 / 4096.0,
            scalar2=1e-8, op0=ALU.mult, op1=ALU.max), waits=[sk_done])
        t_sk2 = op("vector", lambda e: e.tensor_mul(
            out=SM[:, 112:120], in0=SM[:, 112:120], in1=CS[:, 32:40]),
            waits=[t_sk, t_ks])
        csc = None
        for h in range(NH):
            csc = op("vector", lambda e, hh=h: e.tensor_mul(
                out=CS[:, 48 + hh:49 + hh], in0=SM[:, 80 + hh:81 + hh],
                in1=SM[:, 112 + hh // 4:113 + hh // 4]), waits=[t_sq2, t_sk2])
        t_csc = op("vector", lambda e: e.tensor_scalar_mul(
            out=CS[:, 48:80], in0=CS[:, 48:80], scalar1=RSQ), waits=[csc])
        t_sv = op("vector", lambda e: e.tensor_scalar(
            out=SM[:, 120:128], in0=SM[:, 40:48], scalar1=1.0 / 4096.0,
            scalar2=1e-8, op0=ALU.mult, op1=ALU.max), waits=[sv_done])
        t_vsv = op("vector", lambda e: e.tensor_mul(
            out=SM[:, 120:128], in0=SM[:, 120:128], in1=CS[:, 40:48]),
            waits=[t_sv, t_vs])
        t_so = op("vector", lambda e: e.tensor_scalar(
            out=SM[:, 128:160], in0=SM[:, 48:80], scalar1=1.0 / 4096.0,
            scalar2=1e-8, op0=ALU.mult, op1=ALU.max), waits=[so_done])

        def ln1_half(hf, war_stage, war_wsig):
            last_norm = None
            ev_tok = None
            sq_tok = None
            for tc in range(4):
                w = [war_stage if tc == 0 else None, last_norm,
                     sq_tok]
                tdx = dma("sync", stage_f32[:, 0:4096],
                          x[512 * hf + 128 * tc: 512 * hf + 128 * tc + 128, :],
                          waits=[t for t in w if t], sem="XS")
                c0 = 160 + 8 * (tc % 2)
                tsum = op("vector", lambda e: e.reduce_sum(
                    out=SM[:, c0:c0 + 1], in_=stage_f32[:, 0:4096], axis=AX),
                    waits=[tdx])
                tnegm = op("vector", lambda e: e.tensor_scalar_mul(
                    out=SM[:, c0 + 1:c0 + 2], in0=SM[:, c0:c0 + 1],
                    scalar1=-1.0 / 4096.0), waits=[tsum])
                tsq = op("scalar", lambda e: e.activation(
                    out=wsig[:, 0:4096],
                    in_=stage_f32[:, 0:4096], func=AF.Square,
                    bias=SM[:, c0 + 1:c0 + 2], accum_out=SM[:, c0 + 2:c0 + 3]),
                    waits=[tnegm, war_wsig if tc == 0 else None, ev_tok])
                sq_tok = tsq
                tln = op("scalar", lambda e: e.activation(
                    out=SM[:, c0 + 3:c0 + 4], in_=SM[:, c0 + 2:c0 + 3],
                    func=AF.Ln, scale=1.0 / 4096.0, bias=CB[:, 32:33]),
                    waits=[tsq, t_eps])
                trstd = op("scalar", lambda e: e.activation(
                    out=SM[:, c0 + 4:c0 + 5], in_=SM[:, c0 + 3:c0 + 4],
                    func=AF.Exp, scale=-0.5), waits=[tln])
                tn = op("vector", lambda e: e.tensor_scalar(
                    out=wsig[:, 0:4096], in0=stage_f32[:, 0:4096],
                    scalar1=SM[:, c0 + 1:c0 + 2], scalar2=SM[:, c0 + 4:c0 + 5],
                    op0=ALU.add, op1=ALU.mult), waits=[trstd])
                last_norm = tn
                for grp in range(4):
                    pt, bidx = (pbt6, 6) if grp % 2 == 0 else (pbt7, 7)
                    ttr = None
                    for k in range(8):
                        ic = grp * 8 + k
                        ttr = op("tensor", lambda e, icc=ic, ptt=pt, kk=k:
                                 e.transpose(ptt[:, 128 * kk:128 * kk + 128],
                                             wsig[:, 128 * icc:128 * icc + 128],
                                             IDN[:]),
                                 waits=[tn, t_idn, bank_r.get(bidx)])
                    bank_w[bidx] = ttr
                    tev = op("scalar", lambda e, ptt=pt, g=grp, tcc=tc: e.copy(
                        out=LNXT.rearrange("p (ic t) -> p ic t", t=512)[
                            :, 8 * g:8 * g + 8, 128 * tcc:128 * tcc + 128],
                        in_=ptt[:, 0:1024].rearrange(
                            "p (k t) -> p k t", t=128)), waits=[ttr])
                    bank_r[bidx] = tev
                    ev_tok = tev
            return ev_tok, last_norm, sq_tok

        def q_proj(hf, lnxt_tok, war_stage, war_wsig):
            last_sign = None
            last_mm = None
            for qc in range(32):
                tdw = dma("sync", stage_f32[:, 0:4096], wq_t[qc],
                          waits=[lnxt_tok, war_stage if qc == 0 else last_sign],
                          sem="XS")
                tsg = op("scalar", lambda e: e.sign(
                    out=wsig[:, 0:4096], in_=stage_f32[:, 0:4096]),
                    waits=[tdw, war_wsig if qc == 0 else last_mm])
                last_sign = tsg
                bidx = qc % 2
                tmm = None
                for ic in range(32):
                    tmm = op("tensor", lambda e, icc=ic, b=bidx: e.matmul(
                        pb[b][:], wsig[:, 128 * icc:128 * icc + 128],
                        LNXT[:, 512 * icc:512 * icc + 512],
                        start=(icc == 0), stop=(icc == 31)),
                        waits=[tsg, bank_r.get(bidx)] if ic == 0 else [])
                last_mm = tmm
                bank_w[bidx] = tmm
                tev = op("vector", lambda e, q=qc, b=bidx, h=hf:
                         e.tensor_scalar_mul(
                    out=R1[:, 1024 * q + 512 * h:1024 * q + 512 * h + 512],
                    in0=pb[b][:], scalar1=CS[:, 48 + q:49 + q]),
                    waits=[tmm, t_csc])
                bank_r[bidx] = tev
            return last_sign, last_mm

        def k_proj(hf, lnxt_tok, war_stage, war_wsig):
            last_sign = None
            last_mm = None
            for kc8 in range(8):
                tdw = dma("sync", stage_f32[:, 0:4096], wk_t[kc8],
                          waits=[lnxt_tok, war_stage if kc8 == 0 else last_sign],
                          sem="XS")
                tsg = op("scalar", lambda e: e.sign(
                    out=wsig[:, 0:4096], in_=stage_f32[:, 0:4096]),
                    waits=[tdw, war_wsig if kc8 == 0 else last_mm])
                last_sign = tsg
                bidx = 2 + (kc8 % 2)
                tmm = None
                for ic in range(32):
                    tmm = op("tensor", lambda e, icc=ic, b=bidx: e.matmul(
                        pb[b][:], wsig[:, 128 * icc:128 * icc + 128],
                        LNXT[:, 512 * icc:512 * icc + 512],
                        start=(icc == 0), stop=(icc == 31)),
                        waits=[tsg, bank_r.get(bidx)] if ic == 0 else [])
                last_mm = tmm
                bank_w[bidx] = tmm
                tev = op("scalar", lambda e, k=kc8, b=bidx, h=hf: e.copy(
                    out=R2[:, 2048 * k + 512 * h:2048 * k + 512 * h + 512],
                    in_=pb[b][:]), waits=[tmm])
                bank_r[bidx] = tev
            return last_sign, last_mm

        def v_proj(hf, lnxt_tok, war_stage, war_wsig):
            last_sign = None
            mm_tok = None
            for ic in range(32):
                tdw = dma("sync", stage_f32[:, 0:1024], wv_ic[ic],
                          waits=[lnxt_tok, war_stage if ic == 0 else last_sign],
                          sem="XS")
                tsg = op("scalar", lambda e: e.sign(
                    out=wsig[:, 0:1024], in_=stage_f32[:, 0:1024]),
                    waits=[tdw, war_wsig if ic == 0 else mm_tok])
                last_sign = tsg
                for vkc in range(4):
                    for vcol in range(2):
                        b = vkc * 2 + vcol
                        mm_tok = op("tensor",
                                    lambda e, icc=ic, vk=vkc, vc=vcol, bb=b:
                                    e.matmul(
                                        pb[bb][:],
                                        LNXT[:, 512 * icc + 128 * vk:
                                             512 * icc + 128 * vk + 128],
                                        wsig[:, 512 * vc:512 * vc + 512],
                                        start=(icc == 0), stop=(icc == 31)),
                                    waits=[tsg, bank_r.get(b)] if ic == 0
                                    else [tsg])
            ev = None
            for vkc in range(4):
                for vcol in range(2):
                    b = vkc * 2 + vcol
                    bank_w[b] = mm_tok
                    kg = 4 * hf + vkc
                    ev = op("scalar", lambda e, bb=b, kgg=kg, vc=vcol: e.copy(
                        out=R2[:, 16384 + 1024 * kgg + 512 * vc:
                               16384 + 1024 * kgg + 512 * vc + 512],
                        in_=pb[bb][:]), waits=[mm_tok])
                    bank_r[b] = ev
            return last_sign, mm_tok, ev

        war_stage = p.tok("V")
        war_wsig2 = p.tok("V")
        last_vmm = None
        for hf in range(2):
            lnx, lnorm, lsq = ln1_half(
                hf, war_stage, war_wsig2 if hf == 0 else last_vmm)
            sgq, mmq = q_proj(hf, lnx, war_stage=lnorm, war_wsig=lnorm)
            sgk, mmk = k_proj(hf, lnx, war_stage=sgq, war_wsig=mmq)
            sgv, mmv, evv = v_proj(hf, lnx, war_stage=sgk, war_wsig=mmk)
            war_stage = sgv
            last_vmm = mmv
        for hf in range(2, 4):
            lnx, lnorm, lsq = ln1_half(hf, war_stage, last_vmm)
            sgk, mmk = k_proj(hf, lnx, war_stage=lnorm, war_wsig=lnorm)
            sgv, mmv, evv = v_proj(hf, lnx, war_stage=sgk, war_wsig=mmk)
            war_stage = sgv
            last_vmm = mmv

        qt_ready = p.tok("V")
        kv_ready = p.tok("SC")

        cw = [qt_ready, kv_ready, war_stage, last_vmm]
        crow = R3[:, 2048:4096].bitcast(F32)
        crw = None
        qmax = []
        for h in range(NH):
            tq2 = op("vector", lambda e, hh=h: e.tensor_mul(
                out=R3[:, 0:1024], in0=R1[:, 1024 * hh:1024 * hh + 1024],
                in1=R1[:, 1024 * hh:1024 * hh + 1024]),
                waits=cw + (qmax[-1:] if qmax else []))
            tmm = op("tensor", lambda e: e.matmul(
                pb[0][0:1, 0:512], ONE[:, 0:1], R3[:, 0:512],
                start=True, stop=False),
                waits=[tq2, t_one, bank_r.get(0)])
            tmm = op("tensor", lambda e: e.matmul(
                pb[0][0:1, 0:512], ONE[:, 0:1], R3[:, 512:1024],
                start=False, stop=True), waits=[])
            bank_w[0] = tmm
            tmx = op("vector", lambda e, hh=h: e.reduce_max(
                out=crow[0:1, hh:hh + 1], in_=pb[0][0:1, 0:512],
                axis=AX), waits=[tmm] + (cw if h == 0 else [])),
            tmx = tmx[0] if isinstance(tmx, tuple) else tmx
            bank_r[0] = tmx
            qmax.append(tmx)
        kmax = []
        for g in range(NKV):
            tk2 = op("vector", lambda e, gg=g: e.tensor_mul(
                out=R3[:, 0:2048], in0=R2[:, 2048 * gg:2048 * gg + 2048],
                in1=R2[:, 2048 * gg:2048 * gg + 2048]),
                waits=[qmax[-1]] + (kmax[-1:] if kmax else []))
            tmm = None
            for kk in range(4):
                tmm = op("tensor", lambda e, k4=kk: e.matmul(
                    pb[1][0:1, 0:512], ONE[:, 0:1],
                    R3[:, 512 * k4:512 * k4 + 512],
                    start=(k4 == 0), stop=(k4 == 3)),
                    waits=[tk2, t_one, bank_r.get(1)] if kk == 0 else [])
            bank_w[1] = tmm
            tmx = op("vector", lambda e, gg=g: e.reduce_max(
                out=crow[0:1, 32 + gg:33 + gg], in_=pb[1][0:1, 0:512],
                axis=AX), waits=[tmm])
            bank_r[1] = tmx
            kmax.append(tmx)
        tc2 = None
        for h in range(NH):
            tc2 = op("vector", lambda e, hh=h: e.tensor_mul(
                out=crow[0:1, 64 + hh:65 + hh],
                in0=crow[0:1, 0 + hh:1 + hh],
                in1=crow[0:1, 32 + hh // 4:33 + hh // 4]),
                waits=[qmax[-1], kmax[-1]])
        tln = op("scalar", lambda e: e.activation(
            out=crow[0:1, 96:128], in_=crow[0:1, 64:96], func=AF.Ln,
            scale=1.0, bias=CB[0:1, 32:33]), waits=[tc2, t_eps])
        texp = op("scalar", lambda e: e.activation(
            out=crow[0:1, 128:160], in_=crow[0:1, 96:128], func=AF.Exp,
            scale=0.5), waits=[tln])
        tneg = op("vector", lambda e: e.tensor_scalar_mul(
            out=crow[0:1, 160:192], in0=crow[0:1, 128:160], scalar1=-1.0),
            waits=[texp])
        t_cb0 = dmams(d_brd[0:1, 0:32], crow[0:1, 160:192], waits=[tneg])
        t_cb = dmams(CB[:, 0:32], d_brd[0:1, 0:32].broadcast_to([128, 32]),
                     waits=[t_cb0])

        pr_cols = [4096 + 512 * i for i in range(8)]
        pring = {}
        s_cons = {}
        pair_ev = {4: bank_r.get(4), 5: bank_r.get(5),
                   6: bank_r.get(6), 7: bank_r.get(7)}
        ring = 0
        last_evA = None
        first_w = [t_cb, t_vsv, bank_r.get(2), bank_r.get(3)]
        for g in range(NKV):
            for tcol in range(2):
                for pair in range(2):
                    h0 = 4 * g + 2 * pair
                    evw = [pair_ev[4], pair_ev[5], pair_ev[6], pair_ev[7]]
                    exps = {}
                    avs = {}
                    rss = {}
                    for kc in range(16):
                        sb0 = kc % 2
                        for i, h in enumerate((h0, h0 + 1)):
                            b = 2 * sb0 + i
                            wts = [s_cons.get(b)]
                            if g == 0 and tcol == 0 and pair == 0 and kc < 2:
                                wts += first_w + [bank_r.get(b)]
                            tmm = op("tensor", lambda e, gg=g, k=kc, hh=h,
                                     bb=b, t2=tcol: e.matmul(
                                pb[bb][:],
                                R2[:, 2048 * gg + 128 * k:
                                   2048 * gg + 128 * k + 128],
                                R1[:, 1024 * hh + 512 * t2:
                                   1024 * hh + 512 * t2 + 512],
                                start=True, stop=True), waits=wts)
                            bank_w[b] = tmm
                            slot = ring % 8
                            ring += 1
                            tex = op("scalar", lambda e, bb=b, hh=h, sl=slot:
                                     e.activation(
                                out=R3[:, pr_cols[sl]:pr_cols[sl] + 512],
                                in_=pb[bb][:], func=AF.Exp,
                                bias=CB[:, hh:hh + 1]),
                                waits=[tmm, t_cb, pring.get(slot)])
                            s_cons[b] = tex
                            exps[i] = (tex, slot)
                        for i in range(2):
                            tex, slot = exps[i]
                            wts = [tex]
                            if kc == 0:
                                wts += evw
                            tav = op("tensor", lambda e, gg=g, k=kc, sl=slot,
                                     ab=4 + i: e.matmul(
                                pb[ab][:],
                                R2[:, 16384 + 1024 * k + 128 * gg:
                                   16384 + 1024 * k + 128 * gg + 128],
                                R3[:, pr_cols[sl]:pr_cols[sl] + 512],
                                start=(k == 0), stop=(k == 15)), waits=wts)
                            trs = op("tensor", lambda e, k=kc, sl=slot,
                                     rb=6 + i: e.matmul(
                                pb[rb][:], ONE[:],
                                R3[:, pr_cols[sl]:pr_cols[sl] + 512],
                                start=(k == 0), stop=(k == 15)),
                                waits=([t_one] + evw) if kc == 0 else [])
                            pring[slot] = trs
                            bank_w[4 + i] = tav
                            bank_w[6 + i] = trs
                            avs[i] = tav
                            rss[i] = trs
                    for i in range(2):
                        h = h0 + i
                        trr = op("vector", lambda e, ii=i: e.reciprocal(
                            out=stage_f32[:, 1024 * ii:1024 * ii + 512],
                            in_=pb[6 + ii][:]),
                            waits=[rss[i], pair_ev[4 + i]])
                        tevA = op("vector", lambda e, ii=i, hh=h, t2=tcol:
                                  e.tensor_tensor(
                            out=R1[:, 1024 * hh + 512 * t2:
                                   1024 * hh + 512 * t2 + 512],
                            in0=pb[4 + ii][:],
                            in1=stage_f32[:, 1024 * ii:1024 * ii + 512],
                            op=ALU.mult), waits=[avs[i], trr])
                        pair_ev[4 + i] = tevA
                        pair_ev[6 + i] = tevA
                        bank_r[4 + i] = tevA
                        bank_r[6 + i] = tevA
                        last_evA = tevA
        attn_done = last_evA

        dstage = R2[:, 16384:24576].bitcast(F32)
        dsgn = R2[:, 24576:28672]
        OST = R2[:, 28672:32768]
        dp_z = op("vector", lambda e: e.memset(R2[:, 16384:16512], 0.0),
                  waits=[attn_done])
        dp_cp = op("vector", lambda e: e.tensor_copy(
            out=R2[:, 16384:16416], in_=SM[:, 128:160]),
            waits=[dp_z, t_so])
        dp_tr = op("tensor", lambda e: e.transpose(
            pbt6[:, 0:128], R2[:, 16384:16512], IDN[:]),
            waits=[dp_cp, t_idn, bank_r.get(6), bank_w.get(6)])
        bank_w[6] = dp_tr
        dp_ev = op("scalar", lambda e: e.copy(
            out=R2[:, 16512:16640], in_=pbt6[:, 0:128]), waits=[dp_tr])
        bank_r[6] = dp_ev
        dp_row = dmams(R2[0:1, 16640:20736], R2[0:32, 16512:16640],
                       waits=[dp_ev])
        dp_os = dma("gpsimd", R2[0:1, 24576:28672], os_row,
                    waits=[attn_done], sem="GD")
        dp_mul = op("vector", lambda e: e.tensor_mul(
            out=R2[0:1, 16640:20736], in0=R2[0:1, 16640:20736],
            in1=R2[0:1, 24576:28672]), waits=[dp_row, dp_os])
        t_ost0 = dmams(d_brdb[0:1, 0:4096], R2[0:1, 16640:20736],
                       waits=[dp_mul])
        t_ost = dmams(OST[:], d_brdb[0:1, 0:4096].broadcast_to([128, 4096]),
                      waits=[t_ost0])
        last_stt = {0: None, 1: None}
        last_sgn = None
        mm_prev_ocol = None
        for ocol in range(8):
            stts = []
            for ic in range(32):
                i2 = ic % 2
                w = [last_stt[i2]]
                if ocol == 0 and ic < 2:
                    w += [attn_done, dp_mul, t_ost]
                tdw = dma("sync", dstage[:, 2048 * i2:2048 * i2 + 512],
                          wo_t[ocol, ic], waits=w, sem="DO%d" % i2)
                tsg = op("scalar", lambda e, ii=i2: e.sign(
                    out=dsgn[:, 2048 * ii:2048 * ii + 512],
                    in_=dstage[:, 2048 * ii:2048 * ii + 512]),
                    waits=[tdw, last_stt[i2],
                           last_sgn if ic >= 2 or ocol > 0 else attn_done])
                last_sgn = tsg
                tst = op("vector", lambda e, icc=ic, ii=i2, oc=ocol:
                         e.scalar_tensor_tensor(
                    out=R2[:, 512 * icc:512 * icc + 512],
                    in0=dsgn[:, 2048 * ii:2048 * ii + 512],
                    scalar=SM[:, 120 + icc // 4:121 + icc // 4],
                    in1=OST[:, 512 * oc:512 * oc + 512],
                    op0=ALU.mult, op1=ALU.mult),
                    waits=[tsg, t_ost, t_vsv] +
                          ([attn_done] if ocol == 0 and ic == 0 else []) +
                          ([mm_prev_ocol] if ic == 0 and ocol > 0 else []))
                last_stt[i2] = tst
                stts.append(tst)
            mm_last = None
            for ic in range(32):
                for tk in range(8):
                    w = [stts[ic]]
                    if ic == 0:
                        w.append(bank_r.get(tk))
                        w.append(attn_done)
                    mm_last = op("tensor", lambda e, icc=ic, t8=tk: e.matmul(
                        pb[t8][:],
                        R1[:, 1024 * icc + 128 * t8:1024 * icc + 128 * t8 + 128],
                        R2[:, 512 * icc:512 * icc + 512],
                        start=(icc == 0), stop=(icc == 31)), waits=w)
            mm_prev_ocol = mm_last
            for tk in range(8):
                bank_w[tk] = mm_last
                tev = op("scalar", lambda e, t8=tk, oc=ocol: e.copy(
                    out=R3[:, 4096 * t8 + 512 * oc:4096 * t8 + 512 * oc + 512],
                    in_=pb[t8][:]), waits=[mm_last])
                bank_r[tk] = tev
        o_done = p.tok("SC")

        e1 = R1[:, 0:8192].bitcast(F32)
        e2 = R1[:, 8192:16384].bitcast(F32)
        ejunk = R1[:, 16384:24576].bitcast(F32)
        lnx1_tm = R1[:, 24576:28672]
        spills = []
        e_prev = None
        sp_prev = None
        for tk in range(8):
            orow = R3[:, 4096 * tk:4096 * tk + 4096]
            w0 = [o_done, mm_prev_ocol] if tk == 0 else [e_prev]
            tsum = op("vector", lambda e, o=orow: e.reduce_sum(
                out=SM[:, 192:193], in_=o, axis=AX), waits=w0)
            tnegm = op("vector", lambda e: e.tensor_scalar_mul(
                out=SM[:, 193:194], in0=SM[:, 192:193], scalar1=-1.0 / 4096.0),
                waits=[tsum])
            tsq = op("scalar", lambda e, o=orow: e.activation(
                out=ejunk, in_=o, func=AF.Square, bias=SM[:, 193:194],
                accum_out=SM[:, 194:195]),
                waits=[tnegm] + (w0 if tk == 0 else []))
            tln_ = op("scalar", lambda e: e.activation(
                out=SM[:, 195:196], in_=SM[:, 194:195], func=AF.Ln,
                scale=1.0 / 4096.0, bias=CB[:, 32:33]), waits=[tsq])
            trstd = op("scalar", lambda e: e.activation(
                out=SM[:, 196:197], in_=SM[:, 195:196], func=AF.Exp,
                scale=-0.5), waits=[tln_])
            tn = op("vector", lambda e, o=orow: e.tensor_scalar(
                out=e2, in0=o, scalar1=SM[:, 193:194], scalar2=SM[:, 196:197],
                op0=ALU.add, op1=ALU.mult), waits=[trstd])
            tdx = dma("sync", e1, x[128 * tk:128 * tk + 128, :],
                      waits=w0 + [sp_prev], sem="XE")
            tx1 = op("vector", lambda e: e.tensor_add(out=e1, in0=e1, in1=e2),
                     waits=[tn, tdx])
            tsp = dma("sync", x1_spill[128 * tk:128 * tk + 128, :], e1,
                      waits=[tx1], sem="SP")
            spills.append(tsp)
            sp_prev = tsp
            tsum3 = op("vector", lambda e: e.reduce_sum(
                out=SM[:, 197:198], in_=e1, axis=AX), waits=[tx1])
            tnegm3 = op("vector", lambda e: e.tensor_scalar_mul(
                out=SM[:, 198:199], in0=SM[:, 197:198], scalar1=-1.0 / 4096.0),
                waits=[tsum3])
            tsq3 = op("scalar", lambda e: e.activation(
                out=ejunk, in_=e1, func=AF.Square, bias=SM[:, 198:199],
                accum_out=SM[:, 199:200]), waits=[tnegm3, tsq])
            tln3 = op("scalar", lambda e: e.activation(
                out=SM[:, 200:201], in_=SM[:, 199:200], func=AF.Ln,
                scale=1.0 / 4096.0, bias=CB[:, 32:33]), waits=[tsq3])
            trstd3 = op("scalar", lambda e: e.activation(
                out=SM[:, 201:202], in_=SM[:, 200:201], func=AF.Exp,
                scale=-0.5), waits=[tln3])
            tn3 = op("vector", lambda e: e.tensor_scalar(
                out=lnx1_tm, in0=e1, scalar1=SM[:, 198:199],
                scalar2=SM[:, 201:202], op0=ALU.add, op1=ALU.mult),
                waits=[trstd3])
            ttr_ev = None
            for grp in range(4):
                pt, bidx = (pbt6, 6) if grp % 2 == 0 else (pbt7, 7)
                ttr = None
                for k in range(8):
                    ic = grp * 8 + k
                    ttr = op("tensor", lambda e, icc=ic, ptt=pt, kk=k:
                             e.transpose(ptt[:, 128 * kk:128 * kk + 128],
                                         lnx1_tm[:, 128 * icc:128 * icc + 128],
                                         IDN[:]),
                             waits=[tn3, t_idn, bank_r.get(bidx)])
                bank_w[bidx] = ttr
                tev = op("scalar", lambda e, ptt=pt, g=grp, t8=tk: e.copy(
                    out=R2.rearrange("p (ic t) -> p ic t", t=1024)[
                        :, 8 * g:8 * g + 8, 128 * t8:128 * t8 + 128],
                    in_=ptt[:, 0:1024].rearrange(
                        "p (k t) -> p k t", t=128)), waits=[ttr])
                bank_r[bidx] = tev
                ttr_ev = tev
            e_prev = ttr_ev
        lnx1_done = e_prev
        spill_last = spills[-1]

        bias_bf = R3[0:1, 22528:26624]
        frow = R3[:, 26624:28672].bitcast(F32)
        t_bias = dma("gpsimd", bias_bf, bias_row, waits=[lnx1_done], sem="GC")
        t_cs = dmams(SM[0:16, 220:221], cs_col, waits=[lnx1_done])
        t_ss = dmams(SM[0:16, 221:222], ss_col, waits=[lnx1_done])
        t_rr = dmams(RZ[0:16, 0:128], repR, waits=[lnx1_done])
        t_sc = dmams(RZ[0:16, 128:130], selC, waits=[lnx1_done])
        t_w16 = op("vector", lambda e: e.tensor_mul(
            out=SM[0:16, 222:223], in0=SM[0:16, 220:221],
            in1=SM[0:16, 221:222]), waits=[t_cs, t_ss])
        t_wl = op("vector", lambda e: e.tensor_scalar_mul(
            out=RZ[0:16, 130:258], in0=RZ[0:16, 0:128],
            scalar1=SM[0:16, 222:223]), waits=[t_w16, t_rr])
        t_wmm = op("tensor", lambda e: e.matmul(
            pb[4][:, 0:2], RZ[0:16, 130:258], RZ[0:16, 128:130],
            start=True, stop=True),
            waits=[t_wl, t_sc, bank_r.get(4), bank_w.get(4)])
        bank_w[4] = t_wmm
        t_wcm = op("scalar", lambda e: e.copy(out=WCM[:], in_=pb[4][:, 0:2]),
                   waits=[t_wmm])
        bank_r[4] = t_wcm
        t_asb = dma("gpsimd", R3[:, 0:8192],
                    ap_t.rearrange("p ic r -> p (ic r)"),
                    waits=[lnx1_done], sem="GA")
        t_bsb = dma("gpsimd",
                    R3[:, 8192:16384].rearrange("p (rc o) -> p rc o", rc=2),
                    b_t.rearrange("(rc p) o -> p rc o", p=128),
                    waits=[lnx1_done], sem="GB")
        t_bw = op("vector", lambda e: e.tensor_scalar_mul(
            out=R3[:, 8192:12288], in0=R3[:, 8192:12288], scalar1=WCM[:, 0:1]),
            waits=[t_bsb, t_wcm])
        t_bw2 = op("vector", lambda e: e.tensor_scalar_mul(
            out=R3[:, 12288:16384], in0=R3[:, 12288:16384],
            scalar1=WCM[:, 1:2]), waits=[t_bsb, t_wcm])
        xa_ev = []
        for rc in range(2):
            for tcol in range(2):
                b = 2 * rc + tcol
                tmm = None
                for ic in range(32):
                    tmm = op("tensor", lambda e, icc=ic, r=rc, t2=tcol, bb=b:
                             e.matmul(
                        pb[bb][:],
                        R3[:, 256 * icc + 128 * r:256 * icc + 128 * r + 128],
                        R2[:, 1024 * icc + 512 * t2:1024 * icc + 512 * t2 + 512],
                        start=(icc == 0), stop=(icc == 31)),
                        waits=[t_asb, lnx1_done, bank_r.get(b)]
                        if ic == 0 else [])
                bank_w[b] = tmm
                tev = op("scalar", lambda e, r=rc, t2=tcol, bb=b: e.copy(
                    out=R3[:, 16384 + 1024 * r + 512 * t2:
                           16384 + 1024 * r + 512 * t2 + 512],
                    in_=pb[bb][:]), waits=[tmm, lnx1_done])
                bank_r[b] = tev
                xa_ev.append(tev)
        xa_done = xa_ev[-1]

        wf_cols = [20480 + 512 * i for i in range(4)]
        wf_cons = {}
        mm_last = None
        yev_last = None
        for ocol in range(8):
            for ic in range(32):
                slot = (ocol * 32 + ic) % 4
                w = [wf_cons.get(slot)]
                if ocol == 0 and ic < 4:
                    w += [xa_done]
                tdw = dma("gpsimd", R3[:, wf_cols[slot]:wf_cols[slot] + 512],
                          wf_t[ocol, ic], waits=w, sem="GF%d" % slot)
                for tk in range(8):
                    w2 = [tdw, lnx1_done]
                    if ic == 0:
                        w2.append(bank_r.get(tk))
                    mm_last = op("tensor", lambda e, icc=ic, t8=tk, sl=slot:
                                 e.matmul(
                        pb[t8][:],
                        R2[:, 1024 * icc + 128 * t8:1024 * icc + 128 * t8 + 128],
                        R3[:, wf_cols[sl]:wf_cols[sl] + 512],
                        start=(icc == 0), stop=False), waits=w2)
                wf_cons[slot] = mm_last
            for tk in range(8):
                for rc in range(2):
                    mm_last = op("tensor", lambda e, t8=tk, r=rc, oc=ocol:
                                 e.matmul(
                        pb[t8][:],
                        R3[:, 16384 + 1024 * r + 128 * t8:
                           16384 + 1024 * r + 128 * t8 + 128],
                        R3[:, 8192 + 4096 * r + 512 * oc:
                           8192 + 4096 * r + 512 * oc + 512],
                        start=False, stop=False),
                        waits=[t_bw, t_bw2, xa_done]
                        if (ocol == 0 and tk == 0 and rc == 0) else [])
                mm_last = op("tensor", lambda e, t8=tk, oc=ocol: e.matmul(
                    pb[t8][:], ONE[0:1, :],
                    bias_bf[:, 512 * oc:512 * oc + 512],
                    start=False, stop=True),
                    waits=[t_bias, t_one]
                    if (ocol == 0 and tk == 0) else [])
                bank_w[tk] = mm_last
                tev = op("scalar", lambda e, t8=tk, oc=ocol: e.copy(
                    out=R1[:, 4096 * t8 + 512 * oc:4096 * t8 + 512 * oc + 512],
                    in_=pb[t8][:]),
                    waits=[mm_last, spill_last] +
                          ([lnx1_done] if ocol == 0 else []))
                bank_r[tk] = tev
                yev_last = tev
        y_done = yev_last

        f1 = R2[:, 0:8192].bitcast(F32)
        f2 = R2[:, 8192:16384].bitcast(F32)
        fjunk = R2[:, 16384:24576].bitcast(F32)
        out_toks = []
        f_prev = None
        fmm = mm_last
        for tk in range(8):
            yrow = R1[:, 4096 * tk:4096 * tk + 4096]
            w0 = [y_done, fmm] if tk == 0 else [f_prev]
            tsum = op("vector", lambda e, o=yrow: e.reduce_sum(
                out=SM[:, 208:209], in_=o, axis=AX), waits=w0)
            tnegm = op("vector", lambda e: e.tensor_scalar_mul(
                out=SM[:, 209:210], in0=SM[:, 208:209], scalar1=-1.0 / 4096.0),
                waits=[tsum])
            tsq = op("scalar", lambda e, o=yrow: e.activation(
                out=fjunk, in_=o, func=AF.Square, bias=SM[:, 209:210],
                accum_out=SM[:, 210:211]),
                waits=[tnegm] + (w0 if tk == 0 else []))
            tln_ = op("scalar", lambda e: e.activation(
                out=SM[:, 211:212], in_=SM[:, 210:211], func=AF.Ln,
                scale=1.0 / 4096.0, bias=CB[:, 32:33]), waits=[tsq])
            trstd = op("scalar", lambda e: e.activation(
                out=SM[:, 212:213], in_=SM[:, 211:212], func=AF.Exp,
                scale=-0.5), waits=[tln_])
            tn = op("vector", lambda e, o=yrow: e.tensor_scalar(
                out=f2, in0=o, scalar1=SM[:, 209:210], scalar2=SM[:, 212:213],
                op0=ALU.add, op1=ALU.mult), waits=[trstd])
            tdx = dma("sync", f1, x1_spill[128 * tk:128 * tk + 128, :],
                      waits=[spills[tk]] + w0 +
                            ([out_toks[tk - 1]] if tk > 0 else []), sem="XF")
            tadd = op("vector", lambda e: e.tensor_add(out=f1, in0=f1, in1=f2),
                      waits=[tn, tdx])
            tout = dma("sync", out[128 * tk:128 * tk + 128, :], f1,
                       waits=[tadd], sem="OU")
            out_toks.append(tout)
            f_prev = tadd

        p.op("sync", lambda e: None, waits=[out_toks[-1]])

        def emit(eng_name):
            def body(eng):
                for (waits, build, inc, inc_by) in p.q[eng_name]:
                    for (s, v) in waits:
                        eng.wait_ge(sems[s], v)
                    inst = build(eng)
                    if inc is not None:
                        inst.then_inc(sems[inc], inc_by)
            return body

        blk.sync(emit("sync"))
        blk.gpsimd(emit("gpsimd"))
        blk.vector(emit("vector"))
        blk.scalar(emit("scalar"))
        blk.tensor(emit("tensor"))

    return nc


def _layout_inputs(inputs):
    hs = np.ascontiguousarray(inputs["L0_attn_hidden_states"], dtype=np.float32)
    Wq = np.ascontiguousarray(inputs["L0_attn_q_weight"], dtype=np.float32)
    Wk = np.ascontiguousarray(inputs["L0_attn_k_weight"], dtype=np.float32)
    Wv = np.ascontiguousarray(inputs["L0_attn_v_weight"], dtype=np.float32)
    Wo = np.ascontiguousarray(inputs["L0_attn_o_weight"], dtype=np.float32)
    Wf = np.ascontiguousarray(inputs["L0_ffn_W"], dtype=np.float32)
    A = np.ascontiguousarray(inputs["L0_ffn_A"], dtype=np.float32)
    Bf = np.ascontiguousarray(inputs["L0_ffn_Bf"], dtype=np.float32)

    shared = dict(
        wq_t=np.ascontiguousarray(
            Wq.reshape(32, 128, 32, 128).transpose(0, 3, 2, 1)),
        wk_t=np.ascontiguousarray(
            Wk.reshape(8, 128, 32, 128).transpose(0, 3, 2, 1)),
        wv_ic=np.ascontiguousarray(
            Wv.reshape(1024, 32, 128).transpose(1, 2, 0)),
        wo_t=np.ascontiguousarray(
            Wo.reshape(8, 512, 32, 128).transpose(0, 2, 3, 1)),
        wf_t=np.ascontiguousarray(
            Wf.reshape(8, 512, 32, 128).transpose(0, 2, 3, 1)),
        wq_n=Wq, wk_n=Wk, wv_n=Wv, wo_n=Wo,
        qs_cm=np.ascontiguousarray(
            inputs["L0_attn_q_scale"].reshape(32, 128).T.astype(np.float32)),
        ks_cm=np.ascontiguousarray(
            inputs["L0_attn_k_scale"].reshape(8, 128).T.astype(np.float32)),
        vs_cm=np.ascontiguousarray(
            inputs["L0_attn_v_scale"].reshape(8, 128).T.astype(np.float32)),
        os_row=np.ascontiguousarray(
            inputs["L0_attn_o_scale"].reshape(1, 4096).astype(np.float32)),
        bias_row=np.ascontiguousarray(
            inputs["L0_ffn_bias"].reshape(1, 4096).astype(np.float32)),
        ap_t=np.ascontiguousarray(
            A.reshape(256, 32, 128).transpose(2, 1, 0)),
        b_t=np.ascontiguousarray(Bf.transpose(0, 2, 1).reshape(256, 4096)),
        cs_col=np.ascontiguousarray(
            inputs["L0_ffn_coeffs"].reshape(16, 1).astype(np.float32)),
        ss_col=np.ascontiguousarray(
            inputs["L0_ffn_scales"].reshape(16, 1).astype(np.float32)),
        repR=np.ascontiguousarray(
            (np.arange(128)[None, :] // 16 ==
             (np.arange(16) % 8)[:, None]).astype(BF)),
        selC=np.ascontiguousarray(
            (np.arange(2)[None, :] ==
             (np.arange(16) // 8)[:, None]).astype(BF)),
        ones_bf=np.ones((128, 128), dtype=BF),
        ident_bf=np.eye(128, dtype=BF),
    )
    in_maps = []
    for c in range(8):
        b, oh = c // 2, c % 2
        po = 1 - oh
        xc = np.concatenate([hs[b, 1024 * oh:1024 * oh + 1024],
                             hs[b, 1024 * po:1024 * po + 1024]], axis=0)
        m = dict(shared)
        m["x"] = np.ascontiguousarray(xc)
        in_maps.append(m)
    return in_maps


def kernel(**inputs):
    in_maps = _layout_inputs(inputs)
    nc = build_program()
    res = run_bass_kernel_spmd(nc, in_maps, core_ids=list(range(8)))
    o = np.empty((4, 2048, 4096), dtype=np.float32)
    for c in range(8):
        b, oh = c // 2, c % 2
        o[b, 1024 * oh:1024 * oh + 1024] = res.results[c]["out"]
    return o


# version 05
# speedup vs baseline: 1.5337x; 1.5337x over previous
import numpy as np
import ml_dtypes
from contextlib import ExitStack

import concourse.bass as bass
import concourse.mybir as mybir
from concourse.bass_utils import run_bass_kernel_spmd

F32 = mybir.dt.float32
BF16 = mybir.dt.bfloat16
AX = mybir.AxisListType.X
AF = mybir.ActivationFunctionType
ALU = mybir.AluOpType
BF = ml_dtypes.bfloat16

NH = 32
NKV = 8
EPS = 1e-5
RSQ = 1.0 / np.sqrt(128.0)

ENGINES = ("sync", "gpsimd", "vector", "scalar", "tensor")


class Sched:
    def __init__(self):
        self.q = {e: [] for e in ENGINES}
        self.count = {}
        self.last_wait = {e: {} for e in ENGINES}

    def tok(self, sem):
        return (sem, self.count.get(sem, 0))

    def op(self, eng, build, waits=(), inc=None, inc_by=1):
        w = []
        for t in waits:
            if t is None:
                continue
            s, v = t
            if v > self.last_wait[eng].get(s, 0):
                w.append((s, v))
                self.last_wait[eng][s] = v
        tok = None
        if inc is not None:
            self.count[inc] = self.count.get(inc, 0) + inc_by
            tok = (inc, self.count[inc])
        self.q[eng].append((w, build, inc, inc_by))
        return tok


def build_program():
    nc = bass.Bass("TRN2", target_bir_lowering=False, debug=False)

    dI = lambda n, s, dt=F32: nc.dram_tensor(n, list(s), dt, kind="ExternalInput").ap()
    x = dI("x", (2048, 4096))
    wq_t = dI("wq_t", (32, 128, 32, 128))
    wk_t = dI("wk_t", (8, 128, 32, 128))
    wv_ic = dI("wv_ic", (32, 128, 1024))
    wo_t = dI("wo_t", (8, 32, 128, 512))
    wf_t = dI("wf_t", (8, 32, 128, 512))
    wq_n = dI("wq_n", (4096, 4096))
    wk_n = dI("wk_n", (1024, 4096))
    wv_n = dI("wv_n", (1024, 4096))
    wo_n = dI("wo_n", (4096, 4096))
    qs_cm = dI("qs_cm", (128, 32))
    ks_cm = dI("ks_cm", (128, 8))
    vs_cm = dI("vs_cm", (128, 8))
    os_row = dI("os_row", (1, 4096))
    bias_row = dI("bias_row", (1, 4096))
    ap_t = dI("ap_t", (128, 32, 256))
    b_t = dI("b_t", (256, 4096))
    cs_col = dI("cs_col", (16, 1))
    ss_col = dI("ss_col", (16, 1))
    repR = dI("repR", (16, 128), BF16)
    selC = dI("selC", (16, 2), BF16)
    ones_bf = dI("ones_bf", (128, 128), BF16)
    ident_bf = dI("ident_bf", (128, 128), BF16)

    out = nc.dram_tensor("out", [1024, 4096], F32, kind="ExternalOutput").ap()
    x1_spill = nc.dram_tensor("x1_spill", [1024, 4096], F32, kind="Internal").ap()
    d_brd = nc.dram_tensor("d_brd", [1, 4096], F32, kind="Internal").ap()
    d_brdb = nc.dram_tensor("d_brdb", [1, 4096], BF16, kind="Internal").ap()

    p = Sched()

    with ExitStack() as ctx:
        sb = lambda n, s, dt: ctx.enter_context(nc.sbuf_tensor(n, s, dt))
        BIG1 = sb("BIG1", [128, 32768], BF16)
        BIG2 = sb("BIG2", [128, 32768], BF16)
        BIG3 = sb("BIG3", [128, 32768], BF16)
        RING = sb("RING", [128, 2048], F32)
        SGN = sb("SGN", [128, 2048], BF16)
        SM = sb("SM", [128, 256], F32)
        CB = sb("CB", [128, 40], F32)
        CS = sb("CS", [128, 80], F32)
        CSR = sb("CSR", [128, 192], F32)
        ONE = sb("ONE", [128, 128], BF16)
        IDN = sb("IDN", [128, 128], BF16)
        WCM = sb("WCM", [128, 2], F32)
        RZ = sb("RZ", [16, 258], BF16)

        pb = [ctx.enter_context(nc.psum_tensor(f"pb{i}", [128, 512], F32))
              for i in range(8)]
        pbt6 = pb[6].ap().bitcast(BF16)
        pbt7 = pb[7].ap().bitcast(BF16)

        sems = {}
        for s in ("MS", "XS", "WR0", "WR1", "WM0", "WM1", "WM2", "WM3",
                  "SS0", "SS1", "XE", "SP", "XF", "OU", "GA", "GB", "GC",
                  "T", "V", "SC"):
            sems[s] = ctx.enter_context(nc.semaphore(s))

        blk = ctx.enter_context(nc.Block())

        ms_chain = [None]

        def dma(eng, dst, src, waits=(), sem=None):
            return p.op(eng, lambda e, d=dst, s=src: e.dma_start(out=d, in_=s),
                        waits=waits, inc=sem, inc_by=16)

        def dmams(dst, src, waits=()):
            t = dma("scalar", dst, src,
                    waits=list(waits) + [ms_chain[0]], sem="MS")
            ms_chain[0] = t
            return t

        def op(eng, fn, waits=()):
            return p.op(eng, fn, waits=waits,
                        inc={"vector": "V", "scalar": "SC", "tensor": "T"}[eng])

        bank_r = {}
        bank_w = {}

        t_one = dmams(ONE[:], ones_bf)
        t_idn = dmams(IDN[:], ident_bf)
        t_eps = op("vector", lambda e: e.memset(CB[:, 32:33], EPS))
        t_qs = dmams(CS[:, 0:32], qs_cm)
        t_ks = dmams(CS[:, 32:40], ks_cm)
        t_vs = dmams(CS[:, 40:48], vs_cm)

        s_chunks = []
        for c in range(32):
            s_chunks.append((wq_n[128 * c:128 * c + 128, :], c))
        for c in range(8):
            s_chunks.append((wk_n[128 * c:128 * c + 128, :], 32 + c))
        for c in range(8):
            s_chunks.append((wv_n[128 * c:128 * c + 128, :], 40 + c))
        for c in range(32):
            s_chunks.append((wo_n[128 * c:128 * c + 128, :], 48 + c))

        def s_stage(i):
            if i < 32:
                base = 0 if i % 2 == 0 else 4096
            elif i < 40:
                base = 16384 + (0 if i % 2 == 0 else 4096)
            elif i < 64:
                base = 24576 + (0 if i % 2 == 0 else 4096)
            else:
                return BIG1[:, (0 if i % 2 == 0 else 4096):
                            (0 if i % 2 == 0 else 4096) + 4096]
            return BIG3[:, base:base + 4096]

        s_state = {"i": 0, "red": {}, "era_wait": {}}
        s_red_tok = [None] * 80

        def s_step(n=1):
            for _ in range(n):
                i = s_state["i"]
                if i >= 80:
                    return
                s_state["i"] = i + 1
                buf = s_stage(i)
                w = [s_state["red"].get(i - 2)]
                if i in s_state["era_wait"]:
                    w.append(s_state["era_wait"][i])
                src, col = s_chunks[i]
                td = dma("gpsimd", buf, src, waits=[t for t in w if t],
                         sem="SS0" if i % 2 == 0 else "SS1")
                tr = op("vector", lambda e, b=buf, c=col: e.reduce_sum(
                    out=SM[:, c:c + 1], in_=b, axis=AX,
                    apply_absolute_value=True), waits=[td])
                s_state["red"][i] = tr
                s_red_tok[i] = tr

        def ln1_group(grp, stage, normb_of, nb_reuse, extra_w=(), nb_war0=None):
            S = [stage[:, 0:2048], stage[:, 2048:4096]]
            tt = [{} for _ in range(8)]
            ev_toks = []

            def head(tc):
                d = tt[tc]
                c0 = 160 + 16 * (tc % 2)
                d["c0"] = c0
                pv = tt[tc - 1] if tc > 0 else None
                nb = normb_of(tc)
                nbw = None
                if tc - nb_reuse >= 0:
                    nbw = tt[tc - nb_reuse].get("ttr")
                elif nb_war0 is not None:
                    nbw = nb_war0
                for h in range(2):
                    w = list(extra_w) if tc == 0 else []
                    if pv:
                        w += [pv.get("E%d" % h), pv.get("C%d" % h)]
                    d["A%d" % h] = dma(
                        "scalar", S[h],
                        x[1024 * grp + 128 * tc:1024 * grp + 128 * tc + 128,
                          2048 * h:2048 * h + 2048],
                        waits=[t for t in w if t], sem="XS")
                for h in range(2):
                    d["B%d" % h] = op("vector", lambda e, s=S[h], c=c0 + h:
                                      e.reduce_sum(out=SM[:, c:c + 1], in_=s,
                                                   axis=AX),
                                      waits=[d["A%d" % h]])
                for h in range(2):
                    d["C%d" % h] = op("scalar", lambda e, s=S[h], c=c0 + 2 + h,
                                      o=nb[:, 2048 * h:2048 * h + 2048]:
                                      e.activation(out=o, in_=s, func=AF.Square,
                                                   accum_out=SM[:, c:c + 1]),
                                      waits=[d["A%d" % h], nbw])

            def tail(tc):
                d = tt[tc]
                c0 = d["c0"]
                nb = normb_of(tc)
                tneg = op("vector", lambda e, c=c0: e.tensor_scalar(
                    out=SM[:, c + 4:c + 5], in0=SM[:, c:c + 1],
                    scalar1=SM[:, c + 1:c + 2], scalar2=-1.0 / 4096.0,
                    op0=ALU.add, op1=ALU.mult), waits=[d["B0"], d["B1"]])
                tmusq = op("vector", lambda e, c=c0: e.tensor_scalar(
                    out=SM[:, c + 5:c + 6], in0=SM[:, c + 4:c + 5],
                    scalar1=SM[:, c + 4:c + 5], scalar2=-1.0,
                    op0=ALU.mult, op1=ALU.mult), waits=[tneg])
                tbias = op("vector", lambda e, c=c0: e.tensor_scalar_add(
                    out=SM[:, c + 6:c + 7], in0=SM[:, c + 5:c + 6],
                    scalar1=EPS), waits=[tmusq])
                tcomb = op("vector", lambda e, c=c0: e.tensor_scalar(
                    out=SM[:, c + 7:c + 8], in0=SM[:, c + 2:c + 3],
                    scalar1=SM[:, c + 3:c + 4], scalar2=1.0 / 4096.0,
                    op0=ALU.add, op1=ALU.mult), waits=[d["C0"], d["C1"]])
                tln = op("scalar", lambda e, c=c0: e.activation(
                    out=SM[:, c + 8:c + 9], in_=SM[:, c + 7:c + 8],
                    func=AF.Ln, bias=SM[:, c + 6:c + 7]),
                    waits=[tcomb, tbias])
                trstd = op("scalar", lambda e, c=c0: e.activation(
                    out=SM[:, c + 9:c + 10], in_=SM[:, c + 8:c + 9],
                    func=AF.Exp, scale=-0.5), waits=[tln])
                for h in range(2):
                    d["E%d" % h] = op("vector", lambda e, s=S[h], c=c0,
                                      o=nb[:, 2048 * h:2048 * h + 2048]:
                                      e.tensor_scalar(
                        out=o, in0=s, scalar1=SM[:, c + 4:c + 5],
                        scalar2=SM[:, c + 9:c + 10],
                        op0=ALU.add, op1=ALU.mult),
                        waits=[trstd, d["C%d" % h]])
                ttr_last = None
                ev = None
                for g4 in range(4):
                    pt, bidx = (pbt6, 6) if g4 % 2 == 0 else (pbt7, 7)
                    en = d["E0"] if g4 < 2 else d["E1"]
                    ttr = None
                    for k in range(8):
                        ic = g4 * 8 + k
                        ttr = op("tensor", lambda e, icc=ic, ptt=pt, kk=k:
                                 e.transpose(ptt[:, 128 * kk:128 * kk + 128],
                                             nb[:, 128 * icc:128 * icc + 128],
                                             IDN[:]),
                                 waits=[en, t_idn, bank_r.get(bidx)])
                    bank_w[bidx] = ttr
                    ttr_last = ttr
                    ev = op("scalar", lambda e, ptt=pt, g=g4, tcc=tc: e.copy(
                        out=BIG1.rearrange("p (ic t) -> p ic t", t=1024)[
                            :, 8 * g:8 * g + 8, 128 * tcc:128 * tcc + 128],
                        in_=ptt[:, 0:1024].rearrange(
                            "p (k t) -> p k t", t=128)), waits=[ttr])
                    bank_r[bidx] = ev
                d["ttr"] = ttr_last
                d["ev"] = ev
                ev_toks.append(ev)
                if grp == 0:
                    s_step(1)

            for tc in range(8):
                head(tc)
                tail(tc)
            lastd = tt[7]
            return ev_toks, lastd["E1"], [t["ttr"] for t in tt]

        ringf = RING
        ringb = RING[:, 0:2048].bitcast(BF16)
        wq_state = {"cons": {}, "sgn_cons": {}}

        def w_quarter(src_q, slot, waits_extra=()):
            rap = ringf[:, 1024 * slot:1024 * slot + 1024]
            sap = SGN[:, 1024 * slot:1024 * slot + 1024]
            w = [wq_state["cons"].get(slot)] + list(waits_extra)
            td = dma("sync", rap, src_q, waits=[t for t in w if t],
                     sem="WR%d" % slot)
            tsg = op("scalar", lambda e, r=rap, s=sap: e.sign(out=s, in_=r),
                     waits=[td, wq_state["sgn_cons"].get(slot)])
            wq_state["cons"][slot] = tsg
            return tsg, sap

        stage_own = BIG2[:, 24576:32768].bitcast(F32)
        nb_own = lambda tc: BIG3[:, 16384 + 4096 * (tc % 2):
                                 16384 + 4096 * (tc % 2) + 4096]
        ev0, lnx0_n, ttr0 = ln1_group(0, stage_own, nb_own, 2)
        lnx0 = ev0[-1]
        s_state["era_wait"][32] = ev0[6]
        s_state["era_wait"][33] = ev0[7]

        qev = {}
        for qc in range(32):
            mmq = None
            for j in range(4):
                src = wq_t[qc][:, 8 * j:8 * j + 8, :]
                slot = (qc * 4 + j) % 2
                tsg, sap = w_quarter(src, slot)
                for k in range(8):
                    ic = 8 * j + k
                    for th in range(2):
                        b = 2 * (qc % 2) + th
                        w = [tsg]
                        if qc == 0 and j == 0 and k == 0:
                            w.append(lnx0)
                        if ic == 0 and th == 0:
                            w += [bank_r.get(b), bank_r.get(b + 1),
                                  bank_w.get(b), bank_w.get(b + 1)]
                        mmq = op("tensor", lambda e, s=sap, kk=k, icc=ic,
                                 t2=th, bb=b: e.matmul(
                            pb[bb][:], s[:, 128 * kk:128 * kk + 128],
                            BIG1[:, 1024 * icc + 512 * t2:
                                 1024 * icc + 512 * t2 + 512],
                            start=(icc == 0), stop=(icc == 31)),
                            waits=[t for t in w if t])
                wq_state["sgn_cons"][slot] = mmq
            for th in range(2):
                b = 2 * (qc % 2) + th
                bank_w[b] = mmq
                tev = op("vector", lambda e, q=qc, t2=th, bb=b: e.tensor_copy(
                    out=BIG2[:, 1024 * q + 512 * t2:1024 * q + 512 * t2 + 512],
                    in_=pb[bb][:]), waits=[mmq])
                bank_r[b] = tev
                qev[(qc, th)] = tev
            s_step(1)

        t_sq = op("vector", lambda e: e.tensor_scalar(
            out=SM[:, 80:112], in0=SM[:, 0:32], scalar1=1.0 / 4096.0,
            scalar2=1e-8, op0=ALU.mult, op1=ALU.max), waits=[s_red_tok[31]])
        t_sq2 = op("vector", lambda e: e.tensor_mul(
            out=SM[:, 80:112], in0=SM[:, 80:112], in1=CS[:, 0:32]),
            waits=[t_sq, t_qs])
        t_sk = op("vector", lambda e: e.tensor_scalar(
            out=SM[:, 112:120], in0=SM[:, 32:40], scalar1=1.0 / 4096.0,
            scalar2=1e-8, op0=ALU.mult, op1=ALU.max), waits=[s_red_tok[39]])
        t_sk2 = op("vector", lambda e: e.tensor_mul(
            out=SM[:, 112:120], in0=SM[:, 112:120], in1=CS[:, 32:40]),
            waits=[t_sk, t_ks])
        csc = None
        for h in range(NH):
            csc = op("vector", lambda e, hh=h: e.tensor_mul(
                out=CS[:, 48 + hh:49 + hh], in0=SM[:, 80 + hh:81 + hh],
                in1=SM[:, 112 + hh // 4:113 + hh // 4]), waits=[t_sq2, t_sk2])
        t_csc = op("vector", lambda e: e.tensor_scalar_mul(
            out=CS[:, 48:80], in0=CS[:, 48:80], scalar1=RSQ), waits=[csc])
        qscale = {}
        for h in range(NH):
            qscale[h] = op("vector", lambda e, hh=h: e.tensor_scalar_mul(
                out=BIG2[:, 1024 * hh:1024 * hh + 1024],
                in0=BIG2[:, 1024 * hh:1024 * hh + 1024],
                scalar1=CS[:, 48 + hh:49 + hh]),
                waits=[t_csc, qev[(h, 0)], qev[(h, 1)]])
        q_scaled = qscale[NH - 1]

        qm_state = {"h": 0, "mm": {}, "rm": {}}

        def qmax_steps(n):
            for _ in range(n):
                h = qm_state["h"]
                if h >= NH:
                    return
                qm_state["h"] = h + 1
                sl = h % 2
                sq = BIG3[:, 16384 + 1024 * sl:16384 + 1024 * sl + 1024]
                b = 4 + sl
                w = [qscale[h], qm_state["mm"].get(h - 2)]
                if h < 2:
                    w.append(s_red_tok[38])
                tq2 = op("vector", lambda e, hh=h, s=sq: e.tensor_mul(
                    out=s, in0=BIG2[:, 1024 * hh:1024 * hh + 1024],
                    in1=BIG2[:, 1024 * hh:1024 * hh + 1024]),
                    waits=[t for t in w if t])
                tmm = op("tensor", lambda e, s=sq, bb=b: e.matmul(
                    pb[bb][0:1, 0:512], ONE[:, 0:1], s[:, 0:512],
                    start=True, stop=False),
                    waits=[tq2, t_one, bank_r.get(b), bank_w.get(b)])
                tmm = op("tensor", lambda e, s=sq, bb=b: e.matmul(
                    pb[bb][0:1, 0:512], ONE[:, 0:1], s[:, 512:1024],
                    start=False, stop=True), waits=[])
                bank_w[b] = tmm
                qm_state["mm"][h] = tmm
                tmx = op("vector", lambda e, hh=h, bb=b: e.reduce_max(
                    out=CSR[0:1, hh:hh + 1], in_=pb[bb][0:1, 0:512],
                    axis=AX), waits=[tmm])
                bank_r[b] = tmx
                qm_state["rm"][h] = tmx

        def k_group(grp, lnx_tok, post_g=None):
            kev = None
            kev_toks = []
            for g in range(8):
                last_mm = None
                for j in range(4):
                    src = wk_t[g][:, 8 * j:8 * j + 8, :]
                    slot = (g * 4 + j) % 2
                    we = [lnx_tok] if g == 0 and j == 0 else []
                    tsg, sap = w_quarter(src, slot, waits_extra=we)
                    for k in range(8):
                        ic = 8 * j + k
                        for th in range(2):
                            b = 2 * (g % 2) + th
                            w = [tsg]
                            if ic == 0 and th == 0:
                                w += [bank_r.get(b), bank_r.get(b + 1),
                                      bank_w.get(b), bank_w.get(b + 1)]
                            last_mm = op("tensor", lambda e, s=sap, kk=k,
                                         icc=ic, t2=th, bb=b: e.matmul(
                                pb[bb][:], s[:, 128 * kk:128 * kk + 128],
                                BIG1[:, 1024 * icc + 512 * t2:
                                     1024 * icc + 512 * t2 + 512],
                                start=(icc == 0), stop=(icc == 31)),
                                waits=[t for t in w if t])
                    wq_state["sgn_cons"][slot] = last_mm
                for th in range(2):
                    b = 2 * (g % 2) + th
                    bank_w[b] = last_mm
                    col = 2048 * g + 1024 * grp + 512 * th
                    kev = op("scalar", lambda e, cc=col, bb=b: e.copy(
                        out=BIG3[:, cc:cc + 512], in_=pb[bb][:]),
                        waits=[last_mm] +
                              ([s_red_tok[31]] if grp == 0 and g == 0 else []))
                    bank_r[b] = kev
                kev_toks.append(kev)
                if grp == 0:
                    s_step(1)
                if post_g is not None:
                    post_g(g, kev)
            return kev, kev_toks

        k0_done, _ = k_group(0, lnx0, post_g=lambda g, kv: qmax_steps(4))
        qmax_steps(32)
        qm_done = qm_state["mm"][NH - 1]

        def v_group(grp, lnx_tok, ln1p_done=None, first_ev_w=()):
            vev = None
            mini_cons = {m: wq_state["cons"].get(m // 2) for m in range(4)}
            sgn_mini_cons = {m: wq_state["sgn_cons"].get(m // 2)
                             for m in range(4)}
            last_mm = None
            last_sign = None
            first = True
            for vc in range(2):
                for ic in range(32):
                    m = (vc * 32 + ic) % 4
                    rap = ringf[:, 512 * m:512 * m + 512]
                    sap = SGN[:, 512 * m:512 * m + 512]
                    w = [mini_cons.get(m)]
                    if first:
                        w.append(lnx_tok)
                    td = dma("sync", rap, wv_ic[ic][:, 512 * vc:512 * vc + 512],
                             waits=[t for t in w if t], sem="WM%d" % m)
                    tsg = op("scalar", lambda e, r=rap, s=sap: e.sign(
                        out=s, in_=r), waits=[td, sgn_mini_cons.get(m)])
                    mini_cons[m] = tsg
                    last_sign = tsg
                    for sl in range(8):
                        w2 = [tsg]
                        if ic == 0:
                            w2 += [bank_r.get(sl), bank_w.get(sl)]
                        last_mm = op("tensor", lambda e, s=sap, icc=ic,
                                     ss=sl: e.matmul(
                            pb[ss][:],
                            BIG1[:, 1024 * icc + 128 * ss:
                                 1024 * icc + 128 * ss + 128],
                            s[:], start=(icc == 0), stop=(icc == 31)),
                            waits=[t for t in w2 if t])
                    sgn_mini_cons[m] = last_mm
                    first = False
                    if grp == 0 and vc == 0 and ic % 2 == 0:
                        s_step(1)
                for sl in range(8):
                    bank_w[sl] = last_mm
                    col = 16384 + 1024 * (8 * grp + sl) + 512 * vc
                    we = [last_mm]
                    if vc == 0 and sl == 0:
                        we += list(first_ev_w)
                        if grp == 0:
                            we.append(s_red_tok[39])
                        if grp == 1 and ln1p_done:
                            we.append(ln1p_done)
                    vev = op("scalar", lambda e, cc=col, bb=sl: e.copy(
                        out=BIG3[:, cc:cc + 512], in_=pb[bb][:]), waits=we)
                    bank_r[sl] = vev
            wq_state["cons"][0] = last_sign
            wq_state["cons"][1] = last_sign
            wq_state["sgn_cons"][0] = last_mm
            wq_state["sgn_cons"][1] = last_mm
            return vev, last_mm, last_sign

        v0_done, v0_mm, v0_sg = v_group(0, lnx0, first_ev_w=[qm_done])

        stage_par = BIG3[:, 24576:32768].bitcast(F32)
        nb_par_ap = RING[:, 0:2048].bitcast(BF16)
        ev1, lnx1_n, ttr1 = ln1_group(
            1, stage_par, lambda tc: nb_par_ap, 1,
            extra_w=[v0_mm, v0_done, s_red_tok[62], s_red_tok[63]],
            nb_war0=v0_sg)
        lnx1_tok = ev1[-1]

        km_state = {"toks": {}, "mm": {}}

        def kmax_emit(g, kev_tok):
            sl = g % 2
            sq = BIG3[:, 24576 + 2048 * sl:24576 + 2048 * sl + 2048]
            b = 4 + sl
            w = [kev_tok, km_state["mm"].get(g - 2)]
            tk2 = op("vector", lambda e, gg=g, s=sq: e.tensor_mul(
                out=s, in0=BIG3[:, 2048 * gg:2048 * gg + 2048],
                in1=BIG3[:, 2048 * gg:2048 * gg + 2048]),
                waits=[t for t in w if t])
            tmm = None
            for kk in range(4):
                tmm = op("tensor", lambda e, k4=kk, s=sq, bb=b: e.matmul(
                    pb[bb][0:1, 0:512], ONE[:, 0:1],
                    s[:, 512 * k4:512 * k4 + 512],
                    start=(k4 == 0), stop=(k4 == 3)),
                    waits=[tk2, t_one, bank_r.get(b), bank_w.get(b)]
                    if kk == 0 else [])
            bank_w[b] = tmm
            km_state["mm"][g] = tmm
            tmx = op("vector", lambda e, gg=g, bb=b: e.reduce_max(
                out=CSR[0:1, 32 + gg:33 + gg], in_=pb[bb][0:1, 0:512],
                axis=AX), waits=[tmm])
            bank_r[b] = tmx
            km_state["toks"][g] = tmx

        kp_toks = {}

        def kpost(g, kev_tok):
            kp_toks[g] = kev_tok
            if g >= 1:
                kmax_emit(g - 1, kp_toks[g - 1])

        k1_done, _ = k_group(1, lnx1_tok, post_g=kpost)
        kmax_emit(7, kp_toks[7])
        km_done = km_state["mm"][7]
        kmax_last = km_state["toks"][7]

        tc2 = None
        for h in range(NH):
            tc2 = op("vector", lambda e, hh=h: e.tensor_mul(
                out=CSR[0:1, 64 + hh:65 + hh],
                in0=CSR[0:1, 0 + hh:1 + hh],
                in1=CSR[0:1, 32 + hh // 4:33 + hh // 4]),
                waits=[qm_state["rm"][h], km_state["toks"][h // 4]])
        tlnb = op("scalar", lambda e: e.activation(
            out=CSR[0:1, 96:128], in_=CSR[0:1, 64:96], func=AF.Ln,
            scale=1.0, bias=CB[0:1, 32:33]), waits=[tc2, t_eps])
        texpb = op("scalar", lambda e: e.activation(
            out=CSR[0:1, 128:160], in_=CSR[0:1, 96:128], func=AF.Exp,
            scale=0.5), waits=[tlnb])
        tnegb = op("vector", lambda e: e.tensor_scalar_mul(
            out=CSR[0:1, 160:192], in0=CSR[0:1, 128:160], scalar1=-1.0),
            waits=[texpb])
        t_cb0 = dmams(d_brd[0:1, 0:32], CSR[0:1, 160:192], waits=[tnegb])
        t_cb = dmams(CB[:, 0:32], d_brd[0:1, 0:32].broadcast_to([128, 32]),
                     waits=[t_cb0])

        v1_done, kv_mm, v1_sg = v_group(1, lnx1_tok, ln1p_done=lnx1_n,
                                        first_ev_w=[km_done])
        kv_done = v1_done

        t_sv = op("vector", lambda e: e.tensor_scalar(
            out=SM[:, 120:128], in0=SM[:, 40:48], scalar1=1.0 / 4096.0,
            scalar2=1e-8, op0=ALU.mult, op1=ALU.max), waits=[s_red_tok[47]])
        t_vsv = op("vector", lambda e: e.tensor_mul(
            out=SM[:, 120:128], in0=SM[:, 120:128], in1=CS[:, 40:48]),
            waits=[t_sv, t_vs])

        s_state["era_wait"][64] = kv_mm

        crow = BIG1[:, 16384:18432].bitcast(F32)
        pr_cols = [8192 + 512 * i for i in range(8)]
        pring = {}
        s_cons = {}
        pair_ev = {4: bank_r.get(4), 5: bank_r.get(5),
                   6: bank_r.get(6), 7: bank_r.get(7)}
        ring_i = [0]
        last_evA = None
        first_w = [t_cb, kv_done, kv_mm, q_scaled,
                   bank_r.get(2), bank_r.get(3)]
        blk_i = [0]

        for g in range(NKV):
            for tcol in range(2):
                for pair in range(2):
                    h0 = 4 * g + 2 * pair
                    evw = [pair_ev[4], pair_ev[5], pair_ev[6], pair_ev[7]]
                    exps = {}
                    avs = {}
                    rss = {}

                    def issue_qk(kc):
                        for i, h in enumerate((h0, h0 + 1)):
                            b = 2 * (kc % 2) + i
                            wts = [s_cons.get(b)]
                            if g == 0 and tcol == 0 and pair == 0 and kc < 2:
                                wts = wts + first_w + [bank_r.get(b),
                                                       bank_w.get(b)]
                            tmm = op("tensor", lambda e, gg=g, k=kc, hh=h,
                                     bb=b, t2=tcol: e.matmul(
                                pb[bb][:],
                                BIG3[:, 2048 * gg + 128 * k:
                                     2048 * gg + 128 * k + 128],
                                BIG2[:, 1024 * hh + 512 * t2:
                                     1024 * hh + 512 * t2 + 512],
                                start=True, stop=True),
                                waits=[t for t in wts if t])
                            bank_w[b] = tmm
                            slot = ring_i[0] % 8
                            ring_i[0] += 1
                            tex = op("scalar", lambda e, bb=b, hh=h, sl=slot:
                                     e.activation(
                                out=BIG1[:, pr_cols[sl]:pr_cols[sl] + 512],
                                in_=pb[bb][:], func=AF.Exp,
                                bias=CB[:, hh:hh + 1]),
                                waits=[tmm, t_cb, pring.get(slot), kv_mm])
                            s_cons[b] = tex
                            exps[(kc, i)] = (tex, slot)

                    def issue_avrs(kc):
                        for i in range(2):
                            tex, slot = exps[(kc, i)]
                            wts = [tex]
                            if kc == 0:
                                wts += evw
                            tav = op("tensor", lambda e, gg=g, k=kc, sl=slot,
                                     ab=4 + i: e.matmul(
                                pb[ab][:],
                                BIG3[:, 16384 + 1024 * k + 128 * gg:
                                     16384 + 1024 * k + 128 * gg + 128],
                                BIG1[:, pr_cols[sl]:pr_cols[sl] + 512],
                                start=(k == 0), stop=(k == 15)), waits=wts)
                            trs = op("tensor", lambda e, k=kc, sl=slot,
                                     rb=6 + i: e.matmul(
                                pb[rb][:], ONE[:],
                                BIG1[:, pr_cols[sl]:pr_cols[sl] + 512],
                                start=(k == 0), stop=(k == 15)),
                                waits=([t_one] + evw) if kc == 0 else [])
                            pring[slot] = trs
                            bank_w[4 + i] = tav
                            bank_w[6 + i] = trs
                            avs[i] = tav
                            rss[i] = trs

                    issue_qk(0)
                    for kc in range(1, 16):
                        issue_qk(kc)
                        issue_avrs(kc - 1)
                    issue_avrs(15)

                    for i in range(2):
                        h = h0 + i
                        trr = op("vector", lambda e, ii=i: e.reciprocal(
                            out=crow[:, 512 * ii:512 * ii + 512],
                            in_=pb[6 + ii][:]),
                            waits=[rss[i], pair_ev[4 + i]])
                        tevA = op("vector", lambda e, ii=i, hh=h, t2=tcol:
                                  e.tensor_tensor(
                            out=BIG2[:, 1024 * hh + 512 * t2:
                                     1024 * hh + 512 * t2 + 512],
                            in0=pb[4 + ii][:],
                            in1=crow[:, 512 * ii:512 * ii + 512],
                            op=ALU.mult), waits=[avs[i], trr])
                        pair_ev[4 + i] = tevA
                        pair_ev[6 + i] = tevA
                        bank_r[4 + i] = tevA
                        bank_r[6 + i] = tevA
                        last_evA = tevA
                    blk_i[0] += 1
                    if blk_i[0] <= 16:
                        s_step(1)
        attn_done = last_evA

        OST = BIG3[:, 0:4096]
        zone = BIG3[:, 4096:4224]
        evd = BIG3[:, 4224:4352]
        orow_b = BIG3[0:1, 4608:8704]
        osr_b = BIG3[0:1, 8704:12800]
        t_so = op("vector", lambda e: e.tensor_scalar(
            out=SM[:, 128:160], in0=SM[:, 48:80], scalar1=1.0 / 4096.0,
            scalar2=1e-8, op0=ALU.mult, op1=ALU.max),
            waits=[s_red_tok[79], attn_done])
        dp_z = op("vector", lambda e: e.memset(zone, 0.0), waits=[attn_done])
        dp_cp = op("vector", lambda e: e.tensor_copy(
            out=zone[:, 0:32], in_=SM[:, 128:160]), waits=[dp_z, t_so])
        dp_tr = op("tensor", lambda e: e.transpose(
            pbt6[:, 0:128], zone, IDN[:]),
            waits=[dp_cp, t_idn, bank_r.get(6), bank_w.get(6)])
        bank_w[6] = dp_tr
        dp_ev = op("scalar", lambda e: e.copy(out=evd, in_=pbt6[:, 0:128]),
                   waits=[dp_tr])
        bank_r[6] = dp_ev
        dp_row = dmams(orow_b, BIG3[0:32, 4224:4352], waits=[dp_ev])
        dp_os = dma("gpsimd", osr_b, os_row, waits=[attn_done], sem="GA")
        dp_mul = op("vector", lambda e: e.tensor_mul(
            out=orow_b, in0=orow_b, in1=osr_b), waits=[dp_row, dp_os])
        t_ost0 = dmams(d_brdb[0:1, 0:4096], orow_b, waits=[dp_mul])
        t_ost = dmams(OST, d_brdb[0:1, 0:4096].broadcast_to([128, 4096]),
                      waits=[t_ost0])

        o_ring_cons = {m: wq_state["cons"].get(m // 2) for m in range(4)}
        o_sgn_cons = {m: wq_state["sgn_cons"].get(m // 2) for m in range(4)}
        mm_prev_ocol = None
        oev = {}
        for ocol in range(8):
            mm_last = None
            for ic in range(32):
                m = (ocol * 32 + ic) % 4
                rap = ringf[:, 512 * m:512 * m + 512]
                sap = SGN[:, 512 * m:512 * m + 512]
                bap = ringb[:, 1024 * m:1024 * m + 512]
                w = [o_ring_cons.get(m)]
                if ocol == 0 and ic < 4:
                    w.append(attn_done)
                td = dma("sync", rap, wo_t[ocol, ic],
                         waits=[t for t in w if t], sem="WM%d" % m)
                tsg = op("scalar", lambda e, r=rap, s=sap: e.sign(
                    out=s, in_=r),
                    waits=[td, o_sgn_cons.get(m)] +
                          ([attn_done] if ocol == 0 and ic < 4 else []))
                tst = op("vector", lambda e, s=sap, b2=bap, icc=ic, oc=ocol:
                         e.scalar_tensor_tensor(
                    out=b2, in0=s,
                    scalar=SM[:, 120 + icc // 4:121 + icc // 4],
                    in1=OST[:, 512 * oc:512 * oc + 512],
                    op0=ALU.mult, op1=ALU.mult),
                    waits=[tsg, t_ost, t_vsv])
                o_sgn_cons[m] = tst
                for t8 in range(8):
                    w2 = [tst]
                    if ic == 0:
                        w2 += [bank_r.get(t8), bank_w.get(t8)]
                        if ocol > 0:
                            w2.append(mm_prev_ocol)
                    mm_last = op("tensor", lambda e, b2=bap, icc=ic, tt=t8:
                                 e.matmul(
                        pb[tt][:],
                        BIG2[:, 1024 * icc + 128 * tt:
                             1024 * icc + 128 * tt + 128],
                        b2[:], start=(icc == 0), stop=(icc == 31)),
                        waits=w2)
                o_ring_cons[m] = mm_last
            mm_prev_ocol = mm_last
            for t8 in range(8):
                bank_w[t8] = mm_last
                we = [mm_last]
                if ocol == 0 and t8 == 0:
                    we += [s_red_tok[78], s_red_tok[79]]
                tev = op("scalar", lambda e, tt=t8, oc=ocol: e.copy(
                    out=BIG1[:, 4096 * tt + 512 * oc:4096 * tt + 512 * oc + 512],
                    in_=pb[tt][:]), waits=we)
                bank_r[t8] = tev
                oev[(ocol, t8)] = tev
        o_done = oev[(7, 7)]
        o_mm_last = mm_prev_ocol

        xe1 = [BIG2[:, 0:8192].bitcast(F32), BIG2[:, 8192:16384].bitcast(F32)]
        e2buf = BIG2[:, 16384:24576].bitcast(F32)
        ejunk = BIG2[:, 24576:28672]
        lnx1_tm = BIG2[:, 28672:32768]
        et = [{} for _ in range(8)]
        spills = [None] * 8

        def e_head(tk):
            d = et[tk]
            c = 192 + 16 * (tk % 2)
            d["c"] = c
            orow = BIG1[:, 4096 * tk:4096 * tk + 4096]
            d["o"] = orow
            ordy = oev[(7, tk)]
            if tk < 2:
                wx = [o_mm_last]
            else:
                pv = et[tk - 2]
                wx = [spills[tk - 2], pv["H"], pv["J"]]
            d["A"] = dma("scalar", xe1[tk % 2],
                         x[128 * tk:128 * tk + 128, :],
                         waits=[t for t in wx if t], sem="XE")
            d["B"] = op("vector", lambda e, o=orow, cc=c: e.reduce_sum(
                out=SM[:, cc:cc + 1], in_=o, axis=AX), waits=[ordy])
            d["C"] = op("scalar", lambda e, o=orow, cc=c: e.activation(
                out=ejunk, in_=o, func=AF.Square,
                accum_out=SM[:, cc + 1:cc + 2]), waits=[ordy])
            tneg = op("vector", lambda e, cc=c: e.tensor_scalar_mul(
                out=SM[:, cc + 2:cc + 3], in0=SM[:, cc:cc + 1],
                scalar1=-1.0 / 4096.0), waits=[d["B"]])
            tmusq = op("vector", lambda e, cc=c: e.tensor_scalar(
                out=SM[:, cc + 3:cc + 4], in0=SM[:, cc + 2:cc + 3],
                scalar1=SM[:, cc + 2:cc + 3], scalar2=-1.0,
                op0=ALU.mult, op1=ALU.mult), waits=[tneg])
            tbias = op("vector", lambda e, cc=c: e.tensor_scalar_add(
                out=SM[:, cc + 4:cc + 5], in0=SM[:, cc + 3:cc + 4],
                scalar1=EPS), waits=[tmusq])
            tcomb = op("vector", lambda e, cc=c: e.tensor_scalar_mul(
                out=SM[:, cc + 5:cc + 6], in0=SM[:, cc + 1:cc + 2],
                scalar1=1.0 / 4096.0), waits=[d["C"]])
            tln = op("scalar", lambda e, cc=c: e.activation(
                out=SM[:, cc + 6:cc + 7], in_=SM[:, cc + 5:cc + 6],
                func=AF.Ln, bias=SM[:, cc + 4:cc + 5]),
                waits=[tcomb, tbias])
            trstd = op("scalar", lambda e, cc=c: e.activation(
                out=SM[:, cc + 7:cc + 8], in_=SM[:, cc + 6:cc + 7],
                func=AF.Exp, scale=-0.5), waits=[tln])
            d["nb"] = op("vector", lambda e, cc=c: e.tensor_mul(
                out=SM[:, cc + 8:cc + 9], in0=SM[:, cc + 2:cc + 3],
                in1=SM[:, cc + 7:cc + 8]), waits=[trstd])
            d["rstd"] = trstd

        def e_tail(tk):
            d = et[tk]
            c = d["c"]
            xb = xe1[tk % 2]
            tE = op("scalar", lambda e, o=d["o"], cc=c: e.activation(
                out=e2buf, in_=o, func=AF.Identity,
                scale=SM[:, cc + 7:cc + 8], bias=SM[:, cc + 8:cc + 9]),
                waits=[d["nb"]] +
                      ([et[tk - 1]["add"]] if tk > 0 else []))
            d["e2"] = tE
            tF = op("vector", lambda e, xbb=xb: e.tensor_tensor(
                out=xbb, in0=xbb, in1=e2buf, op=ALU.add),
                waits=[tE, d["A"]])
            d["add"] = tF
            spills[tk] = dma("scalar", x1_spill[128 * tk:128 * tk + 128, :],
                             xb, waits=[tF], sem="SP")
            tS1 = op("vector", lambda e, cc=c, xbb=xb: e.reduce_sum(
                out=SM[:, cc + 9:cc + 10], in_=xbb, axis=AX), waits=[tF])
            d["H"] = op("scalar", lambda e, cc=c, xbb=xb: e.activation(
                out=ejunk, in_=xbb, func=AF.Square,
                accum_out=SM[:, cc + 10:cc + 11]), waits=[tF])
            tneg3 = op("vector", lambda e, cc=c: e.tensor_scalar_mul(
                out=SM[:, cc + 11:cc + 12], in0=SM[:, cc + 9:cc + 10],
                scalar1=-1.0 / 4096.0), waits=[tS1])
            tmusq3 = op("vector", lambda e, cc=c: e.tensor_scalar(
                out=SM[:, cc + 12:cc + 13], in0=SM[:, cc + 11:cc + 12],
                scalar1=SM[:, cc + 11:cc + 12], scalar2=-1.0,
                op0=ALU.mult, op1=ALU.mult), waits=[tneg3])
            tbias3 = op("vector", lambda e, cc=c: e.tensor_scalar_add(
                out=SM[:, cc + 13:cc + 14], in0=SM[:, cc + 12:cc + 13],
                scalar1=EPS), waits=[tmusq3])
            tcomb3 = op("vector", lambda e, cc=c: e.tensor_scalar_mul(
                out=SM[:, cc + 14:cc + 15], in0=SM[:, cc + 10:cc + 11],
                scalar1=1.0 / 4096.0), waits=[d["H"]])
            tln3 = op("scalar", lambda e, cc=c: e.activation(
                out=SM[:, cc + 15:cc + 16], in_=SM[:, cc + 14:cc + 15],
                func=AF.Ln, bias=SM[:, cc + 13:cc + 14]),
                waits=[tcomb3, tbias3])
            trstd3 = op("scalar", lambda e, cc=c: e.activation(
                out=SM[:, cc + 14:cc + 15], in_=SM[:, cc + 15:cc + 16],
                func=AF.Exp, scale=-0.5), waits=[tln3])
            tJ = op("vector", lambda e, cc=c, xbb=xb: e.tensor_scalar(
                out=lnx1_tm, in0=xbb, scalar1=SM[:, cc + 11:cc + 12],
                scalar2=SM[:, cc + 14:cc + 15], op0=ALU.add, op1=ALU.mult),
                waits=[trstd3] +
                      ([et[tk - 1]["ttr"]] if tk > 0 else []))
            d["J"] = tJ
            ttr_last = None
            ev = None
            for g4 in range(4):
                pt, bidx = (pbt6, 6) if g4 % 2 == 0 else (pbt7, 7)
                ttr = None
                for k in range(8):
                    ic = g4 * 8 + k
                    ttr = op("tensor", lambda e, icc=ic, ptt=pt, kk=k:
                             e.transpose(ptt[:, 128 * kk:128 * kk + 128],
                                         lnx1_tm[:, 128 * icc:128 * icc + 128],
                                         IDN[:]),
                             waits=[tJ, t_idn, bank_r.get(bidx),
                                    bank_w.get(bidx)])
                bank_w[bidx] = ttr
                ttr_last = ttr
                ev = op("scalar", lambda e, ptt=pt, g=g4, t8=tk: e.copy(
                    out=BIG3.rearrange("p (ic t) -> p ic t", t=1024)[
                        :, 8 * g:8 * g + 8, 128 * t8:128 * t8 + 128],
                    in_=ptt[:, 0:1024].rearrange(
                        "p (k t) -> p k t", t=128)), waits=[ttr])
                bank_r[bidx] = ev
            d["ttr"] = ttr_last
            d["ev"] = ev

        e_head(0)
        for tk in range(8):
            if tk < 7:
                e_head(tk + 1)
            e_tail(tk)
        lnx1_done = et[7]["ev"]
        spill_last = spills[-1]

        bias_bf = BIG1[0:1, 18432:22528]
        t_bias = dma("gpsimd", bias_bf, bias_row,
                     waits=[et[4]["e2"], et[5]["e2"], et[4]["B"], et[5]["B"]],
                     sem="GC")
        t_cs = dmams(SM[0:16, 160:161], cs_col)
        t_ss = dmams(SM[0:16, 161:162], ss_col)
        t_rr = dmams(RZ[0:16, 0:128], repR)
        t_sc = dmams(RZ[0:16, 128:130], selC)
        t_w16 = op("vector", lambda e: e.tensor_mul(
            out=SM[0:16, 162:163], in0=SM[0:16, 160:161],
            in1=SM[0:16, 161:162]), waits=[t_cs, t_ss])
        t_wl = op("vector", lambda e: e.tensor_scalar_mul(
            out=RZ[0:16, 130:258], in0=RZ[0:16, 0:128],
            scalar1=SM[0:16, 162:163]), waits=[t_w16, t_rr])
        t_wmm = op("tensor", lambda e: e.matmul(
            pb[4][:, 0:2], RZ[0:16, 130:258], RZ[0:16, 128:130],
            start=True, stop=True),
            waits=[t_wl, t_sc, bank_r.get(4), bank_w.get(4)])
        bank_w[4] = t_wmm
        t_wcm = op("scalar", lambda e: e.copy(out=WCM[:], in_=pb[4][:, 0:2]),
                   waits=[t_wmm])
        bank_r[4] = t_wcm
        t_asb = dma("gpsimd", BIG1[:, 0:8192],
                    ap_t.rearrange("p ic r -> p (ic r)"),
                    waits=[et[0]["e2"], et[1]["e2"], et[0]["B"], et[1]["B"]],
                    sem="GA")
        t_bsb = dma("gpsimd",
                    BIG1[:, 8192:16384].rearrange("p (rc o) -> p rc o", rc=2),
                    b_t.rearrange("(rc p) o -> p rc o", p=128),
                    waits=[et[2]["e2"], et[3]["e2"], et[2]["B"], et[3]["B"]],
                    sem="GB")
        t_bw = op("vector", lambda e: e.tensor_scalar_mul(
            out=BIG1[:, 8192:12288], in0=BIG1[:, 8192:12288],
            scalar1=WCM[:, 0:1]), waits=[t_bsb, t_wcm])
        t_bw2 = op("vector", lambda e: e.tensor_scalar_mul(
            out=BIG1[:, 12288:16384], in0=BIG1[:, 12288:16384],
            scalar1=WCM[:, 1:2]), waits=[t_bsb, t_wcm])
        xa_ev = []
        for rc in range(2):
            for tcol in range(2):
                b = 2 * rc + tcol
                tmm = None
                for ic in range(32):
                    tmm = op("tensor", lambda e, icc=ic, r=rc, t2=tcol, bb=b:
                             e.matmul(
                        pb[bb][:],
                        BIG1[:, 256 * icc + 128 * r:256 * icc + 128 * r + 128],
                        BIG3[:, 1024 * icc + 512 * t2:
                             1024 * icc + 512 * t2 + 512],
                        start=(icc == 0), stop=(icc == 31)),
                        waits=[t_asb, lnx1_done, bank_r.get(b), bank_w.get(b)]
                        if ic == 0 else [])
                bank_w[b] = tmm
                tev = op("scalar", lambda e, r=rc, t2=tcol, bb=b: e.copy(
                    out=BIG1[:, 16384 + 1024 * r + 512 * t2:
                             16384 + 1024 * r + 512 * t2 + 512],
                    in_=pb[bb][:]), waits=[tmm, lnx1_done])
                bank_r[b] = tev
                xa_ev.append(tev)
        xa_done = xa_ev[-1]

        f_mini_cons = {m: o_sgn_cons.get(m) for m in range(4)}
        mm_last = None
        yev = {}
        for ocol in range(8):
            for ic in range(32):
                m = (ocol * 32 + ic) % 4
                sap = SGN[:, 512 * m:512 * m + 512]
                w = [f_mini_cons.get(m)]
                if ocol == 0 and ic < 4:
                    w += [xa_done]
                tdw = dma("gpsimd", sap, wf_t[ocol, ic],
                          waits=[t for t in w if t], sem="WM%d" % m)
                for t8 in range(8):
                    w2 = [tdw, lnx1_done]
                    if ic == 0:
                        w2 += [bank_r.get(t8), bank_w.get(t8)]
                    mm_last = op("tensor", lambda e, s=sap, icc=ic, tt=t8:
                                 e.matmul(
                        pb[tt][:],
                        BIG3[:, 1024 * icc + 128 * tt:
                             1024 * icc + 128 * tt + 128],
                        s[:], start=(icc == 0), stop=False), waits=w2)
                f_mini_cons[m] = mm_last
            for t8 in range(8):
                for rc in range(2):
                    mm_last = op("tensor", lambda e, tt=t8, r=rc, oc=ocol:
                                 e.matmul(
                        pb[tt][:],
                        BIG1[:, 16384 + 1024 * r + 128 * tt:
                             16384 + 1024 * r + 128 * tt + 128],
                        BIG1[:, 8192 + 4096 * r + 512 * oc:
                             8192 + 4096 * r + 512 * oc + 512],
                        start=False, stop=False),
                        waits=[t_bw, t_bw2, xa_done]
                        if (ocol == 0 and t8 == 0 and rc == 0) else [])
                mm_last = op("tensor", lambda e, tt=t8, oc=ocol: e.matmul(
                    pb[tt][:], ONE[0:1, :],
                    bias_bf[:, 512 * oc:512 * oc + 512],
                    start=False, stop=True),
                    waits=[t_bias, t_one]
                    if (ocol == 0 and t8 == 0) else [])
                bank_w[t8] = mm_last
                tev = op("scalar", lambda e, tt=t8, oc=ocol: e.copy(
                    out=BIG2[:, 4096 * tt + 512 * oc:4096 * tt + 512 * oc + 512],
                    in_=pb[tt][:]),
                    waits=[mm_last, spill_last] +
                          ([lnx1_done] if ocol == 0 else []))
                bank_r[t8] = tev
                yev[(ocol, t8)] = tev
        fmm = mm_last

        f1 = [BIG3[:, 0:8192].bitcast(F32), BIG3[:, 8192:16384].bitcast(F32)]
        xst = [BIG3[:, 16384:24576].bitcast(F32),
               BIG3[:, 24576:32768].bitcast(F32)]
        fjunk = BIG1[:, 0:4096]
        f2buf = BIG1[:, 4096:12288].bitcast(F32)
        ft = [{} for _ in range(8)]
        out_toks = [None] * 8

        def f_head(tk):
            d = ft[tk]
            c = 224 + 16 * (tk % 2)
            d["c"] = c
            yrow = BIG2[:, 4096 * tk:4096 * tk + 4096]
            d["y"] = yrow
            yrdy = yev[(7, tk)]
            wx = [spills[tk]]
            if tk < 2:
                wx.append(fmm)
            else:
                wx.append(ft[tk - 2]["aff"])
            d["A"] = dma("scalar", xst[tk % 2],
                         x1_spill[128 * tk:128 * tk + 128, :],
                         waits=[t for t in wx if t], sem="XF")
            d["B"] = op("vector", lambda e, o=yrow, cc=c: e.reduce_sum(
                out=SM[:, cc:cc + 1], in_=o, axis=AX),
                waits=[yrdy] + ([fmm] if tk == 0 else []))
            d["C"] = op("scalar", lambda e, o=yrow, cc=c: e.activation(
                out=fjunk, in_=o, func=AF.Square,
                accum_out=SM[:, cc + 1:cc + 2]),
                waits=[yrdy] + ([fmm] if tk == 0 else []))
            tneg = op("vector", lambda e, cc=c: e.tensor_scalar_mul(
                out=SM[:, cc + 2:cc + 3], in0=SM[:, cc:cc + 1],
                scalar1=-1.0 / 4096.0), waits=[d["B"]])
            tmusq = op("vector", lambda e, cc=c: e.tensor_scalar(
                out=SM[:, cc + 3:cc + 4], in0=SM[:, cc + 2:cc + 3],
                scalar1=SM[:, cc + 2:cc + 3], scalar2=-1.0,
                op0=ALU.mult, op1=ALU.mult), waits=[tneg])
            tbias = op("vector", lambda e, cc=c: e.tensor_scalar_add(
                out=SM[:, cc + 4:cc + 5], in0=SM[:, cc + 3:cc + 4],
                scalar1=EPS), waits=[tmusq])
            tcomb = op("vector", lambda e, cc=c: e.tensor_scalar_mul(
                out=SM[:, cc + 5:cc + 6], in0=SM[:, cc + 1:cc + 2],
                scalar1=1.0 / 4096.0), waits=[d["C"]])
            tln = op("scalar", lambda e, cc=c: e.activation(
                out=SM[:, cc + 6:cc + 7], in_=SM[:, cc + 5:cc + 6],
                func=AF.Ln, bias=SM[:, cc + 4:cc + 5]),
                waits=[tcomb, tbias])
            trstd = op("scalar", lambda e, cc=c: e.activation(
                out=SM[:, cc + 7:cc + 8], in_=SM[:, cc + 6:cc + 7],
                func=AF.Exp, scale=-0.5), waits=[tln])
            d["nb"] = op("vector", lambda e, cc=c: e.tensor_mul(
                out=SM[:, cc + 8:cc + 9], in0=SM[:, cc + 2:cc + 3],
                in1=SM[:, cc + 7:cc + 8]), waits=[trstd])
            d["rstd"] = trstd

        def f_tail(tk):
            d = ft[tk]
            c = d["c"]
            tf2 = op("scalar", lambda e, cc=c, o=d["y"]: e.activation(
                out=f2buf, in_=o, func=AF.Identity,
                scale=SM[:, cc + 7:cc + 8], bias=SM[:, cc + 8:cc + 9]),
                waits=[d["nb"]] +
                      ([ft[tk - 1]["aff"]] if tk > 0 else [fmm]))
            wf1 = [tf2, d["A"]]
            if tk < 2:
                wf1.append(fmm)
            else:
                wf1.append(out_toks[tk - 2])
            d["aff"] = op("vector", lambda e, xs=xst[tk % 2],
                          fb=f1[tk % 2]: e.tensor_tensor(
                out=fb, in0=xs, in1=f2buf, op=ALU.add),
                waits=[t for t in wf1 if t])
            out_toks[tk] = dma("sync", out[128 * tk:128 * tk + 128, :],
                               f1[tk % 2], waits=[d["aff"]], sem="OU")

        f_head(0)
        for tk in range(8):
            if tk < 7:
                f_head(tk + 1)
            f_tail(tk)

        p.op("sync", lambda e: None, waits=[out_toks[-1]])

        def emit(eng_name):
            def body(eng):
                for (waits, build, inc, inc_by) in p.q[eng_name]:
                    for (s, v) in waits:
                        eng.wait_ge(sems[s], v)
                    inst = build(eng)
                    if inc is not None:
                        inst.then_inc(sems[inc], inc_by)
            return body

        blk.sync(emit("sync"))
        blk.gpsimd(emit("gpsimd"))
        blk.vector(emit("vector"))
        blk.scalar(emit("scalar"))
        blk.tensor(emit("tensor"))

    return nc


def _layout_inputs(inputs):
    hs = np.ascontiguousarray(inputs["L0_attn_hidden_states"], dtype=np.float32)
    Wq = np.ascontiguousarray(inputs["L0_attn_q_weight"], dtype=np.float32)
    Wk = np.ascontiguousarray(inputs["L0_attn_k_weight"], dtype=np.float32)
    Wv = np.ascontiguousarray(inputs["L0_attn_v_weight"], dtype=np.float32)
    Wo = np.ascontiguousarray(inputs["L0_attn_o_weight"], dtype=np.float32)
    Wf = np.ascontiguousarray(inputs["L0_ffn_W"], dtype=np.float32)
    A = np.ascontiguousarray(inputs["L0_ffn_A"], dtype=np.float32)
    Bf = np.ascontiguousarray(inputs["L0_ffn_Bf"], dtype=np.float32)

    shared = dict(
        wq_t=np.ascontiguousarray(
            Wq.reshape(32, 128, 32, 128).transpose(0, 3, 2, 1)),
        wk_t=np.ascontiguousarray(
            Wk.reshape(8, 128, 32, 128).transpose(0, 3, 2, 1)),
        wv_ic=np.ascontiguousarray(
            Wv.reshape(1024, 32, 128).transpose(1, 2, 0)),
        wo_t=np.ascontiguousarray(
            Wo.reshape(8, 512, 32, 128).transpose(0, 2, 3, 1)),
        wf_t=np.ascontiguousarray(
            Wf.reshape(8, 512, 32, 128).transpose(0, 2, 3, 1)),
        wq_n=Wq, wk_n=Wk, wv_n=Wv, wo_n=Wo,
        qs_cm=np.ascontiguousarray(
            inputs["L0_attn_q_scale"].reshape(32, 128).T.astype(np.float32)),
        ks_cm=np.ascontiguousarray(
            inputs["L0_attn_k_scale"].reshape(8, 128).T.astype(np.float32)),
        vs_cm=np.ascontiguousarray(
            inputs["L0_attn_v_scale"].reshape(8, 128).T.astype(np.float32)),
        os_row=np.ascontiguousarray(
            inputs["L0_attn_o_scale"].reshape(1, 4096).astype(np.float32)),
        bias_row=np.ascontiguousarray(
            inputs["L0_ffn_bias"].reshape(1, 4096).astype(np.float32)),
        ap_t=np.ascontiguousarray(
            A.reshape(256, 32, 128).transpose(2, 1, 0)),
        b_t=np.ascontiguousarray(Bf.transpose(0, 2, 1).reshape(256, 4096)),
        cs_col=np.ascontiguousarray(
            inputs["L0_ffn_coeffs"].reshape(16, 1).astype(np.float32)),
        ss_col=np.ascontiguousarray(
            inputs["L0_ffn_scales"].reshape(16, 1).astype(np.float32)),
        repR=np.ascontiguousarray(
            (np.arange(128)[None, :] // 16 ==
             (np.arange(16) % 8)[:, None]).astype(BF)),
        selC=np.ascontiguousarray(
            (np.arange(2)[None, :] ==
             (np.arange(16) // 8)[:, None]).astype(BF)),
        ones_bf=np.ones((128, 128), dtype=BF),
        ident_bf=np.eye(128, dtype=BF),
    )
    in_maps = []
    for c in range(8):
        b, oh = c // 2, c % 2
        po = 1 - oh
        xc = np.concatenate([hs[b, 1024 * oh:1024 * oh + 1024],
                             hs[b, 1024 * po:1024 * po + 1024]], axis=0)
        m = dict(shared)
        m["x"] = np.ascontiguousarray(xc)
        in_maps.append(m)
    return in_maps


def kernel(**inputs):
    in_maps = _layout_inputs(inputs)
    nc = build_program()
    res = run_bass_kernel_spmd(nc, in_maps, core_ids=list(range(8)))
    o = np.empty((4, 2048, 4096), dtype=np.float32)
    for c in range(8):
        b, oh = c // 2, c % 2
        o[b, 1024 * oh:1024 * oh + 1024] = res.results[c]["out"]
    return o


# version 06
# speedup vs baseline: 1.5703x; 1.0239x over previous
import numpy as np
import ml_dtypes
from contextlib import ExitStack

import concourse.bass as bass
import concourse.mybir as mybir
from concourse.bass_utils import run_bass_kernel_spmd

F32 = mybir.dt.float32
BF16 = mybir.dt.bfloat16
AX = mybir.AxisListType.X
AF = mybir.ActivationFunctionType
ALU = mybir.AluOpType
BF = ml_dtypes.bfloat16

NH = 32
NKV = 8
EPS = 1e-5
RSQ = 1.0 / np.sqrt(128.0)

ENGINES = ("sync", "gpsimd", "vector", "scalar", "tensor")


class Sched:
    def __init__(self):
        self.q = {e: [] for e in ENGINES}
        self.count = {}
        self.last_wait = {e: {} for e in ENGINES}

    def tok(self, sem):
        return (sem, self.count.get(sem, 0))

    def op(self, eng, build, waits=(), inc=None, inc_by=1):
        w = []
        for t in waits:
            if t is None:
                continue
            s, v = t
            if v > self.last_wait[eng].get(s, 0):
                w.append((s, v))
                self.last_wait[eng][s] = v
        tok = None
        if inc is not None:
            self.count[inc] = self.count.get(inc, 0) + inc_by
            tok = (inc, self.count[inc])
        self.q[eng].append((w, build, inc, inc_by))
        return tok


def build_program():
    nc = bass.Bass("TRN2", target_bir_lowering=False, debug=False)

    dI = lambda n, s, dt=F32: nc.dram_tensor(n, list(s), dt, kind="ExternalInput").ap()
    x = dI("x", (2048, 4096))
    wq_t = dI("wq_t", (32, 128, 32, 128))
    wk_t = dI("wk_t", (8, 128, 32, 128))
    wv_ic = dI("wv_ic", (32, 128, 1024))
    wo_t = dI("wo_t", (8, 32, 128, 512))
    wf_t = dI("wf_t", (8, 32, 128, 512))
    wq_n = dI("wq_n", (4096, 4096))
    wk_n = dI("wk_n", (1024, 4096))
    wv_n = dI("wv_n", (1024, 4096))
    wo_n = dI("wo_n", (4096, 4096))
    qs_cm = dI("qs_cm", (128, 32))
    ks_cm = dI("ks_cm", (128, 8))
    vs_cm = dI("vs_cm", (128, 8))
    os_row = dI("os_row", (1, 4096))
    bias_row = dI("bias_row", (1, 4096))
    ap_t = dI("ap_t", (128, 32, 256))
    b_t = dI("b_t", (256, 4096))
    cs_col = dI("cs_col", (16, 1))
    ss_col = dI("ss_col", (16, 1))
    repR = dI("repR", (16, 128), BF16)
    selC = dI("selC", (16, 2), BF16)
    ones_bf = dI("ones_bf", (128, 128), BF16)
    ident_bf = dI("ident_bf", (128, 128), BF16)

    out = nc.dram_tensor("out", [1024, 4096], F32, kind="ExternalOutput").ap()
    x1_spill = nc.dram_tensor("x1_spill", [1024, 4096], F32, kind="Internal").ap()
    d_brd = nc.dram_tensor("d_brd", [1, 4096], F32, kind="Internal").ap()
    d_brdb = nc.dram_tensor("d_brdb", [1, 4096], BF16, kind="Internal").ap()

    p = Sched()

    with ExitStack() as ctx:
        sb = lambda n, s, dt: ctx.enter_context(nc.sbuf_tensor(n, s, dt))
        BIG1 = sb("BIG1", [128, 32768], BF16)
        BIG2 = sb("BIG2", [128, 32768], BF16)
        BIG3 = sb("BIG3", [128, 32768], BF16)
        RING = sb("RING", [128, 2048], F32)
        SGN = sb("SGN", [128, 2048], BF16)
        SM = sb("SM", [128, 256], F32)
        CB = sb("CB", [128, 40], F32)
        CS = sb("CS", [128, 80], F32)
        CSR = sb("CSR", [128, 192], F32)
        ONE = sb("ONE", [128, 128], BF16)
        IDN = sb("IDN", [128, 128], BF16)
        WCM = sb("WCM", [128, 2], F32)
        RZ = sb("RZ", [16, 258], BF16)

        pb = [ctx.enter_context(nc.psum_tensor(f"pb{i}", [128, 512], F32))
              for i in range(8)]
        pbt6 = pb[6].ap().bitcast(BF16)
        pbt7 = pb[7].ap().bitcast(BF16)

        sems = {}
        for s in ("MS", "XS", "WR0", "WR1", "WM0", "WM1", "WM2", "WM3",
                  "SS0", "SS1", "XE", "SP", "XF", "OU", "GA", "GB", "GC",
                  "T", "V", "SC"):
            sems[s] = ctx.enter_context(nc.semaphore(s))

        blk = ctx.enter_context(nc.Block())

        ms_chain = [None]

        def dma(eng, dst, src, waits=(), sem=None):
            return p.op(eng, lambda e, d=dst, s=src: e.dma_start(out=d, in_=s),
                        waits=waits, inc=sem, inc_by=16)

        def dmams(dst, src, waits=()):
            t = dma("scalar", dst, src,
                    waits=list(waits) + [ms_chain[0]], sem="MS")
            ms_chain[0] = t
            return t

        def op(eng, fn, waits=()):
            return p.op(eng, fn, waits=waits,
                        inc={"vector": "V", "scalar": "SC", "tensor": "T"}[eng])

        bank_r = {}
        bank_w = {}

        t_one = dmams(ONE[:], ones_bf)
        t_idn = dmams(IDN[:], ident_bf)
        t_eps = op("vector", lambda e: e.memset(CB[:, 32:33], EPS))
        t_qs = dmams(CS[:, 0:32], qs_cm)
        t_ks = dmams(CS[:, 32:40], ks_cm)
        t_vs = dmams(CS[:, 40:48], vs_cm)

        s_chunks = []
        for c in range(32):
            s_chunks.append((wq_n[128 * c:128 * c + 128, :], c))
        for c in range(8):
            s_chunks.append((wk_n[128 * c:128 * c + 128, :], 32 + c))
        for c in range(8):
            s_chunks.append((wv_n[128 * c:128 * c + 128, :], 40 + c))
        for c in range(32):
            s_chunks.append((wo_n[128 * c:128 * c + 128, :], 48 + c))

        def s_stage(i):
            if i < 32:
                base = 0 if i % 2 == 0 else 4096
            elif i < 40:
                base = 16384 + (0 if i % 2 == 0 else 4096)
            elif i < 64:
                base = 24576 + (0 if i % 2 == 0 else 4096)
            else:
                return BIG1[:, (0 if i % 2 == 0 else 4096):
                            (0 if i % 2 == 0 else 4096) + 4096]
            return BIG3[:, base:base + 4096]

        s_state = {"i": 0, "red": {}, "era_wait": {}}
        s_red_tok = [None] * 80

        def s_step(n=1):
            for _ in range(n):
                i = s_state["i"]
                if i >= 80:
                    return
                s_state["i"] = i + 1
                buf = s_stage(i)
                w = [s_state["red"].get(i - 2)]
                if i in s_state["era_wait"]:
                    w.append(s_state["era_wait"][i])
                src, col = s_chunks[i]
                td = dma("gpsimd", buf, src, waits=[t for t in w if t],
                         sem="SS0" if i % 2 == 0 else "SS1")
                tr = op("vector", lambda e, b=buf, c=col: e.reduce_sum(
                    out=SM[:, c:c + 1], in_=b, axis=AX,
                    apply_absolute_value=True), waits=[td])
                s_state["red"][i] = tr
                s_red_tok[i] = tr

        def ln1_group(grp, stage, normb_of, nb_reuse, extra_w=(), nb_war0=None,
                      stage_b=None, junk_of=None, stash=False):
            stages = [stage] + ([stage_b] if stage_b is not None else [])
            nstg = len(stages)
            tt = [{} for _ in range(8)]
            ev_toks = []

            def head(tc):
                d = tt[tc]
                c0 = 160 + 16 * (tc % 2)
                d["c0"] = c0
                stg = stages[tc % nstg]
                S = [stg[:, 0:2048], stg[:, 2048:4096]]
                d["S"] = S
                pv = tt[tc - nstg] if tc - nstg >= 0 else None
                nb = normb_of(tc)
                if junk_of is not None:
                    jks = [junk_of(tc), junk_of(tc)]
                else:
                    jks = [nb[:, 0:2048], nb[:, 2048:4096]]
                nbw = None
                if tc - nb_reuse >= 0:
                    nbw = tt[tc - nb_reuse].get("ttr")
                elif nb_war0 is not None:
                    nbw = nb_war0
                for h in range(2):
                    w = list(extra_w) if tc < nstg else []
                    if pv:
                        w += [pv.get("E%d" % h), pv.get("C%d" % h)]
                    d["A%d" % h] = dma(
                        "scalar", S[h],
                        x[1024 * grp + 128 * tc:1024 * grp + 128 * tc + 128,
                          2048 * h:2048 * h + 2048],
                        waits=[t for t in w if t], sem="XS")
                for h in range(2):
                    d["B%d" % h] = op("vector", lambda e, s=S[h], c=c0 + h:
                                      e.reduce_sum(out=SM[:, c:c + 1], in_=s,
                                                   axis=AX),
                                      waits=[d["A%d" % h]])
                if stash:
                    d["stash"] = op("vector", lambda e, c=c0, tcc=tc:
                                    e.tensor_scalar(
                        out=CSR[:, 48 + tcc:49 + tcc], in0=SM[:, c:c + 1],
                        scalar1=SM[:, c + 1:c + 2], scalar2=1.0,
                        op0=ALU.add, op1=ALU.mult),
                        waits=[d["B0"], d["B1"]])
                d["nbw"] = nbw
                for h in range(2):
                    d["C%d" % h] = op("scalar", lambda e, s=S[h], c=c0 + 2 + h,
                                      o=jks[h]:
                                      e.activation(out=o, in_=s, func=AF.Square,
                                                   accum_out=SM[:, c:c + 1]),
                                      waits=[d["A%d" % h]] +
                                            ([nbw] if junk_of is None else []))

            def tail(tc):
                d = tt[tc]
                c0 = d["c0"]
                S = d["S"]
                nb = normb_of(tc)
                tneg = op("vector", lambda e, c=c0: e.tensor_scalar(
                    out=SM[:, c + 4:c + 5], in0=SM[:, c:c + 1],
                    scalar1=SM[:, c + 1:c + 2], scalar2=-1.0 / 4096.0,
                    op0=ALU.add, op1=ALU.mult), waits=[d["B0"], d["B1"]])
                tmusq = op("vector", lambda e, c=c0: e.tensor_scalar(
                    out=SM[:, c + 5:c + 6], in0=SM[:, c + 4:c + 5],
                    scalar1=SM[:, c + 4:c + 5], scalar2=-1.0,
                    op0=ALU.mult, op1=ALU.mult), waits=[tneg])
                tbias = op("vector", lambda e, c=c0: e.tensor_scalar_add(
                    out=SM[:, c + 6:c + 7], in0=SM[:, c + 5:c + 6],
                    scalar1=EPS), waits=[tmusq])
                tcomb = op("vector", lambda e, c=c0: e.tensor_scalar(
                    out=SM[:, c + 7:c + 8], in0=SM[:, c + 2:c + 3],
                    scalar1=SM[:, c + 3:c + 4], scalar2=1.0 / 4096.0,
                    op0=ALU.add, op1=ALU.mult), waits=[d["C0"], d["C1"]])
                tln = op("scalar", lambda e, c=c0: e.activation(
                    out=SM[:, c + 8:c + 9], in_=SM[:, c + 7:c + 8],
                    func=AF.Ln, bias=SM[:, c + 6:c + 7]),
                    waits=[tcomb, tbias])
                trstd = op("scalar", lambda e, c=c0: e.activation(
                    out=SM[:, c + 9:c + 10], in_=SM[:, c + 8:c + 9],
                    func=AF.Exp, scale=-0.5), waits=[tln])
                for h in range(2):
                    d["E%d" % h] = op("vector", lambda e, s=S[h], c=c0,
                                      o=nb[:, 2048 * h:2048 * h + 2048]:
                                      e.tensor_scalar(
                        out=o, in0=s, scalar1=SM[:, c + 4:c + 5],
                        scalar2=SM[:, c + 9:c + 10],
                        op0=ALU.add, op1=ALU.mult),
                        waits=[trstd, d["C%d" % h]] +
                              ([d["nbw"]] if junk_of is not None else []))
                ttr_last = None
                ev = None
                for g4 in range(4):
                    pt, bidx = (pbt6, 6) if g4 % 2 == 0 else (pbt7, 7)
                    en = d["E0"] if g4 < 2 else d["E1"]
                    ttr = None
                    for k in range(8):
                        ic = g4 * 8 + k
                        ttr = op("tensor", lambda e, icc=ic, ptt=pt, kk=k:
                                 e.transpose(ptt[:, 128 * kk:128 * kk + 128],
                                             nb[:, 128 * icc:128 * icc + 128],
                                             IDN[:]),
                                 waits=[en, t_idn, bank_r.get(bidx)])
                    bank_w[bidx] = ttr
                    ttr_last = ttr
                    ev = op("scalar", lambda e, ptt=pt, g=g4, tcc=tc: e.copy(
                        out=BIG1.rearrange("p (ic t) -> p ic t", t=1024)[
                            :, 8 * g:8 * g + 8, 128 * tcc:128 * tcc + 128],
                        in_=ptt[:, 0:1024].rearrange(
                            "p (k t) -> p k t", t=128)), waits=[ttr])
                    bank_r[bidx] = ev
                d["ttr"] = ttr_last
                d["ev"] = ev
                ev_toks.append(ev)
                if grp == 0:
                    s_step(1)

            for tc in range(8):
                head(tc)
                tail(tc)
            lastd = tt[7]
            return (ev_toks, lastd["E1"], [t["ttr"] for t in tt],
                    [t.get("stash") for t in tt])

        ringf = RING
        ringb = RING[:, 0:2048].bitcast(BF16)
        wq_state = {"cons": {}, "sgn_cons": {}}

        def w_quarter(src_q, slot, waits_extra=()):
            rap = ringf[:, 1024 * slot:1024 * slot + 1024]
            sap = SGN[:, 1024 * slot:1024 * slot + 1024]
            w = [wq_state["cons"].get(slot)] + list(waits_extra)
            td = dma("sync", rap, src_q, waits=[t for t in w if t],
                     sem="WR%d" % slot)
            tsg = op("scalar", lambda e, r=rap, s=sap: e.sign(out=s, in_=r),
                     waits=[td, wq_state["sgn_cons"].get(slot)])
            wq_state["cons"][slot] = tsg
            return tsg, sap

        stage_own = BIG2[:, 24576:32768].bitcast(F32)
        stage_own_b = BIG3[:, 8192:16384].bitcast(F32)
        nb_own = lambda tc: BIG3[:, 16384 + 4096 * (tc % 2):
                                 16384 + 4096 * (tc % 2) + 4096]
        ev0, lnx0_n, ttr0, stash0 = ln1_group(0, stage_own, nb_own, 2,
                                              stage_b=stage_own_b, stash=True)
        lnx0 = ev0[-1]
        s_state["era_wait"][32] = ev0[6]
        s_state["era_wait"][33] = ev0[7]

        qev = {}
        for qc in range(32):
            mmq = None
            for j in range(4):
                src = wq_t[qc][:, 8 * j:8 * j + 8, :]
                slot = (qc * 4 + j) % 2
                tsg, sap = w_quarter(src, slot)
                for k in range(8):
                    ic = 8 * j + k
                    for th in range(2):
                        b = 2 * (qc % 2) + th
                        w = [tsg]
                        if qc == 0 and j == 0 and k == 0:
                            w.append(lnx0)
                        if ic == 0 and th == 0:
                            w += [bank_r.get(b), bank_r.get(b + 1),
                                  bank_w.get(b), bank_w.get(b + 1)]
                        mmq = op("tensor", lambda e, s=sap, kk=k, icc=ic,
                                 t2=th, bb=b: e.matmul(
                            pb[bb][:], s[:, 128 * kk:128 * kk + 128],
                            BIG1[:, 1024 * icc + 512 * t2:
                                 1024 * icc + 512 * t2 + 512],
                            start=(icc == 0), stop=(icc == 31)),
                            waits=[t for t in w if t])
                wq_state["sgn_cons"][slot] = mmq
            for th in range(2):
                b = 2 * (qc % 2) + th
                bank_w[b] = mmq
                tev = op("vector", lambda e, q=qc, t2=th, bb=b: e.tensor_copy(
                    out=BIG2[:, 1024 * q + 512 * t2:1024 * q + 512 * t2 + 512],
                    in_=pb[bb][:]), waits=[mmq])
                bank_r[b] = tev
                qev[(qc, th)] = tev
            s_step(1)

        t_sq = op("vector", lambda e: e.tensor_scalar(
            out=SM[:, 80:112], in0=SM[:, 0:32], scalar1=1.0 / 4096.0,
            scalar2=1e-8, op0=ALU.mult, op1=ALU.max), waits=[s_red_tok[31]])
        t_sq2 = op("vector", lambda e: e.tensor_mul(
            out=SM[:, 80:112], in0=SM[:, 80:112], in1=CS[:, 0:32]),
            waits=[t_sq, t_qs])
        t_sk = op("vector", lambda e: e.tensor_scalar(
            out=SM[:, 112:120], in0=SM[:, 32:40], scalar1=1.0 / 4096.0,
            scalar2=1e-8, op0=ALU.mult, op1=ALU.max), waits=[s_red_tok[39]])
        t_sk2 = op("vector", lambda e: e.tensor_mul(
            out=SM[:, 112:120], in0=SM[:, 112:120], in1=CS[:, 32:40]),
            waits=[t_sk, t_ks])
        csc = None
        for h in range(NH):
            csc = op("vector", lambda e, hh=h: e.tensor_mul(
                out=CS[:, 48 + hh:49 + hh], in0=SM[:, 80 + hh:81 + hh],
                in1=SM[:, 112 + hh // 4:113 + hh // 4]), waits=[t_sq2, t_sk2])
        t_csc = op("vector", lambda e: e.tensor_scalar_mul(
            out=CS[:, 48:80], in0=CS[:, 48:80], scalar1=RSQ), waits=[csc])
        qscale = {}
        for h in range(NH):
            qscale[h] = op("vector", lambda e, hh=h: e.tensor_scalar_mul(
                out=BIG2[:, 1024 * hh:1024 * hh + 1024],
                in0=BIG2[:, 1024 * hh:1024 * hh + 1024],
                scalar1=CS[:, 48 + hh:49 + hh]),
                waits=[t_csc, qev[(h, 0)], qev[(h, 1)]])
        q_scaled = qscale[NH - 1]

        qm_state = {"h": 0, "mm": {}, "rm": {}}

        def qmax_steps(n):
            for _ in range(n):
                h = qm_state["h"]
                if h >= NH:
                    return
                qm_state["h"] = h + 1
                sl = h % 2
                sq = BIG3[:, 16384 + 1024 * sl:16384 + 1024 * sl + 1024]
                b = 4 + sl
                w = [qscale[h], qm_state["mm"].get(h - 2)]
                if h < 2:
                    w.append(s_red_tok[38])
                tq2 = op("vector", lambda e, hh=h, s=sq: e.tensor_mul(
                    out=s, in0=BIG2[:, 1024 * hh:1024 * hh + 1024],
                    in1=BIG2[:, 1024 * hh:1024 * hh + 1024]),
                    waits=[t for t in w if t])
                tmm = op("tensor", lambda e, s=sq, bb=b: e.matmul(
                    pb[bb][0:1, 0:512], ONE[:, 0:1], s[:, 0:512],
                    start=True, stop=False),
                    waits=[tq2, t_one, bank_r.get(b), bank_w.get(b)])
                tmm = op("tensor", lambda e, s=sq, bb=b: e.matmul(
                    pb[bb][0:1, 0:512], ONE[:, 0:1], s[:, 512:1024],
                    start=False, stop=True), waits=[])
                bank_w[b] = tmm
                qm_state["mm"][h] = tmm
                tmx = op("vector", lambda e, hh=h, bb=b: e.reduce_max(
                    out=CSR[0:1, hh:hh + 1], in_=pb[bb][0:1, 0:512],
                    axis=AX), waits=[tmm])
                bank_r[b] = tmx
                qm_state["rm"][h] = tmx

        def k_group(grp, lnx_tok, post_g=None):
            kev = None
            kev_toks = []
            for g in range(8):
                last_mm = None
                for j in range(4):
                    src = wk_t[g][:, 8 * j:8 * j + 8, :]
                    slot = (g * 4 + j) % 2
                    we = [lnx_tok] if g == 0 and j == 0 else []
                    tsg, sap = w_quarter(src, slot, waits_extra=we)
                    for k in range(8):
                        ic = 8 * j + k
                        for th in range(2):
                            b = 2 * (g % 2) + th
                            w = [tsg]
                            if ic == 0 and th == 0:
                                w += [bank_r.get(b), bank_r.get(b + 1),
                                      bank_w.get(b), bank_w.get(b + 1)]
                            last_mm = op("tensor", lambda e, s=sap, kk=k,
                                         icc=ic, t2=th, bb=b: e.matmul(
                                pb[bb][:], s[:, 128 * kk:128 * kk + 128],
                                BIG1[:, 1024 * icc + 512 * t2:
                                     1024 * icc + 512 * t2 + 512],
                                start=(icc == 0), stop=(icc == 31)),
                                waits=[t for t in w if t])
                    wq_state["sgn_cons"][slot] = last_mm
                for th in range(2):
                    b = 2 * (g % 2) + th
                    bank_w[b] = last_mm
                    col = 2048 * g + 1024 * grp + 512 * th
                    kev = op("scalar", lambda e, cc=col, bb=b: e.copy(
                        out=BIG3[:, cc:cc + 512], in_=pb[bb][:]),
                        waits=[last_mm] +
                              ([s_red_tok[31]] if grp == 0 and g == 0 else []))
                    bank_r[b] = kev
                kev_toks.append(kev)
                if grp == 0:
                    s_step(1)
                if post_g is not None:
                    post_g(g, kev)
            return kev, kev_toks

        k0_done, _ = k_group(0, lnx0, post_g=lambda g, kv: qmax_steps(4))
        qmax_steps(32)
        qm_done = qm_state["mm"][NH - 1]

        def v_group(grp, lnx_tok, ln1p_done=None, first_ev_w=()):
            vev = None
            mini_cons = {m: wq_state["cons"].get(m // 2) for m in range(4)}
            sgn_mini_cons = {m: wq_state["sgn_cons"].get(m // 2)
                             for m in range(4)}
            last_mm = None
            last_sign = None
            first = True
            for vc in range(2):
                for ic in range(32):
                    m = (vc * 32 + ic) % 4
                    rap = ringf[:, 512 * m:512 * m + 512]
                    sap = SGN[:, 512 * m:512 * m + 512]
                    w = [mini_cons.get(m)]
                    if first:
                        w.append(lnx_tok)
                    td = dma("sync", rap, wv_ic[ic][:, 512 * vc:512 * vc + 512],
                             waits=[t for t in w if t], sem="WM%d" % m)
                    tsg = op("scalar", lambda e, r=rap, s=sap: e.sign(
                        out=s, in_=r), waits=[td, sgn_mini_cons.get(m)])
                    mini_cons[m] = tsg
                    last_sign = tsg
                    for sl in range(8):
                        w2 = [tsg]
                        if ic == 0:
                            w2 += [bank_r.get(sl), bank_w.get(sl)]
                        last_mm = op("tensor", lambda e, s=sap, icc=ic,
                                     ss=sl: e.matmul(
                            pb[ss][:],
                            BIG1[:, 1024 * icc + 128 * ss:
                                 1024 * icc + 128 * ss + 128],
                            s[:], start=(icc == 0), stop=(icc == 31)),
                            waits=[t for t in w2 if t])
                    sgn_mini_cons[m] = last_mm
                    first = False
                    if grp == 0 and vc == 0 and ic % 2 == 0:
                        s_step(1)
                for sl in range(8):
                    bank_w[sl] = last_mm
                    col = 16384 + 1024 * (8 * grp + sl) + 512 * vc
                    we = [last_mm]
                    if vc == 0 and sl == 0:
                        we += list(first_ev_w)
                        if grp == 0:
                            we.append(s_red_tok[39])
                        if grp == 1 and ln1p_done:
                            we.append(ln1p_done)
                    vev = op("scalar", lambda e, cc=col, bb=sl: e.copy(
                        out=BIG3[:, cc:cc + 512], in_=pb[bb][:]), waits=we)
                    bank_r[sl] = vev
            wq_state["cons"][0] = last_sign
            wq_state["cons"][1] = last_sign
            wq_state["sgn_cons"][0] = last_mm
            wq_state["sgn_cons"][1] = last_mm
            return vev, last_mm, last_sign

        v0_done, v0_mm, v0_sg = v_group(0, lnx0, first_ev_w=[qm_done])

        stage_par = BIG3[:, 24576:32768].bitcast(F32)
        nb_par_ap = RING[:, 0:2048].bitcast(BF16)
        ev1, lnx1_n, ttr1, _ = ln1_group(
            1, stage_par, lambda tc: nb_par_ap, 1,
            extra_w=[v0_mm, v0_done, s_red_tok[62], s_red_tok[63]],
            nb_war0=v0_sg, junk_of=lambda tc: SGN[:, 0:2048])
        lnx1_tok = ev1[-1]

        km_state = {"toks": {}, "mm": {}}

        def kmax_emit(g, kev_tok):
            sl = g % 2
            sq = BIG3[:, 24576 + 2048 * sl:24576 + 2048 * sl + 2048]
            b = 4 + sl
            w = [kev_tok, km_state["mm"].get(g - 2)]
            tk2 = op("vector", lambda e, gg=g, s=sq: e.tensor_mul(
                out=s, in0=BIG3[:, 2048 * gg:2048 * gg + 2048],
                in1=BIG3[:, 2048 * gg:2048 * gg + 2048]),
                waits=[t for t in w if t])
            tmm = None
            for kk in range(4):
                tmm = op("tensor", lambda e, k4=kk, s=sq, bb=b: e.matmul(
                    pb[bb][0:1, 0:512], ONE[:, 0:1],
                    s[:, 512 * k4:512 * k4 + 512],
                    start=(k4 == 0), stop=(k4 == 3)),
                    waits=[tk2, t_one, bank_r.get(b), bank_w.get(b)]
                    if kk == 0 else [])
            bank_w[b] = tmm
            km_state["mm"][g] = tmm
            tmx = op("vector", lambda e, gg=g, bb=b: e.reduce_max(
                out=CSR[0:1, 32 + gg:33 + gg], in_=pb[bb][0:1, 0:512],
                axis=AX), waits=[tmm])
            bank_r[b] = tmx
            km_state["toks"][g] = tmx

        kp_toks = {}

        def kpost(g, kev_tok):
            kp_toks[g] = kev_tok
            if g >= 1:
                kmax_emit(g - 1, kp_toks[g - 1])

        k1_done, _ = k_group(1, lnx1_tok, post_g=kpost)
        kmax_emit(7, kp_toks[7])
        km_done = km_state["mm"][7]
        kmax_last = km_state["toks"][7]

        tc2 = None
        for h in range(NH):
            tc2 = op("vector", lambda e, hh=h: e.tensor_mul(
                out=CSR[0:1, 64 + hh:65 + hh],
                in0=CSR[0:1, 0 + hh:1 + hh],
                in1=CSR[0:1, 32 + hh // 4:33 + hh // 4]),
                waits=[qm_state["rm"][h], km_state["toks"][h // 4]])
        tlnb = op("scalar", lambda e: e.activation(
            out=CSR[0:1, 96:128], in_=CSR[0:1, 64:96], func=AF.Ln,
            scale=1.0, bias=CB[0:1, 32:33]), waits=[tc2, t_eps])
        texpb = op("scalar", lambda e: e.activation(
            out=CSR[0:1, 128:160], in_=CSR[0:1, 96:128], func=AF.Exp,
            scale=0.5), waits=[tlnb])
        tnegb = op("vector", lambda e: e.tensor_scalar_mul(
            out=CSR[0:1, 160:192], in0=CSR[0:1, 128:160], scalar1=-1.0),
            waits=[texpb])
        t_cb0 = dmams(d_brd[0:1, 0:32], CSR[0:1, 160:192], waits=[tnegb])
        t_cb = dmams(CB[:, 0:32], d_brd[0:1, 0:32].broadcast_to([128, 32]),
                     waits=[t_cb0])

        v1_done, kv_mm, v1_sg = v_group(1, lnx1_tok, ln1p_done=lnx1_n,
                                        first_ev_w=[km_done])
        kv_done = v1_done

        t_sv = op("vector", lambda e: e.tensor_scalar(
            out=SM[:, 120:128], in0=SM[:, 40:48], scalar1=1.0 / 4096.0,
            scalar2=1e-8, op0=ALU.mult, op1=ALU.max), waits=[s_red_tok[47]])
        t_vsv = op("vector", lambda e: e.tensor_mul(
            out=SM[:, 120:128], in0=SM[:, 120:128], in1=CS[:, 40:48]),
            waits=[t_sv, t_vs])

        s_state["era_wait"][64] = kv_mm

        crow = BIG1[:, 16384:18432].bitcast(F32)
        pr_cols = [8192 + 512 * i for i in range(8)]
        pring = {}
        s_cons = {}
        pair_ev = {4: bank_r.get(4), 5: bank_r.get(5),
                   6: bank_r.get(6), 7: bank_r.get(7)}
        ring_i = [0]
        last_evA = None
        first_w = [t_cb, kv_done, kv_mm, q_scaled,
                   bank_r.get(2), bank_r.get(3)]
        blk_i = [0]

        for g in range(NKV):
            for tcol in range(2):
                for pair in range(2):
                    h0 = 4 * g + 2 * pair
                    evw = [pair_ev[4], pair_ev[5], pair_ev[6], pair_ev[7]]
                    exps = {}
                    avs = {}
                    rss = {}

                    def issue_qk(kc):
                        for i, h in enumerate((h0, h0 + 1)):
                            b = 2 * (kc % 2) + i
                            wts = [s_cons.get(b)]
                            if g == 0 and tcol == 0 and pair == 0 and kc < 2:
                                wts = wts + first_w + [bank_r.get(b),
                                                       bank_w.get(b)]
                            tmm = op("tensor", lambda e, gg=g, k=kc, hh=h,
                                     bb=b, t2=tcol: e.matmul(
                                pb[bb][:],
                                BIG3[:, 2048 * gg + 128 * k:
                                     2048 * gg + 128 * k + 128],
                                BIG2[:, 1024 * hh + 512 * t2:
                                     1024 * hh + 512 * t2 + 512],
                                start=True, stop=True),
                                waits=[t for t in wts if t])
                            bank_w[b] = tmm
                            slot = ring_i[0] % 8
                            ring_i[0] += 1
                            tex = op("scalar", lambda e, bb=b, hh=h, sl=slot:
                                     e.activation(
                                out=BIG1[:, pr_cols[sl]:pr_cols[sl] + 512],
                                in_=pb[bb][:], func=AF.Exp,
                                bias=CB[:, hh:hh + 1]),
                                waits=[tmm, t_cb, pring.get(slot), kv_mm])
                            s_cons[b] = tex
                            exps[(kc, i)] = (tex, slot)

                    def issue_avrs(kc):
                        for i in range(2):
                            tex, slot = exps[(kc, i)]
                            wts = [tex]
                            if kc == 0:
                                wts += evw
                            tav = op("tensor", lambda e, gg=g, k=kc, sl=slot,
                                     ab=4 + i: e.matmul(
                                pb[ab][:],
                                BIG3[:, 16384 + 1024 * k + 128 * gg:
                                     16384 + 1024 * k + 128 * gg + 128],
                                BIG1[:, pr_cols[sl]:pr_cols[sl] + 512],
                                start=(k == 0), stop=(k == 15)), waits=wts)
                            trs = op("tensor", lambda e, k=kc, sl=slot,
                                     rb=6 + i: e.matmul(
                                pb[rb][:], ONE[:],
                                BIG1[:, pr_cols[sl]:pr_cols[sl] + 512],
                                start=(k == 0), stop=(k == 15)),
                                waits=([t_one] + evw) if kc == 0 else [])
                            pring[slot] = trs
                            bank_w[4 + i] = tav
                            bank_w[6 + i] = trs
                            avs[i] = tav
                            rss[i] = trs

                    issue_qk(0)
                    for kc in range(1, 16):
                        issue_qk(kc)
                        issue_avrs(kc - 1)
                    issue_avrs(15)

                    for i in range(2):
                        h = h0 + i
                        trr = op("vector", lambda e, ii=i: e.reciprocal(
                            out=crow[:, 512 * ii:512 * ii + 512],
                            in_=pb[6 + ii][:]),
                            waits=[rss[i], pair_ev[4 + i]])
                        tevA = op("vector", lambda e, ii=i, hh=h, t2=tcol:
                                  e.tensor_tensor(
                            out=BIG2[:, 1024 * hh + 512 * t2:
                                     1024 * hh + 512 * t2 + 512],
                            in0=pb[4 + ii][:],
                            in1=crow[:, 512 * ii:512 * ii + 512],
                            op=ALU.mult), waits=[avs[i], trr])
                        pair_ev[4 + i] = tevA
                        pair_ev[6 + i] = tevA
                        bank_r[4 + i] = tevA
                        bank_r[6 + i] = tevA
                        last_evA = tevA
                    blk_i[0] += 1
                    if blk_i[0] <= 16:
                        s_step(1)
        attn_done = last_evA

        OST = BIG3[:, 0:4096]
        zone = BIG3[:, 4096:4224]
        evd = BIG3[:, 4224:4352]
        orow_b = BIG3[0:1, 4608:8704]
        osr_b = BIG3[0:1, 8704:12800]
        t_so = op("vector", lambda e: e.tensor_scalar(
            out=SM[:, 128:160], in0=SM[:, 48:80], scalar1=1.0 / 4096.0,
            scalar2=1e-8, op0=ALU.mult, op1=ALU.max),
            waits=[s_red_tok[79], attn_done])
        dp_z = op("vector", lambda e: e.memset(zone, 0.0), waits=[attn_done])
        dp_cp = op("vector", lambda e: e.tensor_copy(
            out=zone[:, 0:32], in_=SM[:, 128:160]), waits=[dp_z, t_so])
        dp_tr = op("tensor", lambda e: e.transpose(
            pbt6[:, 0:128], zone, IDN[:]),
            waits=[dp_cp, t_idn, bank_r.get(6), bank_w.get(6)])
        bank_w[6] = dp_tr
        dp_ev = op("scalar", lambda e: e.copy(out=evd, in_=pbt6[:, 0:128]),
                   waits=[dp_tr])
        bank_r[6] = dp_ev
        dp_row = dmams(orow_b, BIG3[0:32, 4224:4352], waits=[dp_ev])
        dp_os = dma("gpsimd", osr_b, os_row, waits=[attn_done], sem="GA")
        dp_mul = op("vector", lambda e: e.tensor_mul(
            out=orow_b, in0=orow_b, in1=osr_b), waits=[dp_row, dp_os])
        t_ost0 = dmams(d_brdb[0:1, 0:4096], orow_b, waits=[dp_mul])
        t_ost = dmams(OST, d_brdb[0:1, 0:4096].broadcast_to([128, 4096]),
                      waits=[t_ost0])

        o_ring_cons = {m: wq_state["cons"].get(m // 2) for m in range(4)}
        o_sgn_cons = {m: wq_state["sgn_cons"].get(m // 2) for m in range(4)}
        mm_prev_ocol = None
        oev = {}
        for ocol in range(8):
            mm_last = None
            for ic in range(32):
                m = (ocol * 32 + ic) % 4
                rap = ringf[:, 512 * m:512 * m + 512]
                sap = SGN[:, 512 * m:512 * m + 512]
                bap = ringb[:, 1024 * m:1024 * m + 512]
                w = [o_ring_cons.get(m)]
                if ocol == 0 and ic < 4:
                    w.append(attn_done)
                td = dma("sync", rap, wo_t[ocol, ic],
                         waits=[t for t in w if t], sem="WM%d" % m)
                tsg = op("scalar", lambda e, r=rap, s=sap: e.sign(
                    out=s, in_=r),
                    waits=[td, o_sgn_cons.get(m)] +
                          ([attn_done] if ocol == 0 and ic < 4 else []))
                tst = op("vector", lambda e, s=sap, b2=bap, icc=ic, oc=ocol:
                         e.scalar_tensor_tensor(
                    out=b2, in0=s,
                    scalar=SM[:, 120 + icc // 4:121 + icc // 4],
                    in1=OST[:, 512 * oc:512 * oc + 512],
                    op0=ALU.mult, op1=ALU.mult),
                    waits=[tsg, t_ost, t_vsv])
                o_sgn_cons[m] = tst
                for t8 in range(8):
                    w2 = [tst]
                    if ic == 0:
                        w2 += [bank_r.get(t8), bank_w.get(t8)]
                        if ocol > 0:
                            w2.append(mm_prev_ocol)
                    mm_last = op("tensor", lambda e, b2=bap, icc=ic, tt=t8:
                                 e.matmul(
                        pb[tt][:],
                        BIG2[:, 1024 * icc + 128 * tt:
                             1024 * icc + 128 * tt + 128],
                        b2[:], start=(icc == 0), stop=(icc == 31)),
                        waits=w2)
                o_ring_cons[m] = mm_last
            mm_prev_ocol = mm_last
            for t8 in range(8):
                bank_w[t8] = mm_last
                we = [mm_last]
                if ocol == 0 and t8 == 0:
                    we += [s_red_tok[78], s_red_tok[79]]
                tev = op("scalar", lambda e, tt=t8, oc=ocol: e.copy(
                    out=BIG1[:, 4096 * tt + 512 * oc:4096 * tt + 512 * oc + 512],
                    in_=pb[tt][:]), waits=we)
                bank_r[t8] = tev
                oev[(ocol, t8)] = tev
        o_done = oev[(7, 7)]
        o_mm_last = mm_prev_ocol

        xe1 = [BIG2[:, 0:8192].bitcast(F32), BIG2[:, 8192:16384].bitcast(F32)]
        e2buf = BIG2[:, 16384:24576].bitcast(F32)
        ejunk = BIG2[:, 24576:28672]
        lnx1_tm = BIG2[:, 28672:32768]
        et = [{} for _ in range(8)]
        spills = [None] * 8

        def e_head(tk):
            d = et[tk]
            c = 192 + 16 * (tk % 2)
            d["c"] = c
            orow = BIG1[:, 4096 * tk:4096 * tk + 4096]
            d["o"] = orow
            ordy = oev[(7, tk)]
            if tk < 2:
                wx = [o_mm_last]
            else:
                pv = et[tk - 2]
                wx = [spills[tk - 2], pv["H"], pv["J"]]
            d["A"] = dma("scalar", xe1[tk % 2],
                         x[128 * tk:128 * tk + 128, :],
                         waits=[t for t in wx if t], sem="XE")
            d["B"] = op("vector", lambda e, o=orow, cc=c: e.reduce_sum(
                out=SM[:, cc:cc + 1], in_=o, axis=AX), waits=[ordy])
            d["C"] = op("scalar", lambda e, o=orow, cc=c: e.activation(
                out=ejunk, in_=o, func=AF.Square,
                accum_out=SM[:, cc + 1:cc + 2]), waits=[ordy])
            tneg = op("vector", lambda e, cc=c: e.tensor_scalar_mul(
                out=SM[:, cc + 2:cc + 3], in0=SM[:, cc:cc + 1],
                scalar1=-1.0 / 4096.0), waits=[d["B"]])
            tmusq = op("vector", lambda e, cc=c: e.tensor_scalar(
                out=SM[:, cc + 3:cc + 4], in0=SM[:, cc + 2:cc + 3],
                scalar1=SM[:, cc + 2:cc + 3], scalar2=-1.0,
                op0=ALU.mult, op1=ALU.mult), waits=[tneg])
            tbias = op("vector", lambda e, cc=c: e.tensor_scalar_add(
                out=SM[:, cc + 4:cc + 5], in0=SM[:, cc + 3:cc + 4],
                scalar1=EPS), waits=[tmusq])
            tcomb = op("vector", lambda e, cc=c: e.tensor_scalar_mul(
                out=SM[:, cc + 5:cc + 6], in0=SM[:, cc + 1:cc + 2],
                scalar1=1.0 / 4096.0), waits=[d["C"]])
            tln = op("scalar", lambda e, cc=c: e.activation(
                out=SM[:, cc + 6:cc + 7], in_=SM[:, cc + 5:cc + 6],
                func=AF.Ln, bias=SM[:, cc + 4:cc + 5]),
                waits=[tcomb, tbias])
            trstd = op("scalar", lambda e, cc=c: e.activation(
                out=SM[:, cc + 7:cc + 8], in_=SM[:, cc + 6:cc + 7],
                func=AF.Exp, scale=-0.5), waits=[tln])
            d["nb"] = op("vector", lambda e, cc=c: e.tensor_mul(
                out=SM[:, cc + 8:cc + 9], in0=SM[:, cc + 2:cc + 3],
                in1=SM[:, cc + 7:cc + 8]), waits=[trstd])
            d["rstd"] = trstd

        def e_tail(tk):
            d = et[tk]
            c = d["c"]
            xb = xe1[tk % 2]
            tE = op("scalar", lambda e, o=d["o"], cc=c: e.activation(
                out=e2buf, in_=o, func=AF.Identity,
                scale=SM[:, cc + 7:cc + 8], bias=SM[:, cc + 8:cc + 9],
                accum_out=SM[:, cc + 9:cc + 10]),
                waits=[d["nb"]] +
                      ([et[tk - 1]["add"]] if tk > 0 else []))
            d["e2"] = tE
            tF = op("vector", lambda e, xbb=xb: e.tensor_tensor(
                out=xbb, in0=xbb, in1=e2buf, op=ALU.add),
                waits=[tE, d["A"]])
            d["add"] = tF
            spills[tk] = dma("scalar", x1_spill[128 * tk:128 * tk + 128, :],
                             xb, waits=[tF], sem="SP")
            tS1 = op("vector", lambda e, cc=c, tkk=tk: e.tensor_tensor(
                out=SM[:, cc + 9:cc + 10], in0=SM[:, cc + 9:cc + 10],
                in1=CSR[:, 48 + tkk:49 + tkk], op=ALU.add),
                waits=[tE, stash0[tk]])
            d["H"] = op("scalar", lambda e, cc=c, xbb=xb: e.activation(
                out=ejunk, in_=xbb, func=AF.Square,
                accum_out=SM[:, cc + 10:cc + 11]), waits=[tF])
            tneg3 = op("vector", lambda e, cc=c: e.tensor_scalar_mul(
                out=SM[:, cc + 11:cc + 12], in0=SM[:, cc + 9:cc + 10],
                scalar1=-1.0 / 4096.0), waits=[tS1])
            tmusq3 = op("vector", lambda e, cc=c: e.tensor_scalar(
                out=SM[:, cc + 12:cc + 13], in0=SM[:, cc + 11:cc + 12],
                scalar1=SM[:, cc + 11:cc + 12], scalar2=-1.0,
                op0=ALU.mult, op1=ALU.mult), waits=[tneg3])
            tbias3 = op("vector", lambda e, cc=c: e.tensor_scalar_add(
                out=SM[:, cc + 13:cc + 14], in0=SM[:, cc + 12:cc + 13],
                scalar1=EPS), waits=[tmusq3])
            tcomb3 = op("vector", lambda e, cc=c: e.tensor_scalar_mul(
                out=SM[:, cc + 14:cc + 15], in0=SM[:, cc + 10:cc + 11],
                scalar1=1.0 / 4096.0), waits=[d["H"]])
            tln3 = op("scalar", lambda e, cc=c: e.activation(
                out=SM[:, cc + 15:cc + 16], in_=SM[:, cc + 14:cc + 15],
                func=AF.Ln, bias=SM[:, cc + 13:cc + 14]),
                waits=[tcomb3, tbias3])
            trstd3 = op("scalar", lambda e, cc=c: e.activation(
                out=SM[:, cc + 14:cc + 15], in_=SM[:, cc + 15:cc + 16],
                func=AF.Exp, scale=-0.5), waits=[tln3])
            tJ = op("vector", lambda e, cc=c, xbb=xb: e.tensor_scalar(
                out=lnx1_tm, in0=xbb, scalar1=SM[:, cc + 11:cc + 12],
                scalar2=SM[:, cc + 14:cc + 15], op0=ALU.add, op1=ALU.mult),
                waits=[trstd3] +
                      ([et[tk - 1]["ttr"]] if tk > 0 else []))
            d["J"] = tJ
            ttr_last = None
            ev = None
            for g4 in range(4):
                pt, bidx = (pbt6, 6) if g4 % 2 == 0 else (pbt7, 7)
                ttr = None
                for k in range(8):
                    ic = g4 * 8 + k
                    ttr = op("tensor", lambda e, icc=ic, ptt=pt, kk=k:
                             e.transpose(ptt[:, 128 * kk:128 * kk + 128],
                                         lnx1_tm[:, 128 * icc:128 * icc + 128],
                                         IDN[:]),
                             waits=[tJ, t_idn, bank_r.get(bidx),
                                    bank_w.get(bidx)])
                bank_w[bidx] = ttr
                ttr_last = ttr
                ev = op("scalar", lambda e, ptt=pt, g=g4, t8=tk: e.copy(
                    out=BIG3.rearrange("p (ic t) -> p ic t", t=1024)[
                        :, 8 * g:8 * g + 8, 128 * t8:128 * t8 + 128],
                    in_=ptt[:, 0:1024].rearrange(
                        "p (k t) -> p k t", t=128)), waits=[ttr])
                bank_r[bidx] = ev
            d["ttr"] = ttr_last
            d["ev"] = ev

        e_head(0)
        for tk in range(8):
            if tk < 7:
                e_head(tk + 1)
            e_tail(tk)
        lnx1_done = et[7]["ev"]
        spill_last = spills[-1]

        bias_bf = BIG1[0:1, 18432:22528]
        t_bias = dma("gpsimd", bias_bf, bias_row,
                     waits=[et[4]["e2"], et[5]["e2"], et[4]["B"], et[5]["B"]],
                     sem="GC")
        t_cs = dmams(SM[0:16, 160:161], cs_col)
        t_ss = dmams(SM[0:16, 161:162], ss_col)
        t_rr = dmams(RZ[0:16, 0:128], repR)
        t_sc = dmams(RZ[0:16, 128:130], selC)
        t_w16 = op("vector", lambda e: e.tensor_mul(
            out=SM[0:16, 162:163], in0=SM[0:16, 160:161],
            in1=SM[0:16, 161:162]), waits=[t_cs, t_ss])
        t_wl = op("vector", lambda e: e.tensor_scalar_mul(
            out=RZ[0:16, 130:258], in0=RZ[0:16, 0:128],
            scalar1=SM[0:16, 162:163]), waits=[t_w16, t_rr])
        t_wmm = op("tensor", lambda e: e.matmul(
            pb[4][:, 0:2], RZ[0:16, 130:258], RZ[0:16, 128:130],
            start=True, stop=True),
            waits=[t_wl, t_sc, bank_r.get(4), bank_w.get(4)])
        bank_w[4] = t_wmm
        t_wcm = op("scalar", lambda e: e.copy(out=WCM[:], in_=pb[4][:, 0:2]),
                   waits=[t_wmm])
        bank_r[4] = t_wcm
        t_asb = dma("gpsimd", BIG1[:, 0:8192],
                    ap_t.rearrange("p ic r -> p (ic r)"),
                    waits=[et[0]["e2"], et[1]["e2"], et[0]["B"], et[1]["B"]],
                    sem="GA")
        t_bsb = dma("gpsimd",
                    BIG1[:, 8192:16384].rearrange("p (rc o) -> p rc o", rc=2),
                    b_t.rearrange("(rc p) o -> p rc o", p=128),
                    waits=[et[2]["e2"], et[3]["e2"], et[2]["B"], et[3]["B"]],
                    sem="GB")
        t_bw = op("vector", lambda e: e.tensor_scalar_mul(
            out=BIG1[:, 8192:12288], in0=BIG1[:, 8192:12288],
            scalar1=WCM[:, 0:1]), waits=[t_bsb, t_wcm])
        t_bw2 = op("vector", lambda e: e.tensor_scalar_mul(
            out=BIG1[:, 12288:16384], in0=BIG1[:, 12288:16384],
            scalar1=WCM[:, 1:2]), waits=[t_bsb, t_wcm])
        xa_ev = []
        for rc in range(2):
            for tcol in range(2):
                b = 2 * rc + tcol
                tmm = None
                for ic in range(32):
                    tmm = op("tensor", lambda e, icc=ic, r=rc, t2=tcol, bb=b:
                             e.matmul(
                        pb[bb][:],
                        BIG1[:, 256 * icc + 128 * r:256 * icc + 128 * r + 128],
                        BIG3[:, 1024 * icc + 512 * t2:
                             1024 * icc + 512 * t2 + 512],
                        start=(icc == 0), stop=(icc == 31)),
                        waits=[t_asb, lnx1_done, bank_r.get(b), bank_w.get(b)]
                        if ic == 0 else [])
                bank_w[b] = tmm
                tev = op("scalar", lambda e, r=rc, t2=tcol, bb=b: e.copy(
                    out=BIG1[:, 16384 + 1024 * r + 512 * t2:
                             16384 + 1024 * r + 512 * t2 + 512],
                    in_=pb[bb][:]), waits=[tmm, lnx1_done])
                bank_r[b] = tev
                xa_ev.append(tev)
        xa_done = xa_ev[-1]

        f_mini_cons = {m: o_sgn_cons.get(m) for m in range(4)}
        mm_last = None
        yev = {}
        for ocol in range(8):
            for ic in range(32):
                m = (ocol * 32 + ic) % 4
                sap = SGN[:, 512 * m:512 * m + 512]
                w = [f_mini_cons.get(m)]
                if ocol == 0 and ic < 4:
                    w += [xa_done]
                tdw = dma("gpsimd", sap, wf_t[ocol, ic],
                          waits=[t for t in w if t], sem="WM%d" % m)
                for t8 in range(8):
                    w2 = [tdw, lnx1_done]
                    if ic == 0:
                        w2 += [bank_r.get(t8), bank_w.get(t8)]
                    mm_last = op("tensor", lambda e, s=sap, icc=ic, tt=t8:
                                 e.matmul(
                        pb[tt][:],
                        BIG3[:, 1024 * icc + 128 * tt:
                             1024 * icc + 128 * tt + 128],
                        s[:], start=(icc == 0), stop=False), waits=w2)
                f_mini_cons[m] = mm_last
            for t8 in range(8):
                for rc in range(2):
                    mm_last = op("tensor", lambda e, tt=t8, r=rc, oc=ocol:
                                 e.matmul(
                        pb[tt][:],
                        BIG1[:, 16384 + 1024 * r + 128 * tt:
                             16384 + 1024 * r + 128 * tt + 128],
                        BIG1[:, 8192 + 4096 * r + 512 * oc:
                             8192 + 4096 * r + 512 * oc + 512],
                        start=False, stop=False),
                        waits=[t_bw, t_bw2, xa_done]
                        if (ocol == 0 and t8 == 0 and rc == 0) else [])
                mm_last = op("tensor", lambda e, tt=t8, oc=ocol: e.matmul(
                    pb[tt][:], ONE[0:1, :],
                    bias_bf[:, 512 * oc:512 * oc + 512],
                    start=False, stop=True),
                    waits=[t_bias, t_one]
                    if (ocol == 0 and t8 == 0) else [])
                bank_w[t8] = mm_last
                tev = op("scalar", lambda e, tt=t8, oc=ocol: e.copy(
                    out=BIG2[:, 4096 * tt + 512 * oc:4096 * tt + 512 * oc + 512],
                    in_=pb[tt][:]),
                    waits=[mm_last, spill_last] +
                          ([lnx1_done] if ocol == 0 else []))
                bank_r[t8] = tev
                yev[(ocol, t8)] = tev
        fmm = mm_last

        f1 = [BIG3[:, 0:8192].bitcast(F32), BIG3[:, 8192:16384].bitcast(F32)]
        xst = [BIG3[:, 16384:24576].bitcast(F32),
               BIG3[:, 24576:32768].bitcast(F32)]
        fjunk = BIG1[:, 0:4096]
        f2b = [BIG1[:, 4096:12288].bitcast(F32),
               BIG1[:, 12288:20480].bitcast(F32)]
        ft = [{} for _ in range(8)]
        out_toks = [None] * 8

        def f_head(tk):
            d = ft[tk]
            c = 224 + 16 * (tk % 2)
            d["c"] = c
            yrow = BIG2[:, 4096 * tk:4096 * tk + 4096]
            d["y"] = yrow
            yrdy = yev[(7, tk)]
            wx = [spills[tk]]
            if tk < 2:
                wx.append(fmm)
            else:
                wx.append(ft[tk - 2]["aff"])
            d["A"] = dma("scalar", xst[tk % 2],
                         x1_spill[128 * tk:128 * tk + 128, :],
                         waits=[t for t in wx if t], sem="XF")
            d["B"] = op("vector", lambda e, o=yrow, cc=c: e.reduce_sum(
                out=SM[:, cc:cc + 1], in_=o, axis=AX),
                waits=[yrdy] + ([fmm] if tk == 0 else []))
            d["C"] = op("scalar", lambda e, o=yrow, cc=c: e.activation(
                out=fjunk, in_=o, func=AF.Square,
                accum_out=SM[:, cc + 1:cc + 2]),
                waits=[yrdy] + ([fmm] if tk == 0 else []))
            tneg = op("vector", lambda e, cc=c: e.tensor_scalar_mul(
                out=SM[:, cc + 2:cc + 3], in0=SM[:, cc:cc + 1],
                scalar1=-1.0 / 4096.0), waits=[d["B"]])
            tmusq = op("vector", lambda e, cc=c: e.tensor_scalar(
                out=SM[:, cc + 3:cc + 4], in0=SM[:, cc + 2:cc + 3],
                scalar1=SM[:, cc + 2:cc + 3], scalar2=-1.0,
                op0=ALU.mult, op1=ALU.mult), waits=[tneg])
            tbias = op("vector", lambda e, cc=c: e.tensor_scalar_add(
                out=SM[:, cc + 4:cc + 5], in0=SM[:, cc + 3:cc + 4],
                scalar1=EPS), waits=[tmusq])
            tcomb = op("vector", lambda e, cc=c: e.tensor_scalar_mul(
                out=SM[:, cc + 5:cc + 6], in0=SM[:, cc + 1:cc + 2],
                scalar1=1.0 / 4096.0), waits=[d["C"]])
            tln = op("scalar", lambda e, cc=c: e.activation(
                out=SM[:, cc + 6:cc + 7], in_=SM[:, cc + 5:cc + 6],
                func=AF.Ln, bias=SM[:, cc + 4:cc + 5]),
                waits=[tcomb, tbias])
            trstd = op("scalar", lambda e, cc=c: e.activation(
                out=SM[:, cc + 7:cc + 8], in_=SM[:, cc + 6:cc + 7],
                func=AF.Exp, scale=-0.5), waits=[tln])
            d["nb"] = op("vector", lambda e, cc=c: e.tensor_mul(
                out=SM[:, cc + 8:cc + 9], in0=SM[:, cc + 2:cc + 3],
                in1=SM[:, cc + 7:cc + 8]), waits=[trstd])
            d["rstd"] = trstd

        def f_tail(tk):
            d = ft[tk]
            c = d["c"]
            tf2 = op("scalar", lambda e, cc=c, o=d["y"], fo=f2b[tk % 2]:
                     e.activation(
                out=fo, in_=o, func=AF.Identity,
                scale=SM[:, cc + 7:cc + 8], bias=SM[:, cc + 8:cc + 9]),
                waits=[d["nb"]] +
                      ([ft[tk - 2]["aff"]] if tk > 1 else [fmm]))
            wf1 = [tf2, d["A"]]
            if tk < 2:
                wf1.append(fmm)
            else:
                wf1.append(out_toks[tk - 2])
            d["aff"] = op("vector", lambda e, xs=xst[tk % 2],
                          fb=f1[tk % 2], fo=f2b[tk % 2]: e.tensor_tensor(
                out=fb, in0=xs, in1=fo, op=ALU.add),
                waits=[t for t in wf1 if t])
            out_toks[tk] = dma("sync", out[128 * tk:128 * tk + 128, :],
                               f1[tk % 2], waits=[d["aff"]], sem="OU")

        f_head(0)
        for tk in range(8):
            if tk < 7:
                f_head(tk + 1)
            f_tail(tk)

        p.op("sync", lambda e: None, waits=[out_toks[-1]])

        def emit(eng_name):
            def body(eng):
                for (waits, build, inc, inc_by) in p.q[eng_name]:
                    for (s, v) in waits:
                        eng.wait_ge(sems[s], v)
                    inst = build(eng)
                    if inc is not None:
                        inst.then_inc(sems[inc], inc_by)
            return body

        blk.sync(emit("sync"))
        blk.gpsimd(emit("gpsimd"))
        blk.vector(emit("vector"))
        blk.scalar(emit("scalar"))
        blk.tensor(emit("tensor"))

    return nc


def _layout_inputs(inputs):
    hs = np.ascontiguousarray(inputs["L0_attn_hidden_states"], dtype=np.float32)
    Wq = np.ascontiguousarray(inputs["L0_attn_q_weight"], dtype=np.float32)
    Wk = np.ascontiguousarray(inputs["L0_attn_k_weight"], dtype=np.float32)
    Wv = np.ascontiguousarray(inputs["L0_attn_v_weight"], dtype=np.float32)
    Wo = np.ascontiguousarray(inputs["L0_attn_o_weight"], dtype=np.float32)
    Wf = np.ascontiguousarray(inputs["L0_ffn_W"], dtype=np.float32)
    A = np.ascontiguousarray(inputs["L0_ffn_A"], dtype=np.float32)
    Bf = np.ascontiguousarray(inputs["L0_ffn_Bf"], dtype=np.float32)

    shared = dict(
        wq_t=np.ascontiguousarray(
            Wq.reshape(32, 128, 32, 128).transpose(0, 3, 2, 1)),
        wk_t=np.ascontiguousarray(
            Wk.reshape(8, 128, 32, 128).transpose(0, 3, 2, 1)),
        wv_ic=np.ascontiguousarray(
            Wv.reshape(1024, 32, 128).transpose(1, 2, 0)),
        wo_t=np.ascontiguousarray(
            Wo.reshape(8, 512, 32, 128).transpose(0, 2, 3, 1)),
        wf_t=np.ascontiguousarray(
            Wf.reshape(8, 512, 32, 128).transpose(0, 2, 3, 1)),
        wq_n=Wq, wk_n=Wk, wv_n=Wv, wo_n=Wo,
        qs_cm=np.ascontiguousarray(
            inputs["L0_attn_q_scale"].reshape(32, 128).T.astype(np.float32)),
        ks_cm=np.ascontiguousarray(
            inputs["L0_attn_k_scale"].reshape(8, 128).T.astype(np.float32)),
        vs_cm=np.ascontiguousarray(
            inputs["L0_attn_v_scale"].reshape(8, 128).T.astype(np.float32)),
        os_row=np.ascontiguousarray(
            inputs["L0_attn_o_scale"].reshape(1, 4096).astype(np.float32)),
        bias_row=np.ascontiguousarray(
            inputs["L0_ffn_bias"].reshape(1, 4096).astype(np.float32)),
        ap_t=np.ascontiguousarray(
            A.reshape(256, 32, 128).transpose(2, 1, 0)),
        b_t=np.ascontiguousarray(Bf.transpose(0, 2, 1).reshape(256, 4096)),
        cs_col=np.ascontiguousarray(
            inputs["L0_ffn_coeffs"].reshape(16, 1).astype(np.float32)),
        ss_col=np.ascontiguousarray(
            inputs["L0_ffn_scales"].reshape(16, 1).astype(np.float32)),
        repR=np.ascontiguousarray(
            (np.arange(128)[None, :] // 16 ==
             (np.arange(16) % 8)[:, None]).astype(BF)),
        selC=np.ascontiguousarray(
            (np.arange(2)[None, :] ==
             (np.arange(16) // 8)[:, None]).astype(BF)),
        ones_bf=np.ones((128, 128), dtype=BF),
        ident_bf=np.eye(128, dtype=BF),
    )
    in_maps = []
    for c in range(8):
        b, oh = c // 2, c % 2
        po = 1 - oh
        xc = np.concatenate([hs[b, 1024 * oh:1024 * oh + 1024],
                             hs[b, 1024 * po:1024 * po + 1024]], axis=0)
        m = dict(shared)
        m["x"] = np.ascontiguousarray(xc)
        in_maps.append(m)
    return in_maps


def kernel(**inputs):
    in_maps = _layout_inputs(inputs)
    nc = build_program()
    res = run_bass_kernel_spmd(nc, in_maps, core_ids=list(range(8)))
    o = np.empty((4, 2048, 4096), dtype=np.float32)
    for c in range(8):
        b, oh = c // 2, c % 2
        o[b, 1024 * oh:1024 * oh + 1024] = res.results[c]["out"]
    return o


# version 07 (final)
# speedup vs baseline: 1.5871x; 1.0107x over previous
import numpy as np
import ml_dtypes
from contextlib import ExitStack

import concourse.bass as bass
import concourse.mybir as mybir
from concourse.bass_utils import run_bass_kernel_spmd

F32 = mybir.dt.float32
BF16 = mybir.dt.bfloat16
AX = mybir.AxisListType.X
AF = mybir.ActivationFunctionType
ALU = mybir.AluOpType
BF = ml_dtypes.bfloat16

NH = 32
NKV = 8
EPS = 1e-5
RSQ = 1.0 / np.sqrt(128.0)

ENGINES = ("sync", "gpsimd", "vector", "scalar", "tensor")


class Sched:
    def __init__(self):
        self.q = {e: [] for e in ENGINES}
        self.count = {}
        self.last_wait = {e: {} for e in ENGINES}

    def tok(self, sem):
        return (sem, self.count.get(sem, 0))

    def op(self, eng, build, waits=(), inc=None, inc_by=1):
        w = []
        for t in waits:
            if t is None:
                continue
            s, v = t
            if v > self.last_wait[eng].get(s, 0):
                w.append((s, v))
                self.last_wait[eng][s] = v
        tok = None
        if inc is not None:
            self.count[inc] = self.count.get(inc, 0) + inc_by
            tok = (inc, self.count[inc])
        self.q[eng].append((w, build, inc, inc_by))
        return tok


def build_program():
    nc = bass.Bass("TRN2", target_bir_lowering=False, debug=False)

    dI = lambda n, s, dt=F32: nc.dram_tensor(n, list(s), dt, kind="ExternalInput").ap()
    x = dI("x", (2048, 4096))
    wq_t = dI("wq_t", (32, 128, 32, 128))
    wk_t = dI("wk_t", (8, 128, 32, 128))
    wv_ic = dI("wv_ic", (32, 128, 1024))
    wo_t = dI("wo_t", (8, 32, 128, 512))
    wf_t = dI("wf_t", (8, 32, 128, 512))
    wq_n = dI("wq_n", (4096, 4096))
    wk_n = dI("wk_n", (1024, 4096))
    wv_n = dI("wv_n", (1024, 4096))
    wo_n = dI("wo_n", (4096, 4096))
    qs_cm = dI("qs_cm", (128, 32))
    ks_cm = dI("ks_cm", (128, 8))
    vs_cm = dI("vs_cm", (128, 8))
    os_row = dI("os_row", (1, 4096))
    bias_row = dI("bias_row", (1, 4096))
    ap_t = dI("ap_t", (128, 32, 256))
    b_t = dI("b_t", (256, 4096))
    cs_col = dI("cs_col", (16, 1))
    ss_col = dI("ss_col", (16, 1))
    repR = dI("repR", (16, 128), BF16)
    selC = dI("selC", (16, 2), BF16)
    ones_bf = dI("ones_bf", (128, 128), BF16)
    ident_bf = dI("ident_bf", (128, 128), BF16)

    out = nc.dram_tensor("out", [1024, 4096], F32, kind="ExternalOutput").ap()
    x1_spill = nc.dram_tensor("x1_spill", [1024, 4096], F32, kind="Internal").ap()
    d_brd = nc.dram_tensor("d_brd", [1, 4096], F32, kind="Internal").ap()
    d_brdb = nc.dram_tensor("d_brdb", [1, 4096], BF16, kind="Internal").ap()

    p = Sched()

    with ExitStack() as ctx:
        sb = lambda n, s, dt: ctx.enter_context(nc.sbuf_tensor(n, s, dt))
        BIG1 = sb("BIG1", [128, 32768], BF16)
        BIG2 = sb("BIG2", [128, 32768], BF16)
        BIG3 = sb("BIG3", [128, 32768], BF16)
        RING = sb("RING", [128, 2048], F32)
        SGN = sb("SGN", [128, 2048], BF16)
        SM = sb("SM", [128, 256], F32)
        CB = sb("CB", [128, 40], F32)
        CS = sb("CS", [128, 80], F32)
        CSR = sb("CSR", [128, 192], F32)
        ONE = sb("ONE", [128, 128], BF16)
        IDN = sb("IDN", [128, 128], BF16)
        WCM = sb("WCM", [128, 2], F32)
        RZ = sb("RZ", [16, 258], BF16)

        pb = [ctx.enter_context(nc.psum_tensor(f"pb{i}", [128, 512], F32))
              for i in range(8)]
        pbt6 = pb[6].ap().bitcast(BF16)
        pbt7 = pb[7].ap().bitcast(BF16)

        sems = {}
        for s in ("MS", "XS", "WR0", "WR1", "WM0", "WM1", "WM2", "WM3",
                  "SS0", "SS1", "XE", "SP", "XF", "OU", "GA", "GB", "GC",
                  "T", "V", "SC"):
            sems[s] = ctx.enter_context(nc.semaphore(s))

        blk = ctx.enter_context(nc.Block())

        ms_chain = [None]

        def dma(eng, dst, src, waits=(), sem=None):
            return p.op(eng, lambda e, d=dst, s=src: e.dma_start(out=d, in_=s),
                        waits=waits, inc=sem, inc_by=16)

        def dmams(dst, src, waits=()):
            t = dma("scalar", dst, src,
                    waits=list(waits) + [ms_chain[0]], sem="MS")
            ms_chain[0] = t
            return t

        def op(eng, fn, waits=()):
            return p.op(eng, fn, waits=waits,
                        inc={"vector": "V", "scalar": "SC", "tensor": "T"}[eng])

        bank_r = {}
        bank_w = {}

        t_one = dmams(ONE[:], ones_bf)
        t_idn = dmams(IDN[:], ident_bf)
        t_eps = op("vector", lambda e: e.memset(CB[:, 32:33], EPS))
        t_qs = dmams(CS[:, 0:32], qs_cm)
        t_ks = dmams(CS[:, 32:40], ks_cm)
        t_vs = dmams(CS[:, 40:48], vs_cm)

        s_chunks = []
        for c in range(32):
            s_chunks.append((wq_n[128 * c:128 * c + 128, :], c))
        for c in range(8):
            s_chunks.append((wk_n[128 * c:128 * c + 128, :], 32 + c))
        for c in range(8):
            s_chunks.append((wv_n[128 * c:128 * c + 128, :], 40 + c))
        for c in range(32):
            s_chunks.append((wo_n[128 * c:128 * c + 128, :], 48 + c))

        def s_stage(i):
            if i < 32:
                base = 0 if i % 2 == 0 else 4096
            elif i < 40:
                base = 16384 + (0 if i % 2 == 0 else 4096)
            elif i < 64:
                base = 24576 + (0 if i % 2 == 0 else 4096)
            else:
                return BIG1[:, (0 if i % 2 == 0 else 4096):
                            (0 if i % 2 == 0 else 4096) + 4096]
            return BIG3[:, base:base + 4096]

        s_state = {"i": 0, "red": {}, "era_wait": {}}
        s_red_tok = [None] * 80

        def s_step(n=1):
            for _ in range(n):
                i = s_state["i"]
                if i >= 80:
                    return
                s_state["i"] = i + 1
                buf = s_stage(i)
                w = [s_state["red"].get(i - 2)]
                if i in s_state["era_wait"]:
                    w.append(s_state["era_wait"][i])
                src, col = s_chunks[i]
                td = dma("gpsimd", buf, src, waits=[t for t in w if t],
                         sem="SS0" if i % 2 == 0 else "SS1")
                tr = op("vector", lambda e, b=buf, c=col: e.reduce_sum(
                    out=SM[:, c:c + 1], in_=b, axis=AX,
                    apply_absolute_value=True), waits=[td])
                s_state["red"][i] = tr
                s_red_tok[i] = tr

        def ln1_group(grp, stage, normb_of, nb_reuse, extra_w=(), nb_war0=None,
                      stage_b=None, junk_of=None, stash=False):
            stages = [stage] + ([stage_b] if stage_b is not None else [])
            nstg = len(stages)
            tt = [{} for _ in range(8)]
            ev_toks = []

            def head(tc):
                d = tt[tc]
                c0 = 160 + 16 * (tc % 2)
                d["c0"] = c0
                stg = stages[tc % nstg]
                S = [stg[:, 0:2048], stg[:, 2048:4096]]
                d["S"] = S
                pv = tt[tc - nstg] if tc - nstg >= 0 else None
                nb = normb_of(tc)
                if junk_of is not None:
                    jks = [junk_of(tc), junk_of(tc)]
                else:
                    jks = [nb[:, 0:2048], nb[:, 2048:4096]]
                nbw = None
                if tc - nb_reuse >= 0:
                    nbw = tt[tc - nb_reuse].get("ttr")
                elif nb_war0 is not None:
                    nbw = nb_war0
                for h in range(2):
                    w = list(extra_w) if tc < nstg else []
                    if pv:
                        w += [pv.get("E%d" % h), pv.get("C%d" % h)]
                    d["A%d" % h] = dma(
                        "scalar", S[h],
                        x[1024 * grp + 128 * tc:1024 * grp + 128 * tc + 128,
                          2048 * h:2048 * h + 2048],
                        waits=[t for t in w if t], sem="XS")
                for h in range(2):
                    d["B%d" % h] = op("vector", lambda e, s=S[h], c=c0 + h:
                                      e.reduce_sum(out=SM[:, c:c + 1], in_=s,
                                                   axis=AX),
                                      waits=[d["A%d" % h]])
                if stash:
                    d["stash"] = op("vector", lambda e, c=c0, tcc=tc:
                                    e.tensor_scalar(
                        out=CSR[:, 48 + tcc:49 + tcc], in0=SM[:, c:c + 1],
                        scalar1=SM[:, c + 1:c + 2], scalar2=1.0,
                        op0=ALU.add, op1=ALU.mult),
                        waits=[d["B0"], d["B1"]])
                d["nbw"] = nbw
                for h in range(2):
                    d["C%d" % h] = op("scalar", lambda e, s=S[h], c=c0 + 2 + h,
                                      o=jks[h]:
                                      e.activation(out=o, in_=s, func=AF.Square,
                                                   accum_out=SM[:, c:c + 1]),
                                      waits=[d["A%d" % h]] +
                                            ([nbw] if junk_of is None else []))

            def tail(tc):
                d = tt[tc]
                c0 = d["c0"]
                S = d["S"]
                nb = normb_of(tc)
                tneg = op("vector", lambda e, c=c0: e.tensor_scalar(
                    out=SM[:, c + 4:c + 5], in0=SM[:, c:c + 1],
                    scalar1=SM[:, c + 1:c + 2], scalar2=-1.0 / 4096.0,
                    op0=ALU.add, op1=ALU.mult), waits=[d["B0"], d["B1"]])
                tmusq = op("vector", lambda e, c=c0: e.tensor_scalar(
                    out=SM[:, c + 5:c + 6], in0=SM[:, c + 4:c + 5],
                    scalar1=SM[:, c + 4:c + 5], scalar2=-1.0,
                    op0=ALU.mult, op1=ALU.mult), waits=[tneg])
                tbias = op("vector", lambda e, c=c0: e.tensor_scalar_add(
                    out=SM[:, c + 6:c + 7], in0=SM[:, c + 5:c + 6],
                    scalar1=EPS), waits=[tmusq])
                tcomb = op("vector", lambda e, c=c0: e.tensor_scalar(
                    out=SM[:, c + 7:c + 8], in0=SM[:, c + 2:c + 3],
                    scalar1=SM[:, c + 3:c + 4], scalar2=1.0 / 4096.0,
                    op0=ALU.add, op1=ALU.mult), waits=[d["C0"], d["C1"]])
                tsq = op("scalar", lambda e, c=c0: e.activation(
                    out=SM[:, c + 8:c + 9], in_=SM[:, c + 7:c + 8],
                    func=AF.Sqrt, bias=SM[:, c + 6:c + 7]),
                    waits=[tcomb, tbias])
                trstd = op("vector", lambda e, c=c0: e.reciprocal(
                    out=SM[:, c + 9:c + 10], in_=SM[:, c + 8:c + 9]),
                    waits=[tsq])
                for h in range(2):
                    d["E%d" % h] = op("vector", lambda e, s=S[h], c=c0,
                                      o=nb[:, 2048 * h:2048 * h + 2048]:
                                      e.tensor_scalar(
                        out=o, in0=s, scalar1=SM[:, c + 4:c + 5],
                        scalar2=SM[:, c + 9:c + 10],
                        op0=ALU.add, op1=ALU.mult),
                        waits=[trstd, d["C%d" % h]] +
                              ([d["nbw"]] if junk_of is not None else []))
                ttr_last = None
                ev = None
                for g4 in range(4):
                    pt, bidx = (pbt6, 6) if g4 % 2 == 0 else (pbt7, 7)
                    en = d["E0"] if g4 < 2 else d["E1"]
                    ttr = None
                    for k in range(8):
                        ic = g4 * 8 + k
                        ttr = op("tensor", lambda e, icc=ic, ptt=pt, kk=k:
                                 e.transpose(ptt[:, 128 * kk:128 * kk + 128],
                                             nb[:, 128 * icc:128 * icc + 128],
                                             IDN[:]),
                                 waits=[en, t_idn, bank_r.get(bidx)])
                    bank_w[bidx] = ttr
                    ttr_last = ttr
                    ev = op("scalar", lambda e, ptt=pt, g=g4, tcc=tc: e.copy(
                        out=BIG1.rearrange("p (ic t) -> p ic t", t=1024)[
                            :, 8 * g:8 * g + 8, 128 * tcc:128 * tcc + 128],
                        in_=ptt[:, 0:1024].rearrange(
                            "p (k t) -> p k t", t=128)), waits=[ttr])
                    bank_r[bidx] = ev
                d["ttr"] = ttr_last
                d["ev"] = ev
                ev_toks.append(ev)
                if grp == 0:
                    s_step(1)

            for tc in range(8):
                head(tc)
                tail(tc)
            lastd = tt[7]
            return (ev_toks, lastd["E1"], [t["ttr"] for t in tt],
                    [t.get("stash") for t in tt])

        ringf = RING
        ringb = RING[:, 0:2048].bitcast(BF16)
        wq_state = {"cons": {}, "sgn_cons": {}}

        def w_quarter(src_q, slot, waits_extra=()):
            rap = ringf[:, 1024 * slot:1024 * slot + 1024]
            sap = SGN[:, 1024 * slot:1024 * slot + 1024]
            w = [wq_state["cons"].get(slot)] + list(waits_extra)
            td = dma("sync", rap, src_q, waits=[t for t in w if t],
                     sem="WR%d" % slot)
            tsg = op("scalar", lambda e, r=rap, s=sap: e.sign(out=s, in_=r),
                     waits=[td, wq_state["sgn_cons"].get(slot)])
            wq_state["cons"][slot] = tsg
            return tsg, sap

        stage_own = BIG2[:, 24576:32768].bitcast(F32)
        stage_own_b = BIG3[:, 8192:16384].bitcast(F32)
        nb_own = lambda tc: BIG3[:, 16384 + 4096 * (tc % 2):
                                 16384 + 4096 * (tc % 2) + 4096]
        ev0, lnx0_n, ttr0, stash0 = ln1_group(0, stage_own, nb_own, 2,
                                              stage_b=stage_own_b, stash=True)
        lnx0 = ev0[-1]
        s_state["era_wait"][32] = ev0[6]
        s_state["era_wait"][33] = ev0[7]

        qev = {}
        for qc in range(32):
            mmq = None
            for j in range(4):
                src = wq_t[qc][:, 8 * j:8 * j + 8, :]
                slot = (qc * 4 + j) % 2
                tsg, sap = w_quarter(src, slot)
                for k in range(8):
                    ic = 8 * j + k
                    for th in range(2):
                        b = 2 * (qc % 2) + th
                        w = [tsg]
                        if qc == 0 and j == 0 and k == 0:
                            w.append(lnx0)
                        if ic == 0 and th == 0:
                            w += [bank_r.get(b), bank_r.get(b + 1),
                                  bank_w.get(b), bank_w.get(b + 1)]
                        mmq = op("tensor", lambda e, s=sap, kk=k, icc=ic,
                                 t2=th, bb=b: e.matmul(
                            pb[bb][:], s[:, 128 * kk:128 * kk + 128],
                            BIG1[:, 1024 * icc + 512 * t2:
                                 1024 * icc + 512 * t2 + 512],
                            start=(icc == 0), stop=(icc == 31)),
                            waits=[t for t in w if t])
                wq_state["sgn_cons"][slot] = mmq
            for th in range(2):
                b = 2 * (qc % 2) + th
                bank_w[b] = mmq
                tev = op("vector", lambda e, q=qc, t2=th, bb=b: e.tensor_copy(
                    out=BIG2[:, 1024 * q + 512 * t2:1024 * q + 512 * t2 + 512],
                    in_=pb[bb][:]), waits=[mmq])
                bank_r[b] = tev
                qev[(qc, th)] = tev
            s_step(1)

        t_sq = op("vector", lambda e: e.tensor_scalar(
            out=SM[:, 80:112], in0=SM[:, 0:32], scalar1=1.0 / 4096.0,
            scalar2=1e-8, op0=ALU.mult, op1=ALU.max), waits=[s_red_tok[31]])
        t_sq2 = op("vector", lambda e: e.tensor_mul(
            out=SM[:, 80:112], in0=SM[:, 80:112], in1=CS[:, 0:32]),
            waits=[t_sq, t_qs])
        t_sk = op("vector", lambda e: e.tensor_scalar(
            out=SM[:, 112:120], in0=SM[:, 32:40], scalar1=1.0 / 4096.0,
            scalar2=1e-8, op0=ALU.mult, op1=ALU.max), waits=[s_red_tok[39]])
        t_sk2 = op("vector", lambda e: e.tensor_mul(
            out=SM[:, 112:120], in0=SM[:, 112:120], in1=CS[:, 32:40]),
            waits=[t_sk, t_ks])
        csc = None
        for h in range(NH):
            csc = op("vector", lambda e, hh=h: e.tensor_mul(
                out=CS[:, 48 + hh:49 + hh], in0=SM[:, 80 + hh:81 + hh],
                in1=SM[:, 112 + hh // 4:113 + hh // 4]), waits=[t_sq2, t_sk2])
        t_csc = op("vector", lambda e: e.tensor_scalar_mul(
            out=CS[:, 48:80], in0=CS[:, 48:80], scalar1=RSQ), waits=[csc])
        qscale = {}
        for h in range(NH):
            qscale[h] = op("vector", lambda e, hh=h: e.tensor_scalar_mul(
                out=BIG2[:, 1024 * hh:1024 * hh + 1024],
                in0=BIG2[:, 1024 * hh:1024 * hh + 1024],
                scalar1=CS[:, 48 + hh:49 + hh]),
                waits=[t_csc, qev[(h, 0)], qev[(h, 1)]])
        q_scaled = qscale[NH - 1]

        qm_state = {"h": 0, "mm": {}, "rm": {}}

        def qmax_steps(n):
            for _ in range(n):
                h = qm_state["h"]
                if h >= NH:
                    return
                qm_state["h"] = h + 1
                sl = h % 2
                sq = BIG3[:, 16384 + 1024 * sl:16384 + 1024 * sl + 1024]
                b = 4 + sl
                w = [qscale[h], qm_state["mm"].get(h - 2)]
                if h < 2:
                    w.append(s_red_tok[38])
                tq2 = op("vector", lambda e, hh=h, s=sq: e.tensor_mul(
                    out=s, in0=BIG2[:, 1024 * hh:1024 * hh + 1024],
                    in1=BIG2[:, 1024 * hh:1024 * hh + 1024]),
                    waits=[t for t in w if t])
                tmm = op("tensor", lambda e, s=sq, bb=b: e.matmul(
                    pb[bb][0:1, 0:512], ONE[:, 0:1], s[:, 0:512],
                    start=True, stop=False),
                    waits=[tq2, t_one, bank_r.get(b), bank_w.get(b)])
                tmm = op("tensor", lambda e, s=sq, bb=b: e.matmul(
                    pb[bb][0:1, 0:512], ONE[:, 0:1], s[:, 512:1024],
                    start=False, stop=True), waits=[])
                bank_w[b] = tmm
                qm_state["mm"][h] = tmm
                tmx = op("vector", lambda e, hh=h, bb=b: e.reduce_max(
                    out=CSR[0:1, hh:hh + 1], in_=pb[bb][0:1, 0:512],
                    axis=AX), waits=[tmm])
                bank_r[b] = tmx
                qm_state["rm"][h] = tmx

        def k_group(grp, lnx_tok, post_g=None):
            kev = None
            kev_toks = []
            for g in range(8):
                last_mm = None
                for j in range(4):
                    src = wk_t[g][:, 8 * j:8 * j + 8, :]
                    slot = (g * 4 + j) % 2
                    we = [lnx_tok] if g == 0 and j == 0 else []
                    tsg, sap = w_quarter(src, slot, waits_extra=we)
                    for k in range(8):
                        ic = 8 * j + k
                        for th in range(2):
                            b = 2 * (g % 2) + th
                            w = [tsg]
                            if ic == 0 and th == 0:
                                w += [bank_r.get(b), bank_r.get(b + 1),
                                      bank_w.get(b), bank_w.get(b + 1)]
                            last_mm = op("tensor", lambda e, s=sap, kk=k,
                                         icc=ic, t2=th, bb=b: e.matmul(
                                pb[bb][:], s[:, 128 * kk:128 * kk + 128],
                                BIG1[:, 1024 * icc + 512 * t2:
                                     1024 * icc + 512 * t2 + 512],
                                start=(icc == 0), stop=(icc == 31)),
                                waits=[t for t in w if t])
                    wq_state["sgn_cons"][slot] = last_mm
                for th in range(2):
                    b = 2 * (g % 2) + th
                    bank_w[b] = last_mm
                    col = 2048 * g + 1024 * grp + 512 * th
                    kev = op("scalar", lambda e, cc=col, bb=b: e.copy(
                        out=BIG3[:, cc:cc + 512], in_=pb[bb][:]),
                        waits=[last_mm] +
                              ([s_red_tok[31]] if grp == 0 and g == 0 else []))
                    bank_r[b] = kev
                kev_toks.append(kev)
                if grp == 0:
                    s_step(1)
                if post_g is not None:
                    post_g(g, kev)
            return kev, kev_toks

        k0_done, _ = k_group(0, lnx0, post_g=lambda g, kv: qmax_steps(4))
        qmax_steps(32)
        qm_done = qm_state["mm"][NH - 1]

        def v_group(grp, lnx_tok, ln1p_done=None, first_ev_w=()):
            vev = None
            mini_cons = {m: wq_state["cons"].get(m // 2) for m in range(4)}
            sgn_mini_cons = {m: wq_state["sgn_cons"].get(m // 2)
                             for m in range(4)}
            last_mm = None
            last_sign = None
            first = True
            for vc in range(2):
                for ic in range(32):
                    m = (vc * 32 + ic) % 4
                    rap = ringf[:, 512 * m:512 * m + 512]
                    sap = SGN[:, 512 * m:512 * m + 512]
                    w = [mini_cons.get(m)]
                    if first:
                        w.append(lnx_tok)
                    td = dma("sync", rap, wv_ic[ic][:, 512 * vc:512 * vc + 512],
                             waits=[t for t in w if t], sem="WM%d" % m)
                    tsg = op("scalar", lambda e, r=rap, s=sap: e.sign(
                        out=s, in_=r), waits=[td, sgn_mini_cons.get(m)])
                    mini_cons[m] = tsg
                    last_sign = tsg
                    for sl in range(8):
                        w2 = [tsg]
                        if ic == 0:
                            w2 += [bank_r.get(sl), bank_w.get(sl)]
                        last_mm = op("tensor", lambda e, s=sap, icc=ic,
                                     ss=sl: e.matmul(
                            pb[ss][:],
                            BIG1[:, 1024 * icc + 128 * ss:
                                 1024 * icc + 128 * ss + 128],
                            s[:], start=(icc == 0), stop=(icc == 31)),
                            waits=[t for t in w2 if t])
                    sgn_mini_cons[m] = last_mm
                    first = False
                    if grp == 0 and vc == 0 and ic % 2 == 0:
                        s_step(1)
                for sl in range(8):
                    bank_w[sl] = last_mm
                    col = 16384 + 1024 * (8 * grp + sl) + 512 * vc
                    we = [last_mm]
                    if vc == 0 and sl == 0:
                        we += list(first_ev_w)
                        if grp == 0:
                            we.append(s_red_tok[39])
                        if grp == 1 and ln1p_done:
                            we.append(ln1p_done)
                    vev = op("scalar", lambda e, cc=col, bb=sl: e.copy(
                        out=BIG3[:, cc:cc + 512], in_=pb[bb][:]), waits=we)
                    bank_r[sl] = vev
            wq_state["cons"][0] = last_sign
            wq_state["cons"][1] = last_sign
            wq_state["sgn_cons"][0] = last_mm
            wq_state["sgn_cons"][1] = last_mm
            return vev, last_mm, last_sign

        v0_done, v0_mm, v0_sg = v_group(0, lnx0, first_ev_w=[qm_done])

        stage_par = BIG3[:, 24576:32768].bitcast(F32)
        nb_par_ap = RING[:, 0:2048].bitcast(BF16)
        ev1, lnx1_n, ttr1, _ = ln1_group(
            1, stage_par, lambda tc: nb_par_ap, 1,
            extra_w=[v0_mm, v0_done, s_red_tok[62], s_red_tok[63]],
            nb_war0=v0_sg, junk_of=lambda tc: SGN[:, 0:2048])
        lnx1_tok = ev1[-1]

        km_state = {"toks": {}, "mm": {}}

        def kmax_emit(g, kev_tok):
            sl = g % 2
            sq = BIG3[:, 24576 + 2048 * sl:24576 + 2048 * sl + 2048]
            b = 4 + sl
            w = [kev_tok, km_state["mm"].get(g - 2)]
            tk2 = op("vector", lambda e, gg=g, s=sq: e.tensor_mul(
                out=s, in0=BIG3[:, 2048 * gg:2048 * gg + 2048],
                in1=BIG3[:, 2048 * gg:2048 * gg + 2048]),
                waits=[t for t in w if t])
            tmm = None
            for kk in range(4):
                tmm = op("tensor", lambda e, k4=kk, s=sq, bb=b: e.matmul(
                    pb[bb][0:1, 0:512], ONE[:, 0:1],
                    s[:, 512 * k4:512 * k4 + 512],
                    start=(k4 == 0), stop=(k4 == 3)),
                    waits=[tk2, t_one, bank_r.get(b), bank_w.get(b)]
                    if kk == 0 else [])
            bank_w[b] = tmm
            km_state["mm"][g] = tmm
            tmx = op("vector", lambda e, gg=g, bb=b: e.reduce_max(
                out=CSR[0:1, 32 + gg:33 + gg], in_=pb[bb][0:1, 0:512],
                axis=AX), waits=[tmm])
            bank_r[b] = tmx
            km_state["toks"][g] = tmx

        kp_toks = {}

        def kpost(g, kev_tok):
            kp_toks[g] = kev_tok
            if g >= 1:
                kmax_emit(g - 1, kp_toks[g - 1])

        k1_done, _ = k_group(1, lnx1_tok, post_g=kpost)
        kmax_emit(7, kp_toks[7])
        km_done = km_state["mm"][7]
        kmax_last = km_state["toks"][7]

        tc2 = None
        for h in range(NH):
            tc2 = op("vector", lambda e, hh=h: e.tensor_mul(
                out=CSR[0:1, 64 + hh:65 + hh],
                in0=CSR[0:1, 0 + hh:1 + hh],
                in1=CSR[0:1, 32 + hh // 4:33 + hh // 4]),
                waits=[qm_state["rm"][h], km_state["toks"][h // 4]])
        trsb = op("scalar", lambda e: e.activation(
            out=CSR[0:1, 128:160], in_=CSR[0:1, 64:96], func=AF.Sqrt,
            scale=1.0, bias=CB[0:1, 32:33]), waits=[tc2, t_eps])
        tnegb = op("vector", lambda e: e.tensor_scalar_mul(
            out=CSR[0:1, 160:192], in0=CSR[0:1, 128:160], scalar1=-1.0),
            waits=[trsb])
        t_cb0 = dmams(d_brd[0:1, 0:32], CSR[0:1, 160:192], waits=[tnegb])
        t_cb = dmams(CB[:, 0:32], d_brd[0:1, 0:32].broadcast_to([128, 32]),
                     waits=[t_cb0])

        v1_done, kv_mm, v1_sg = v_group(1, lnx1_tok, ln1p_done=lnx1_n,
                                        first_ev_w=[km_done])
        kv_done = v1_done

        t_sv = op("vector", lambda e: e.tensor_scalar(
            out=SM[:, 120:128], in0=SM[:, 40:48], scalar1=1.0 / 4096.0,
            scalar2=1e-8, op0=ALU.mult, op1=ALU.max), waits=[s_red_tok[47]])
        t_vsv = op("vector", lambda e: e.tensor_mul(
            out=SM[:, 120:128], in0=SM[:, 120:128], in1=CS[:, 40:48]),
            waits=[t_sv, t_vs])

        s_state["era_wait"][64] = kv_mm

        crow = BIG1[:, 16384:18432].bitcast(F32)
        pr_cols = [8192 + 512 * i for i in range(8)]
        pring = {}
        s_cons = {}
        pair_ev = {4: bank_r.get(4), 5: bank_r.get(5),
                   6: bank_r.get(6), 7: bank_r.get(7)}
        ring_i = [0]
        last_evA = None
        first_w = [t_cb, kv_done, kv_mm, q_scaled,
                   bank_r.get(2), bank_r.get(3)]
        blk_i = [0]

        for g in range(NKV):
            for tcol in range(2):
                for pair in range(2):
                    h0 = 4 * g + 2 * pair
                    evw = [pair_ev[4], pair_ev[5], pair_ev[6], pair_ev[7]]
                    exps = {}
                    avs = {}
                    rss = {}

                    def issue_qk(kc):
                        for i, h in enumerate((h0, h0 + 1)):
                            b = 2 * (kc % 2) + i
                            wts = [s_cons.get(b)]
                            if g == 0 and tcol == 0 and pair == 0 and kc < 2:
                                wts = wts + first_w + [bank_r.get(b),
                                                       bank_w.get(b)]
                            tmm = op("tensor", lambda e, gg=g, k=kc, hh=h,
                                     bb=b, t2=tcol: e.matmul(
                                pb[bb][:],
                                BIG3[:, 2048 * gg + 128 * k:
                                     2048 * gg + 128 * k + 128],
                                BIG2[:, 1024 * hh + 512 * t2:
                                     1024 * hh + 512 * t2 + 512],
                                start=True, stop=True),
                                waits=[t for t in wts if t])
                            bank_w[b] = tmm
                            slot = ring_i[0] % 8
                            ring_i[0] += 1
                            tex = op("scalar", lambda e, bb=b, hh=h, sl=slot:
                                     e.activation(
                                out=BIG1[:, pr_cols[sl]:pr_cols[sl] + 512],
                                in_=pb[bb][:], func=AF.Exp,
                                bias=CB[:, hh:hh + 1]),
                                waits=[tmm, t_cb, pring.get(slot), kv_mm])
                            s_cons[b] = tex
                            exps[(kc, i)] = (tex, slot)

                    def issue_avrs(kc):
                        for i in range(2):
                            tex, slot = exps[(kc, i)]
                            wts = [tex]
                            if kc == 0:
                                wts += evw
                            tav = op("tensor", lambda e, gg=g, k=kc, sl=slot,
                                     ab=4 + i: e.matmul(
                                pb[ab][:],
                                BIG3[:, 16384 + 1024 * k + 128 * gg:
                                     16384 + 1024 * k + 128 * gg + 128],
                                BIG1[:, pr_cols[sl]:pr_cols[sl] + 512],
                                start=(k == 0), stop=(k == 15)), waits=wts)
                            trs = op("tensor", lambda e, k=kc, sl=slot,
                                     rb=6 + i: e.matmul(
                                pb[rb][:], ONE[:],
                                BIG1[:, pr_cols[sl]:pr_cols[sl] + 512],
                                start=(k == 0), stop=(k == 15)),
                                waits=([t_one] + evw) if kc == 0 else [])
                            pring[slot] = trs
                            bank_w[4 + i] = tav
                            bank_w[6 + i] = trs
                            avs[i] = tav
                            rss[i] = trs

                    issue_qk(0)
                    for kc in range(1, 16):
                        issue_qk(kc)
                        issue_avrs(kc - 1)
                    issue_avrs(15)

                    for i in range(2):
                        h = h0 + i
                        trr = op("vector", lambda e, ii=i: e.reciprocal(
                            out=crow[:, 512 * ii:512 * ii + 512],
                            in_=pb[6 + ii][:]),
                            waits=[rss[i], pair_ev[4 + i]])
                        tevA = op("vector", lambda e, ii=i, hh=h, t2=tcol:
                                  e.tensor_tensor(
                            out=BIG2[:, 1024 * hh + 512 * t2:
                                     1024 * hh + 512 * t2 + 512],
                            in0=pb[4 + ii][:],
                            in1=crow[:, 512 * ii:512 * ii + 512],
                            op=ALU.mult), waits=[avs[i], trr])
                        pair_ev[4 + i] = tevA
                        pair_ev[6 + i] = tevA
                        bank_r[4 + i] = tevA
                        bank_r[6 + i] = tevA
                        last_evA = tevA
                    blk_i[0] += 1
                    if blk_i[0] <= 16:
                        s_step(1)
        attn_done = last_evA

        OST = BIG3[:, 0:4096]
        zone = BIG3[:, 4096:4224]
        evd = BIG3[:, 4224:4352]
        orow_b = BIG3[0:1, 4608:8704]
        osr_b = BIG3[0:1, 8704:12800]
        t_so = op("vector", lambda e: e.tensor_scalar(
            out=SM[:, 128:160], in0=SM[:, 48:80], scalar1=1.0 / 4096.0,
            scalar2=1e-8, op0=ALU.mult, op1=ALU.max),
            waits=[s_red_tok[79], attn_done])
        dp_z = op("vector", lambda e: e.memset(zone, 0.0), waits=[attn_done])
        dp_cp = op("vector", lambda e: e.tensor_copy(
            out=zone[:, 0:32], in_=SM[:, 128:160]), waits=[dp_z, t_so])
        dp_tr = op("tensor", lambda e: e.transpose(
            pbt6[:, 0:128], zone, IDN[:]),
            waits=[dp_cp, t_idn, bank_r.get(6), bank_w.get(6)])
        bank_w[6] = dp_tr
        dp_ev = op("scalar", lambda e: e.copy(out=evd, in_=pbt6[:, 0:128]),
                   waits=[dp_tr])
        bank_r[6] = dp_ev
        dp_row = dmams(orow_b, BIG3[0:32, 4224:4352], waits=[dp_ev])
        dp_os = dma("gpsimd", osr_b, os_row, waits=[attn_done], sem="GA")
        dp_mul = op("vector", lambda e: e.tensor_mul(
            out=orow_b, in0=orow_b, in1=osr_b), waits=[dp_row, dp_os])
        t_ost0 = dmams(d_brdb[0:1, 0:4096], orow_b, waits=[dp_mul])
        t_ost = dmams(OST, d_brdb[0:1, 0:4096].broadcast_to([128, 4096]),
                      waits=[t_ost0])

        o_ring_cons = {m: wq_state["cons"].get(m // 2) for m in range(4)}
        o_sgn_cons = {m: wq_state["sgn_cons"].get(m // 2) for m in range(4)}
        mm_prev_ocol = None
        oev = {}
        for ocol in range(8):
            mm_last = None
            for ic in range(32):
                m = (ocol * 32 + ic) % 4
                rap = ringf[:, 512 * m:512 * m + 512]
                sap = SGN[:, 512 * m:512 * m + 512]
                bap = ringb[:, 1024 * m:1024 * m + 512]
                w = [o_ring_cons.get(m)]
                if ocol == 0 and ic < 4:
                    w.append(attn_done)
                td = dma("sync", rap, wo_t[ocol, ic],
                         waits=[t for t in w if t], sem="WM%d" % m)
                tsg = op("scalar", lambda e, r=rap, s=sap: e.sign(
                    out=s, in_=r),
                    waits=[td, o_sgn_cons.get(m)] +
                          ([attn_done] if ocol == 0 and ic < 4 else []))
                tst = op("vector", lambda e, s=sap, b2=bap, icc=ic, oc=ocol:
                         e.scalar_tensor_tensor(
                    out=b2, in0=s,
                    scalar=SM[:, 120 + icc // 4:121 + icc // 4],
                    in1=OST[:, 512 * oc:512 * oc + 512],
                    op0=ALU.mult, op1=ALU.mult),
                    waits=[tsg, t_ost, t_vsv])
                o_sgn_cons[m] = tst
                for t8 in range(8):
                    w2 = [tst]
                    if ic == 0:
                        w2 += [bank_r.get(t8), bank_w.get(t8)]
                        if ocol > 0:
                            w2.append(mm_prev_ocol)
                    mm_last = op("tensor", lambda e, b2=bap, icc=ic, tt=t8:
                                 e.matmul(
                        pb[tt][:],
                        BIG2[:, 1024 * icc + 128 * tt:
                             1024 * icc + 128 * tt + 128],
                        b2[:], start=(icc == 0), stop=(icc == 31)),
                        waits=w2)
                o_ring_cons[m] = mm_last
            mm_prev_ocol = mm_last
            for t8 in range(8):
                bank_w[t8] = mm_last
                we = [mm_last]
                if ocol == 0 and t8 == 0:
                    we += [s_red_tok[78], s_red_tok[79]]
                tev = op("scalar", lambda e, tt=t8, oc=ocol: e.copy(
                    out=BIG1[:, 4096 * tt + 512 * oc:4096 * tt + 512 * oc + 512],
                    in_=pb[tt][:]), waits=we)
                bank_r[t8] = tev
                oev[(ocol, t8)] = tev
        o_done = oev[(7, 7)]
        o_mm_last = mm_prev_ocol

        xe1 = [BIG2[:, 0:8192].bitcast(F32), BIG2[:, 8192:16384].bitcast(F32)]
        e2buf = BIG2[:, 16384:24576].bitcast(F32)
        ejunk = BIG2[:, 24576:28672]
        lnx1_tm = BIG2[:, 28672:32768]
        et = [{} for _ in range(8)]
        spills = [None] * 8

        def e_head(tk):
            d = et[tk]
            c = 192 + 16 * (tk % 2)
            d["c"] = c
            orow = BIG1[:, 4096 * tk:4096 * tk + 4096]
            d["o"] = orow
            ordy = oev[(7, tk)]
            if tk < 2:
                wx = [o_mm_last]
            else:
                pv = et[tk - 2]
                wx = [spills[tk - 2], pv["H"], pv["J"]]
            d["A"] = dma("scalar", xe1[tk % 2],
                         x[128 * tk:128 * tk + 128, :],
                         waits=[t for t in wx if t], sem="XE")
            d["B"] = op("vector", lambda e, o=orow, cc=c: e.reduce_sum(
                out=SM[:, cc:cc + 1], in_=o, axis=AX), waits=[ordy])
            d["C"] = op("scalar", lambda e, o=orow, cc=c: e.activation(
                out=ejunk, in_=o, func=AF.Square,
                accum_out=SM[:, cc + 1:cc + 2]), waits=[ordy])
            tneg = op("vector", lambda e, cc=c: e.tensor_scalar_mul(
                out=SM[:, cc + 2:cc + 3], in0=SM[:, cc:cc + 1],
                scalar1=-1.0 / 4096.0), waits=[d["B"]])
            tmusq = op("vector", lambda e, cc=c: e.tensor_scalar(
                out=SM[:, cc + 3:cc + 4], in0=SM[:, cc + 2:cc + 3],
                scalar1=SM[:, cc + 2:cc + 3], scalar2=-1.0,
                op0=ALU.mult, op1=ALU.mult), waits=[tneg])
            tbias = op("vector", lambda e, cc=c: e.tensor_scalar_add(
                out=SM[:, cc + 4:cc + 5], in0=SM[:, cc + 3:cc + 4],
                scalar1=EPS), waits=[tmusq])
            tcomb = op("vector", lambda e, cc=c: e.tensor_scalar_mul(
                out=SM[:, cc + 5:cc + 6], in0=SM[:, cc + 1:cc + 2],
                scalar1=1.0 / 4096.0), waits=[d["C"]])
            tsq = op("scalar", lambda e, cc=c: e.activation(
                out=SM[:, cc + 6:cc + 7], in_=SM[:, cc + 5:cc + 6],
                func=AF.Sqrt, bias=SM[:, cc + 4:cc + 5]),
                waits=[tcomb, tbias])
            trstd = op("vector", lambda e, cc=c: e.reciprocal(
                out=SM[:, cc + 7:cc + 8], in_=SM[:, cc + 6:cc + 7]),
                waits=[tsq])
            d["nb"] = op("vector", lambda e, cc=c: e.tensor_mul(
                out=SM[:, cc + 8:cc + 9], in0=SM[:, cc + 2:cc + 3],
                in1=SM[:, cc + 7:cc + 8]), waits=[trstd])
            d["rstd"] = trstd

        def e_tail(tk):
            d = et[tk]
            c = d["c"]
            xb = xe1[tk % 2]
            tE = op("scalar", lambda e, o=d["o"], cc=c: e.activation(
                out=e2buf, in_=o, func=AF.Identity,
                scale=SM[:, cc + 7:cc + 8], bias=SM[:, cc + 8:cc + 9],
                accum_out=SM[:, cc + 9:cc + 10]),
                waits=[d["nb"]] +
                      ([et[tk - 1]["add"]] if tk > 0 else []))
            d["e2"] = tE
            tF = op("vector", lambda e, xbb=xb: e.tensor_tensor(
                out=xbb, in0=xbb, in1=e2buf, op=ALU.add),
                waits=[tE, d["A"]])
            d["add"] = tF
            spills[tk] = dma("scalar", x1_spill[128 * tk:128 * tk + 128, :],
                             xb, waits=[tF], sem="SP")
            tS1 = op("vector", lambda e, cc=c, tkk=tk: e.tensor_tensor(
                out=SM[:, cc + 9:cc + 10], in0=SM[:, cc + 9:cc + 10],
                in1=CSR[:, 48 + tkk:49 + tkk], op=ALU.add),
                waits=[tE, stash0[tk]])
            d["H"] = op("scalar", lambda e, cc=c, xbb=xb: e.activation(
                out=ejunk, in_=xbb, func=AF.Square,
                accum_out=SM[:, cc + 10:cc + 11]), waits=[tF])
            tneg3 = op("vector", lambda e, cc=c: e.tensor_scalar_mul(
                out=SM[:, cc + 11:cc + 12], in0=SM[:, cc + 9:cc + 10],
                scalar1=-1.0 / 4096.0), waits=[tS1])
            tmusq3 = op("vector", lambda e, cc=c: e.tensor_scalar(
                out=SM[:, cc + 12:cc + 13], in0=SM[:, cc + 11:cc + 12],
                scalar1=SM[:, cc + 11:cc + 12], scalar2=-1.0,
                op0=ALU.mult, op1=ALU.mult), waits=[tneg3])
            tbias3 = op("vector", lambda e, cc=c: e.tensor_scalar_add(
                out=SM[:, cc + 13:cc + 14], in0=SM[:, cc + 12:cc + 13],
                scalar1=EPS), waits=[tmusq3])
            tcomb3 = op("vector", lambda e, cc=c: e.tensor_scalar_mul(
                out=SM[:, cc + 14:cc + 15], in0=SM[:, cc + 10:cc + 11],
                scalar1=1.0 / 4096.0), waits=[d["H"]])
            tsq3 = op("scalar", lambda e, cc=c: e.activation(
                out=SM[:, cc + 15:cc + 16], in_=SM[:, cc + 14:cc + 15],
                func=AF.Sqrt, bias=SM[:, cc + 13:cc + 14]),
                waits=[tcomb3, tbias3])
            trstd3 = op("vector", lambda e, cc=c: e.reciprocal(
                out=SM[:, cc + 14:cc + 15], in_=SM[:, cc + 15:cc + 16]),
                waits=[tsq3])
            tJ = op("vector", lambda e, cc=c, xbb=xb: e.tensor_scalar(
                out=lnx1_tm, in0=xbb, scalar1=SM[:, cc + 11:cc + 12],
                scalar2=SM[:, cc + 14:cc + 15], op0=ALU.add, op1=ALU.mult),
                waits=[trstd3] +
                      ([et[tk - 1]["ttr"]] if tk > 0 else []))
            d["J"] = tJ
            ttr_last = None
            ev = None
            for g4 in range(4):
                pt, bidx = (pbt6, 6) if g4 % 2 == 0 else (pbt7, 7)
                ttr = None
                for k in range(8):
                    ic = g4 * 8 + k
                    ttr = op("tensor", lambda e, icc=ic, ptt=pt, kk=k:
                             e.transpose(ptt[:, 128 * kk:128 * kk + 128],
                                         lnx1_tm[:, 128 * icc:128 * icc + 128],
                                         IDN[:]),
                             waits=[tJ, t_idn, bank_r.get(bidx),
                                    bank_w.get(bidx)])
                bank_w[bidx] = ttr
                ttr_last = ttr
                ev = op("scalar", lambda e, ptt=pt, g=g4, t8=tk: e.copy(
                    out=BIG3.rearrange("p (ic t) -> p ic t", t=1024)[
                        :, 8 * g:8 * g + 8, 128 * t8:128 * t8 + 128],
                    in_=ptt[:, 0:1024].rearrange(
                        "p (k t) -> p k t", t=128)), waits=[ttr])
                bank_r[bidx] = ev
            d["ttr"] = ttr_last
            d["ev"] = ev

        e_head(0)
        for tk in range(8):
            if tk < 7:
                e_head(tk + 1)
            e_tail(tk)
        lnx1_done = et[7]["ev"]
        spill_last = spills[-1]

        bias_bf = BIG1[0:1, 18432:22528]
        t_bias = dma("gpsimd", bias_bf, bias_row,
                     waits=[et[4]["e2"], et[5]["e2"], et[4]["B"], et[5]["B"]],
                     sem="GC")
        t_cs = dmams(SM[0:16, 160:161], cs_col)
        t_ss = dmams(SM[0:16, 161:162], ss_col)
        t_rr = dmams(RZ[0:16, 0:128], repR)
        t_sc = dmams(RZ[0:16, 128:130], selC)
        t_w16 = op("vector", lambda e: e.tensor_mul(
            out=SM[0:16, 162:163], in0=SM[0:16, 160:161],
            in1=SM[0:16, 161:162]), waits=[t_cs, t_ss])
        t_wl = op("vector", lambda e: e.tensor_scalar_mul(
            out=RZ[0:16, 130:258], in0=RZ[0:16, 0:128],
            scalar1=SM[0:16, 162:163]), waits=[t_w16, t_rr])
        t_wmm = op("tensor", lambda e: e.matmul(
            pb[4][:, 0:2], RZ[0:16, 130:258], RZ[0:16, 128:130],
            start=True, stop=True),
            waits=[t_wl, t_sc, bank_r.get(4), bank_w.get(4)])
        bank_w[4] = t_wmm
        t_wcm = op("scalar", lambda e: e.copy(out=WCM[:], in_=pb[4][:, 0:2]),
                   waits=[t_wmm])
        bank_r[4] = t_wcm
        t_asb = dma("gpsimd", BIG1[:, 0:8192],
                    ap_t.rearrange("p ic r -> p (ic r)"),
                    waits=[et[0]["e2"], et[1]["e2"], et[0]["B"], et[1]["B"]],
                    sem="GA")
        t_bsb = dma("gpsimd",
                    BIG1[:, 8192:16384].rearrange("p (rc o) -> p rc o", rc=2),
                    b_t.rearrange("(rc p) o -> p rc o", p=128),
                    waits=[et[2]["e2"], et[3]["e2"], et[2]["B"], et[3]["B"]],
                    sem="GB")
        t_bw = op("vector", lambda e: e.tensor_scalar_mul(
            out=BIG1[:, 8192:12288], in0=BIG1[:, 8192:12288],
            scalar1=WCM[:, 0:1]), waits=[t_bsb, t_wcm])
        t_bw2 = op("vector", lambda e: e.tensor_scalar_mul(
            out=BIG1[:, 12288:16384], in0=BIG1[:, 12288:16384],
            scalar1=WCM[:, 1:2]), waits=[t_bsb, t_wcm])
        xa_ev = []
        for rc in range(2):
            for tcol in range(2):
                b = 2 * rc + tcol
                tmm = None
                for ic in range(32):
                    tmm = op("tensor", lambda e, icc=ic, r=rc, t2=tcol, bb=b:
                             e.matmul(
                        pb[bb][:],
                        BIG1[:, 256 * icc + 128 * r:256 * icc + 128 * r + 128],
                        BIG3[:, 1024 * icc + 512 * t2:
                             1024 * icc + 512 * t2 + 512],
                        start=(icc == 0), stop=(icc == 31)),
                        waits=[t_asb, lnx1_done, bank_r.get(b), bank_w.get(b)]
                        if ic == 0 else [])
                bank_w[b] = tmm
                tev = op("scalar", lambda e, r=rc, t2=tcol, bb=b: e.copy(
                    out=BIG1[:, 16384 + 1024 * r + 512 * t2:
                             16384 + 1024 * r + 512 * t2 + 512],
                    in_=pb[bb][:]), waits=[tmm, lnx1_done])
                bank_r[b] = tev
                xa_ev.append(tev)
        xa_done = xa_ev[-1]

        f_mini_cons = {m: o_sgn_cons.get(m) for m in range(4)}
        mm_last = None
        yev = {}
        for ocol in range(8):
            for ic in range(32):
                m = (ocol * 32 + ic) % 4
                sap = SGN[:, 512 * m:512 * m + 512]
                w = [f_mini_cons.get(m)]
                if ocol == 0 and ic < 4:
                    w += [xa_done]
                tdw = dma("gpsimd", sap, wf_t[ocol, ic],
                          waits=[t for t in w if t], sem="WM%d" % m)
                for t8 in range(8):
                    w2 = [tdw, lnx1_done]
                    if ic == 0:
                        w2 += [bank_r.get(t8), bank_w.get(t8)]
                    mm_last = op("tensor", lambda e, s=sap, icc=ic, tt=t8:
                                 e.matmul(
                        pb[tt][:],
                        BIG3[:, 1024 * icc + 128 * tt:
                             1024 * icc + 128 * tt + 128],
                        s[:], start=(icc == 0), stop=False), waits=w2)
                f_mini_cons[m] = mm_last
            for t8 in range(8):
                for rc in range(2):
                    mm_last = op("tensor", lambda e, tt=t8, r=rc, oc=ocol:
                                 e.matmul(
                        pb[tt][:],
                        BIG1[:, 16384 + 1024 * r + 128 * tt:
                             16384 + 1024 * r + 128 * tt + 128],
                        BIG1[:, 8192 + 4096 * r + 512 * oc:
                             8192 + 4096 * r + 512 * oc + 512],
                        start=False, stop=False),
                        waits=[t_bw, t_bw2, xa_done]
                        if (ocol == 0 and t8 == 0 and rc == 0) else [])
                mm_last = op("tensor", lambda e, tt=t8, oc=ocol: e.matmul(
                    pb[tt][:], ONE[0:1, :],
                    bias_bf[:, 512 * oc:512 * oc + 512],
                    start=False, stop=True),
                    waits=[t_bias, t_one]
                    if (ocol == 0 and t8 == 0) else [])
                bank_w[t8] = mm_last
                tev = op("scalar", lambda e, tt=t8, oc=ocol: e.copy(
                    out=BIG2[:, 4096 * tt + 512 * oc:4096 * tt + 512 * oc + 512],
                    in_=pb[tt][:]),
                    waits=[mm_last, spill_last] +
                          ([lnx1_done] if ocol == 0 else []))
                bank_r[t8] = tev
                yev[(ocol, t8)] = tev
        fmm = mm_last

        f1 = [BIG3[:, 0:8192].bitcast(F32), BIG3[:, 8192:16384].bitcast(F32)]
        xst = [BIG3[:, 16384:24576].bitcast(F32),
               BIG3[:, 24576:32768].bitcast(F32)]
        fjunk = BIG1[:, 0:4096]
        f2b = [BIG1[:, 4096:12288].bitcast(F32),
               BIG1[:, 12288:20480].bitcast(F32)]
        ft = [{} for _ in range(8)]
        out_toks = [None] * 8

        def f_head(tk):
            d = ft[tk]
            c = 224 + 16 * (tk % 2)
            d["c"] = c
            yrow = BIG2[:, 4096 * tk:4096 * tk + 4096]
            d["y"] = yrow
            yrdy = yev[(7, tk)]
            wx = [spills[tk]]
            if tk < 2:
                wx.append(fmm)
            else:
                wx.append(ft[tk - 2]["aff"])
            d["A"] = dma("scalar", xst[tk % 2],
                         x1_spill[128 * tk:128 * tk + 128, :],
                         waits=[t for t in wx if t], sem="XF")
            d["B"] = op("vector", lambda e, o=yrow, cc=c: e.reduce_sum(
                out=SM[:, cc:cc + 1], in_=o, axis=AX),
                waits=[yrdy] + ([fmm] if tk == 0 else []))
            d["C"] = op("scalar", lambda e, o=yrow, cc=c: e.activation(
                out=fjunk, in_=o, func=AF.Square,
                accum_out=SM[:, cc + 1:cc + 2]),
                waits=[yrdy] + ([fmm] if tk == 0 else []))
            tneg = op("vector", lambda e, cc=c: e.tensor_scalar_mul(
                out=SM[:, cc + 2:cc + 3], in0=SM[:, cc:cc + 1],
                scalar1=-1.0 / 4096.0), waits=[d["B"]])
            tmusq = op("vector", lambda e, cc=c: e.tensor_scalar(
                out=SM[:, cc + 3:cc + 4], in0=SM[:, cc + 2:cc + 3],
                scalar1=SM[:, cc + 2:cc + 3], scalar2=-1.0,
                op0=ALU.mult, op1=ALU.mult), waits=[tneg])
            tbias = op("vector", lambda e, cc=c: e.tensor_scalar_add(
                out=SM[:, cc + 4:cc + 5], in0=SM[:, cc + 3:cc + 4],
                scalar1=EPS), waits=[tmusq])
            tcomb = op("vector", lambda e, cc=c: e.tensor_scalar_mul(
                out=SM[:, cc + 5:cc + 6], in0=SM[:, cc + 1:cc + 2],
                scalar1=1.0 / 4096.0), waits=[d["C"]])
            tsq = op("scalar", lambda e, cc=c: e.activation(
                out=SM[:, cc + 6:cc + 7], in_=SM[:, cc + 5:cc + 6],
                func=AF.Sqrt, bias=SM[:, cc + 4:cc + 5]),
                waits=[tcomb, tbias])
            trstd = op("vector", lambda e, cc=c: e.reciprocal(
                out=SM[:, cc + 7:cc + 8], in_=SM[:, cc + 6:cc + 7]),
                waits=[tsq])
            d["nb"] = op("vector", lambda e, cc=c: e.tensor_mul(
                out=SM[:, cc + 8:cc + 9], in0=SM[:, cc + 2:cc + 3],
                in1=SM[:, cc + 7:cc + 8]), waits=[trstd])
            d["rstd"] = trstd

        def f_tail(tk):
            d = ft[tk]
            c = d["c"]
            tf2 = op("scalar", lambda e, cc=c, o=d["y"], fo=f2b[tk % 2]:
                     e.activation(
                out=fo, in_=o, func=AF.Identity,
                scale=SM[:, cc + 7:cc + 8], bias=SM[:, cc + 8:cc + 9]),
                waits=[d["nb"]] +
                      ([ft[tk - 2]["aff"]] if tk > 1 else [fmm]))
            wf1 = [tf2, d["A"]]
            if tk < 2:
                wf1.append(fmm)
            else:
                wf1.append(out_toks[tk - 2])
            d["aff"] = op("vector", lambda e, xs=xst[tk % 2],
                          fb=f1[tk % 2], fo=f2b[tk % 2]: e.tensor_tensor(
                out=fb, in0=xs, in1=fo, op=ALU.add),
                waits=[t for t in wf1 if t])
            out_toks[tk] = dma("sync", out[128 * tk:128 * tk + 128, :],
                               f1[tk % 2], waits=[d["aff"]], sem="OU")

        f_head(0)
        for tk in range(8):
            if tk < 7:
                f_head(tk + 1)
            f_tail(tk)

        p.op("sync", lambda e: None, waits=[out_toks[-1]])

        def emit(eng_name):
            def body(eng):
                for (waits, build, inc, inc_by) in p.q[eng_name]:
                    for (s, v) in waits:
                        eng.wait_ge(sems[s], v)
                    inst = build(eng)
                    if inc is not None:
                        inst.then_inc(sems[inc], inc_by)
            return body

        blk.sync(emit("sync"))
        blk.gpsimd(emit("gpsimd"))
        blk.vector(emit("vector"))
        blk.scalar(emit("scalar"))
        blk.tensor(emit("tensor"))

    return nc


def _layout_inputs(inputs):
    hs = np.ascontiguousarray(inputs["L0_attn_hidden_states"], dtype=np.float32)
    Wq = np.ascontiguousarray(inputs["L0_attn_q_weight"], dtype=np.float32)
    Wk = np.ascontiguousarray(inputs["L0_attn_k_weight"], dtype=np.float32)
    Wv = np.ascontiguousarray(inputs["L0_attn_v_weight"], dtype=np.float32)
    Wo = np.ascontiguousarray(inputs["L0_attn_o_weight"], dtype=np.float32)
    Wf = np.ascontiguousarray(inputs["L0_ffn_W"], dtype=np.float32)
    A = np.ascontiguousarray(inputs["L0_ffn_A"], dtype=np.float32)
    Bf = np.ascontiguousarray(inputs["L0_ffn_Bf"], dtype=np.float32)

    shared = dict(
        wq_t=np.ascontiguousarray(
            Wq.reshape(32, 128, 32, 128).transpose(0, 3, 2, 1)),
        wk_t=np.ascontiguousarray(
            Wk.reshape(8, 128, 32, 128).transpose(0, 3, 2, 1)),
        wv_ic=np.ascontiguousarray(
            Wv.reshape(1024, 32, 128).transpose(1, 2, 0)),
        wo_t=np.ascontiguousarray(
            Wo.reshape(8, 512, 32, 128).transpose(0, 2, 3, 1)),
        wf_t=np.ascontiguousarray(
            Wf.reshape(8, 512, 32, 128).transpose(0, 2, 3, 1)),
        wq_n=Wq, wk_n=Wk, wv_n=Wv, wo_n=Wo,
        qs_cm=np.ascontiguousarray(
            inputs["L0_attn_q_scale"].reshape(32, 128).T.astype(np.float32)),
        ks_cm=np.ascontiguousarray(
            inputs["L0_attn_k_scale"].reshape(8, 128).T.astype(np.float32)),
        vs_cm=np.ascontiguousarray(
            inputs["L0_attn_v_scale"].reshape(8, 128).T.astype(np.float32)),
        os_row=np.ascontiguousarray(
            inputs["L0_attn_o_scale"].reshape(1, 4096).astype(np.float32)),
        bias_row=np.ascontiguousarray(
            inputs["L0_ffn_bias"].reshape(1, 4096).astype(np.float32)),
        ap_t=np.ascontiguousarray(
            A.reshape(256, 32, 128).transpose(2, 1, 0)),
        b_t=np.ascontiguousarray(Bf.transpose(0, 2, 1).reshape(256, 4096)),
        cs_col=np.ascontiguousarray(
            inputs["L0_ffn_coeffs"].reshape(16, 1).astype(np.float32)),
        ss_col=np.ascontiguousarray(
            inputs["L0_ffn_scales"].reshape(16, 1).astype(np.float32)),
        repR=np.ascontiguousarray(
            (np.arange(128)[None, :] // 16 ==
             (np.arange(16) % 8)[:, None]).astype(BF)),
        selC=np.ascontiguousarray(
            (np.arange(2)[None, :] ==
             (np.arange(16) // 8)[:, None]).astype(BF)),
        ones_bf=np.ones((128, 128), dtype=BF),
        ident_bf=np.eye(128, dtype=BF),
    )
    in_maps = []
    for c in range(8):
        b, oh = c // 2, c % 2
        po = 1 - oh
        xc = np.concatenate([hs[b, 1024 * oh:1024 * oh + 1024],
                             hs[b, 1024 * po:1024 * po + 1024]], axis=0)
        m = dict(shared)
        m["x"] = np.ascontiguousarray(xc)
        in_maps.append(m)
    return in_maps


def kernel(**inputs):
    in_maps = _layout_inputs(inputs)
    nc = build_program()
    res = run_bass_kernel_spmd(nc, in_maps, core_ids=list(range(8)))
    o = np.empty((4, 2048, 4096), dtype=np.float32)
    for c in range(8):
        b, oh = c // 2, c % 2
        o[b, 1024 * oh:1024 * oh + 1024] = res.results[c]["out"]
    return o
